# version 29
# baseline (speedup 1.0000x reference)
import numpy as np
import ml_dtypes
from contextlib import ExitStack

import concourse.bass as bass
import concourse.mybir as mybir
from concourse.bass_utils import run_bass_kernel_spmd

F32 = mybir.dt.float32
BF16 = mybir.dt.bfloat16
I32 = mybir.dt.int32
I8 = mybir.dt.int8
AF = mybir.ActivationFunctionType
ALU = mybir.AluOpType
AX = mybir.AxisListType

P = 128
D = 1024
ND = 8
DI = 2048
NCI = 16
L = 2048
NT = 16
H = 3072
NH = 24
TCM = 256
NKM = L // TCM
NTM = TCM // P
TCF = 512
NKF = L // TCF
NTF = TCF // P
EPS = 1e-5

N_SUBLAYERS = 4
CUT = None

ENGINES = ("sync", "tensor", "vector", "scalar", "gpsimd")

AB = 203584
G_IDENT = 0
G_MASK = 256
G_CONVW = 768
G_BDT = 1024
G_DPAR = 1088
G_NSUM = 1152
G_SSQ = 1216
G_VAR = 1280
G_RSTD = 1344
G_NMR = 1408
G_1PR = 1472
G_CARRY = 1536
G_MSQ = 1600
G_NMAX = 1664
G_RSUM = 1728
G_RECIP = 1792
G_SD = 1856
G_DUMP = 4096
A0 = 8192

M_WIN = 0
M_WOUT = 65536
M_WX = 98304
M_XIN = 100352
M_XN = 116736
M_XNT = 120832
M_XC = 129024
M_U = 137472
M_SZ = 145664
M_ACC = 153856
M_R2 = 155904
M_R3 = 160000
M_SP = 164096
M_A2 = 168192
M_UB = 172288
M_YS = 176384
M_Y = 180480
M_STAGE = 188672
M_END = 192768

F_WG = 0
F_STG = 49152
F_STG2 = 55296
F_WU = 65536
F_WD = 114688
F_XINB = 118784
F_XIN2 = 129024
F_XN = 145408
F_XNT = 149504
F_H = 165888
F_GS = 190464

T_XNT = 0
T_KT = 32768
T_V = 65536
T_WT = 98304
T_PROBS = 131072
T_PROBT = 139264
T_X = 143360
T_XN = 151552
T_XPNO = 155648
T_XPNI = 163840
T_XOUT = 172032
T_END = 180224


class Sem:
    __slots__ = ("h", "n")

    def __init__(self, h):
        self.h = h
        self.n = 0

    def mark(self):
        return (self, self.n)


class Em:
    def __init__(self):
        self.q = {e: [] for e in ENGINES}
        self.n_ops = 0

    def op(self, eng, fn, wait=(), inc=(), drain=True):
        self.n_ops += 1
        if CUT is not None and self.n_ops > CUT:
            return
        waits = []
        for m in wait:
            if m is None:
                continue
            s, v = m
            assert v <= s.n, f"forward wait on {eng}: {v} > {s.n}"
            if v > 0:
                waits.append((s.h, v))
        incs = []
        for s, d in inc:
            s.n += d
            incs.append((s.h, d))

        need_drain = drain and eng in ("vector", "scalar", "gpsimd")

        def run(e, fn=fn, waits=waits, incs=incs, nd=need_drain):
            if nd:
                e.drain()
            for h, v in waits:
                e.wait_ge(h, v)
            r = fn(e)
            for h, d in incs:
                r = r.then_inc(h, d)

        self.q[eng].append(run)


def ap3(view, off, blk_step, blk_n, inner_n):
    return bass.AP(view.tensor, view.offset + off,
                   [list(view.ap[0]), [blk_step, blk_n], [1, inner_n]])


def build_nc():
    nc = bass.Bass("TRN2", target_bir_lowering=False)

    x_ext = nc.dram_tensor("x", [L, D], F32, kind="ExternalInput")
    w_in = nc.dram_tensor("w_in", [D, 4096], F32, kind="ExternalInput")
    conv_wt = nc.dram_tensor("conv_wt", [P, 64], F32, kind="ExternalInput")
    w_x = nc.dram_tensor("w_x", [P, NCI * 32], F32, kind="ExternalInput")
    w_dt = nc.dram_tensor("w_dt", [17, DI], F32, kind="ExternalInput")
    b_dt = nc.dram_tensor("b_dt", [P, 16], F32, kind="ExternalInput")
    w_bp = nc.dram_tensor("w_bp", [16, DI], F32, kind="ExternalInput")
    d_par = nc.dram_tensor("d_par", [P, 16], F32, kind="ExternalInput")
    w_out = nc.dram_tensor("w_out", [DI, D], F32, kind="ExternalInput")
    wt_at = nc.dram_tensor("wt_at", [D, 2048], F32, kind="ExternalInput")
    wg_t = nc.dram_tensor("wg_t", [D, H], I32, kind="ExternalInput")
    wu_t = nc.dram_tensor("wu_t", [D, H], I32, kind="ExternalInput")
    wd_t = nc.dram_tensor("wd_t", [H, D], I32, kind="ExternalInput")
    ident_e = nc.dram_tensor("ident", [P, P], BF16, kind="ExternalInput")
    mask_e = nc.dram_tensor("mask", [P, P], F32, kind="ExternalInput")
    out_ext = nc.dram_tensor("out", [L, D], F32, kind="ExternalOutput")

    xs1 = nc.dram_tensor("xs1", [L, D], F32)
    xs2 = nc.dram_tensor("xs2", [L, D], F32)
    xs3 = nc.dram_tensor("xs3", [L, D], F32)
    xpn_d = nc.dram_tensor("xpn_d", [L, D], F32)
    wg_c = nc.dram_tensor("wg_c", [D, H], BF16)
    wu_c = nc.dram_tensor("wu_c", [D, H], BF16)
    wd_c = nc.dram_tensor("wd_c", [H, D], BF16)

    em = Em()

    with ExitStack() as ctx:
        arena = ctx.enter_context(nc.sbuf_tensor("arena", [P, AB], I8))
        wdt_bf = ctx.enter_context(nc.sbuf_tensor("wdt_bf", [17, DI], BF16))
        wbp_bf = ctx.enter_context(nc.sbuf_tensor("wbp_bf", [16, DI], BF16))
        xpd_bf = ctx.enter_context(nc.sbuf_tensor("xpd_bf", [32, TCM], BF16))
        xpb_bf = ctx.enter_context(nc.sbuf_tensor("xpb_bf", [16, TCM], BF16))

        def V(off, nbytes, dt):
            return arena[:, off:off + nbytes].bitcast(dt)

        def VA(off, nbytes, dt):
            return V(A0 + off, nbytes, dt)

        g_ident = V(G_IDENT, 256, BF16)
        g_mask = V(G_MASK, 512, F32)
        g_convw = V(G_CONVW, 256, F32)
        g_bdt = V(G_BDT, 64, F32)
        g_dpar = V(G_DPAR, 64, F32)
        g_nsum = V(G_NSUM, 64, F32)
        g_ssq = V(G_SSQ, 64, F32)
        g_var = V(G_VAR, 64, F32)
        g_rstd = V(G_RSTD, 64, F32)
        g_nmr = V(G_NMR, 64, F32)
        g_1pr = V(G_1PR, 64, F32)
        g_carry = V(G_CARRY, 64, F32)
        g_msq = V(G_MSQ, 64, F32)
        g_nmax = V(G_NMAX, 64, F32)
        g_rsum = V(G_RSUM, 64, F32)
        g_recip = V(G_RECIP, 64, F32)
        g_sd = V(G_SD, 64, F32)
        g_dump = V(G_DUMP, 2048, BF16)

        ps_s = ctx.enter_context(nc.psum_tensor("ps_s", [P, 2048], F32))
        ps4 = ctx.enter_context(nc.psum_tensor("ps4", [P, 512], F32))
        ps5 = ctx.enter_context(nc.psum_tensor("ps5", [P, 512], F32))
        ps6 = ctx.enter_context(nc.psum_tensor("ps6", [P, 512], F32))
        ps7 = ctx.enter_context(nc.psum_tensor("ps7", [P, 512], F32))
        psb = [ps_s[:, i * 512:(i + 1) * 512] for i in range(4)] + \
              [ps4[:], ps5[:], ps6[:], ps7[:]]

        sd_sm = Sem(ctx.enter_context(nc.semaphore("sd_sm")))
        sd_wa = Sem(ctx.enter_context(nc.semaphore("sd_wa")))
        sd_wb = Sem(ctx.enter_context(nc.semaphore("sd_wb")))
        sd_wc = Sem(ctx.enter_context(nc.semaphore("sd_wc")))
        sd_w4 = Sem(ctx.enter_context(nc.semaphore("sd_w4")))
        sd_cache = Sem(ctx.enter_context(nc.semaphore("sd_cache")))
        sd_ca0 = Sem(ctx.enter_context(nc.semaphore("sd_ca0")))
        sd_ca1 = Sem(ctx.enter_context(nc.semaphore("sd_ca1")))
        sd_g1 = Sem(ctx.enter_context(nc.semaphore("sd_g1")))
        sd_g2 = Sem(ctx.enter_context(nc.semaphore("sd_g2")))
        sd_xa = Sem(ctx.enter_context(nc.semaphore("sd_xa")))
        sd_xb = Sem(ctx.enter_context(nc.semaphore("sd_xb")))
        sd_sta = Sem(ctx.enter_context(nc.semaphore("sd_sta")))
        sd_stb = Sem(ctx.enter_context(nc.semaphore("sd_stb")))
        sd_stp = Sem(ctx.enter_context(nc.semaphore("sd_stp")))
        sd_stq = Sem(ctx.enter_context(nc.semaphore("sd_stq")))
        sd_stg = Sem(ctx.enter_context(nc.semaphore("sd_stg")))
        sd_m0 = Sem(ctx.enter_context(nc.semaphore("sd_m0")))
        sd_m1 = Sem(ctx.enter_context(nc.semaphore("sd_m1")))
        sd_x = [sd_xa, sd_xb]
        sd_st = [sd_sta, sd_stb]
        s_pe = Sem(ctx.enter_context(nc.semaphore("s_pe")))
        s_ve = Sem(ctx.enter_context(nc.semaphore("s_ve")))
        s_se = Sem(ctx.enter_context(nc.semaphore("s_se")))
        s_gp = Sem(ctx.enter_context(nc.semaphore("s_gp")))

        col_free = {}
        trdone = {}
        mk_eps = None

        def dma(queue, dst, src, sem, wait=()):
            em.op(queue, lambda e, d=dst, s=src: e.dma_start(d, s),
                  wait=wait, inc=[(sem, 16)])
            return sem.mark()

        def ln_stats_group(tiles, col0, wait_x, extra_1pr=False):
            n = len(tiles)
            J = slice(col0, col0 + n)
            cf = [col_free.get(col0 + i) for i in range(n)]
            for i, t in enumerate(tiles):
                em.op("scalar",
                      lambda e, s=t, j=col0 + i: e.activation(
                          g_dump[:, 0:1024], s, AF.Square,
                          accum_out=g_ssq[:, j:j + 1]),
                      wait=(list(wait_x) + cf) if i == 0 else (),
                      drain=(i > 0),
                      inc=([(s_se, 1)] if i == n - 1 else ()))
            mk_ssq = s_se.mark()
            for i, t in enumerate(tiles):
                em.op("vector",
                      lambda e, s=t, j=col0 + i: e.reduce_sum(
                          g_nsum[:, j:j + 1], s, axis=AX.X, negate=True),
                      wait=(list(wait_x) + cf) if i == 0 else (), drain=False,
                      inc=([(s_ve, 1)] if i == n - 1 else ()))
            em.op("vector", lambda e, J=J: e.tensor_scalar_mul(
                g_nsum[:, J], g_nsum[:, J], 1.0 / D))
            em.op("vector", lambda e, J=J: e.tensor_mul(
                g_msq[:, J], g_nsum[:, J], g_nsum[:, J]))
            em.op("vector", lambda e, J=J: e.scalar_tensor_tensor(
                g_var[:, J], g_ssq[:, J], 1.0 / D,
                g_msq[:, J], op0=ALU.mult, op1=ALU.subtract),
                wait=[mk_ssq], inc=[(s_ve, 1)])
            mk_var = s_ve.mark()
            em.op("scalar", lambda e, J=J: e.activation(
                g_sd[:, J], g_var[:, J], AF.Sqrt,
                bias=g_sd[:, 15:16]),
                wait=[mk_var, mk_eps], inc=[(s_se, 1)])
            mk_sd = s_se.mark()
            em.op("vector", lambda e, J=J: e.reciprocal(
                g_rstd[:, J], g_sd[:, J]), wait=[mk_sd], drain=False)
            em.op("vector", lambda e, J=J: e.tensor_mul(
                g_nmr[:, J], g_nsum[:, J], g_rstd[:, J]),
                inc=[(s_ve, 1)])
            if extra_1pr:
                em.op("vector", lambda e, J=J: e.tensor_scalar_add(
                    g_1pr[:, J], g_rstd[:, J], 1.0),
                    inc=[(s_ve, 1)], drain=False)
            return s_ve.mark()

        def ln_norm(dst_bf, src_tile, col, mk_stats, extra_wait=None,
                    first=True):
            ew = extra_wait if isinstance(extra_wait, list) else [extra_wait]
            em.op("vector", lambda e, d=dst_bf, s=src_tile, j=col:
                  e.tensor_scalar(d, s, g_rstd[:, j:j + 1],
                                  g_nmr[:, j:j + 1],
                                  op0=ALU.mult, op1=ALU.add),
                  wait=[mk_stats] + ew, inc=[(s_ve, 1)], drain=first)
            col_free[col] = s_ve.mark()
            return col_free[col]

        def transpose_tile(xn_bf_tile, bank, mk_xn, mk_bank_free):
            pbf = psb[bank].bitcast(BF16)
            for dj in range(ND):
                em.op("tensor",
                      lambda e, s=xn_bf_tile, dj=dj, pbf=pbf: e.transpose(
                          pbf[:, dj * P:(dj + 1) * P],
                          s[:, dj * P:(dj + 1) * P], g_ident),
                      wait=([mk_xn, mk_bank_free] if dj == 0 else ()),
                      inc=([(s_pe, 1)] if dj == ND - 1 else ()))
            return s_pe.mark()

        def evac_xnt(xnt_view, blk_step, tt, bank, mk_tr):
            pbf = psb[bank].bitcast(BF16)
            dst = ap3(xnt_view, tt * P, blk_step, ND, P)
            em.op("vector", lambda e, d=dst, s=pbf: e.tensor_copy(d, s),
                  wait=[mk_tr], inc=[(s_ve, 1)], drain=False)
            return s_ve.mark()

        xc = VA(M_XC, 8448, BF16)
        em.op("vector", lambda e: e.memset(ap3(xc, 0, 264, NCI, 3), 0.0),
              inc=[(s_ve, 1)], drain=False)
        em.op("vector", lambda e: e.memset(g_carry[:], 0.0), inc=[(s_ve, 1)],
              drain=False)
        em.op("vector", lambda e: e.memset(g_sd[:, 15:16], EPS),
              inc=[(s_ve, 1)], drain=False)
        mk_init = s_ve.mark()
        mk_eps = mk_init

        xin = [VA(M_XIN + i * 4096, 4096, F32) for i in range(4)]
        xnbf = [VA(M_XN + i * 2048, 2048, BF16) for i in range(2)]
        xnt = [VA(M_XNT + i * 4096, 4096, BF16) for i in range(2)]
        u_bf = VA(M_U, 8192, BF16)
        sz_bf = VA(M_SZ, 8192, BF16)
        acc_p = VA(M_ACC, 2048, F32)
        r2_p = [VA(M_R2 + i * 2048, 2048, F32) for i in range(2)]
        r3_p = [VA(M_R3 + i * 2048, 2048, F32) for i in range(2)]
        sp = [VA(M_SP + i * 2048, 2048, F32) for i in range(2)]
        a_f = [VA(M_A2 + i * 2048, 2048, F32) for i in range(2)]
        ub_f = [VA(M_UB + i * 2048, 2048, F32) for i in range(2)]
        ys_f = [VA(M_YS + i * 2048, 2048, F32) for i in range(2)]
        y_bf = VA(M_Y, 8192, BF16)

        mk_ln = [None] * (NKM + 1)
        mk_xnt = [None] * (NKM + 1)
        mk_cevax = [None, None]
        mk_store = [None] * NKM
        mk_y_done = [None]
        mk_gp_hist = {}

        def mamba_load_ln(k):
            par = k % 2
            for tt in range(NTM):
                if k >= 2:
                    w = [mk_cevax[par]]
                elif k == 1:
                    w = [mk_wx_stage]
                else:
                    w = []
                mk_x = dma("scalar", xin[par * 2 + tt][:],
                           x_ext[(k * NTM + tt) * P:(k * NTM + tt + 1) * P, :],
                           sd_x[par], wait=w)
            mst = ln_stats_group([xin[par * 2 + tt][:] for tt in range(NTM)],
                                 par * 2, [mk_x])
            mks = []
            for tt in range(NTM):
                col = par * 2 + tt
                mks.append(ln_norm(xnbf[tt][:], xin[par * 2 + tt][:], col,
                                   mst, trdone.get(("m", tt)),
                                   first=(tt == 0)))
            mk_ln[k] = mks

        def mamba_transpose(k, mk_free):
            par = k % 2
            banks = [4, 5]
            for tt in range(NTM):
                mk_tr = transpose_tile(xnbf[tt][:], banks[tt],
                                       mk_ln[k][tt], mk_free[tt])
                trdone[("m", tt)] = mk_tr
                mk_xnt[k] = evac_xnt(xnt[par], 256, tt, banks[tt], mk_tr)

        stage = VA(M_STAGE, 4096, F32)
        wx_bf = VA(M_WX, 1024, BF16)
        wx_stage = VA(M_Y, 2048, F32)
        dma("sync", g_ident, ident_e[:], sd_sm)
        dma("sync", g_mask, mask_e[:], sd_sm)
        dma("sync", g_convw, conv_wt[:], sd_sm)
        dma("sync", g_bdt, b_dt[:], sd_sm)
        dma("sync", g_dpar, d_par[:], sd_sm)
        mk_sm = dma("sync", wx_stage[:, 0:512], w_x[:], sd_sm)
        mk_sm_all = mk_sm
        mamba_load_ln(0)
        mamba_transpose(0, [mk_sm_all, mk_sm_all])

        em.op("vector", lambda e: e.tensor_copy(
            wx_bf[:], wx_stage[:, 0:512]), wait=[mk_sm], inc=[(s_ve, 1)],
            drain=False)
        mk_wx = s_ve.mark()
        mk_wx_stage = mk_wx
        prevc = None
        for m, (srcw, dstw, nrow) in enumerate(
                ((w_dt, wdt_bf, 17), (w_bp, wbp_bf, 16))):
            for hh in range(2):
                w = [prevc] if prevc else []
                mkd = dma("sync", stage[0:nrow, 0:1024],
                          srcw[:, hh * 1024:(hh + 1) * 1024], sd_g1, wait=w)
                em.op("vector",
                      lambda e, dstw=dstw, hh=hh, nrow=nrow: e.tensor_copy(
                          dstw[0:nrow, hh * 1024:(hh + 1) * 1024],
                          stage[0:nrow, 0:1024]),
                      wait=[mkd], inc=[(s_ve, 1)], drain=False)
                prevc = s_ve.mark()
        mk_wdtbp = prevc
        em.op("vector", lambda e: e.memset(xpd_bf[0:32, :], 1.0),
              inc=[(s_ve, 1)], drain=False)
        mk_ones = s_ve.mark()

        win_bf = VA(M_WIN, 65536, BF16)
        win_stage = [VA(M_U, 8192, F32), VA(M_SZ, 8192, F32),
                     VA(M_ACC, 8192, F32), VA(M_SP, 8192, F32)]
        win_sem = [sd_wa, sd_wb, sd_wc, sd_w4]
        mk_win_cast = [None] * ND
        chain = {i: None for i in range(4)}
        gi = 0
        for dj in range(ND):
            for hh in range(2):
                slot = gi % 4
                gi += 1
                mkd = dma("sync" if slot % 2 == 0 else "scalar",
                          win_stage[slot][:, 0:2048],
                          w_in[dj * P:(dj + 1) * P,
                               hh * 2048:(hh + 1) * 2048], win_sem[slot],
                          wait=[chain[slot]] if chain[slot] else [])
                em.op("vector",
                      lambda e, dj=dj, hh=hh, slot=slot: e.tensor_copy(
                          win_bf[:, dj * 4096 + hh * 2048:
                                 dj * 4096 + (hh + 1) * 2048],
                          win_stage[slot][:, 0:2048]),
                      wait=[mkd], inc=[(s_ve, 1)], drain=False)
                mk_win_cast[dj] = chain[slot] = s_ve.mark()

        wout_bf = VA(M_WOUT, 32768, BF16)
        wo_stage = [VA(M_UB, 4096, F32), VA(M_YS, 4096, F32)]
        wo_sem = [sd_wa, sd_wb]
        last_cast = {0: chain[0], 1: chain[1]}
        for ci in range(NCI):
            slot = ci % 2
            mkd = dma("sync", wo_stage[slot][:, 0:1024],
                      w_out[ci * P:(ci + 1) * P, :], wo_sem[slot],
                      wait=[last_cast[slot]] if last_cast[slot] else [])
            em.op("vector",
                  lambda e, ci=ci, slot=slot: e.tensor_copy(
                      wout_bf[:, ci * 1024:(ci + 1) * 1024],
                      wo_stage[slot][:, 0:1024]),
                  wait=[mkd], inc=[(s_ve, 1)], drain=False)
            last_cast[slot] = s_ve.mark()
        mk_startup = s_ve.mark()

        dst1 = xs1 if N_SUBLAYERS > 1 else out_ext

        mk_xz_last = None
        xzbank = {0: None, 1: None}
        mk_c67 = [None]
        mk_gp_acc = [None]
        mk_gp_conv = {0: None, 1: None}
        mk_xp_prev = [None]
        mk_scan_hist = {}
        mk_u = {}
        mk_sz = {}
        mk_sp30_hist = {}
        mk_sp2_hist = {}
        mk_ubm_hist = {}
        mk_xpe_k = {}
        mk_t1 = {-1: None}
        mk_r_hist = {}

        def phaseA_xc_pair(kk, p):
            par = kk % 2
            bkx = p % 2
            for half in range(2):
                ci = 2 * p + half
                wait_mm = [mk_xnt[kk], xzbank[bkx] if half == 0 else None]
                if kk == 0 and ci == 0:
                    wait_mm += [mk_win_cast[-1], mk_wdtbp, mk_wx, mk_ones]
                for dj in range(ND):
                    em.op("tensor",
                          lambda e, dj=dj, ci=ci, bkx=bkx, half=half, par=par: e.matmul(
                              psb[bkx][:, half * TCM:(half + 1) * TCM],
                              win_bf[:, dj * 4096 + ci * P: dj * 4096 + (ci + 1) * P],
                              xnt[par][:, dj * 256:(dj + 1) * 256],
                              start=(dj == 0), stop=(dj == ND - 1)),
                          wait=(wait_mm if dj == 0 else ()),
                          inc=([(s_pe, 1)] if (dj == ND - 1 and half == 1) else ()))
            mk_mm = s_pe.mark()
            em.op("scalar",
                  lambda e, p=p, bkx=bkx: e.copy(
                      ap3(xc, 2 * p * 264 + 3, 264, 2, TCM),
                      psb[bkx]),
                  wait=[mk_mm], inc=[(s_se, 1)], drain=False)
            xzbank[bkx] = s_se.mark()
            for half in range(2):
                ci = 2 * p + half
                em.op("vector",
                      lambda e, ci=ci, half=half: e.tensor_scalar(
                          acc_p[:, half * TCM:(half + 1) * TCM],
                          xc[:, ci * 264 + 0: ci * 264 + TCM],
                          g_convw[:, ci * 4: ci * 4 + 1], None,
                          op0=ALU.mult),
                      wait=([mk_gp_acc[0], xzbank[bkx]] if half == 0 else ()),
                      drain=(half == 0))
            for half in range(2):
                ci = 2 * p + half
                em.op("vector",
                      lambda e, ci=ci, half=half: e.scalar_tensor_tensor(
                          acc_p[:, half * TCM:(half + 1) * TCM],
                          xc[:, ci * 264 + 1: ci * 264 + 1 + TCM],
                          g_convw[:, ci * 4 + 1: ci * 4 + 2],
                          acc_p[:, half * TCM:(half + 1) * TCM],
                          op0=ALU.mult, op1=ALU.add),
                      drain=(half == 0),
                      inc=([(s_ve, 1)] if half == 1 else ()))
            mk_acc = s_ve.mark()
            pp = p % 2
            for half in range(2):
                ci = 2 * p + half
                em.op("scalar",
                      lambda e, ci=ci, half=half, pp=pp: e.mul(
                          r2_p[pp][:, half * TCM:(half + 1) * TCM],
                          xc[:, ci * 264 + 2: ci * 264 + 2 + TCM],
                          g_convw[:, ci * 4 + 2: ci * 4 + 3]),
                      wait=([xzbank[bkx], mk_gp_conv[pp]] if half == 0 else ()),
                      drain=False)
                em.op("scalar",
                      lambda e, ci=ci, half=half, pp=pp: e.mul(
                          r3_p[pp][:, half * TCM:(half + 1) * TCM],
                          xc[:, ci * 264 + 3: ci * 264 + 3 + TCM],
                          g_convw[:, ci * 4 + 3: ci * 4 + 4]),
                      drain=False,
                      inc=([(s_se, 1)] if half == 1 else ()))
            mk_r = s_se.mark()
            mk_r_hist[(kk, p)] = mk_r
            em.op("gpsimd",
                  lambda e, pp=pp: e.tensor_add(
                      r2_p[pp][:], r2_p[pp][:], acc_p[:]),
                  wait=[mk_acc, mk_r], inc=[(s_gp, 1)], drain=False)
            mk_gp_acc[0] = s_gp.mark()
            em.op("gpsimd",
                  lambda e, pp=pp: e.tensor_add(
                      r2_p[pp][:], r2_p[pp][:], r3_p[pp][:]),
                  inc=[(s_gp, 1)])
            mk_gp_conv[pp] = s_gp.mark()
            w_sil = [mk_gp_conv[pp]]
            if kk >= 1:
                w_sil += [mk_gp_hist.get((kk - 1, p)), mk_xp_prev[0]]
            else:
                w_sil.append(mk_startup)
            em.op("scalar",
                  lambda e, p=p, pp=pp: e.activation(
                      u_bf[:, 2 * p * TCM: 2 * (p + 1) * TCM],
                      r2_p[pp][:], AF.Silu),
                  wait=w_sil, inc=[(s_se, 1)], drain=False)
            mk_u[(kk, p)] = s_se.mark()

        def phaseA_z_pair(kk, zp):
            par = kk % 2
            bkx = zp % 2
            for half in range(2):
                ci = NCI + 2 * zp + half
                wait_mm = [mk_xnt[kk], xzbank[bkx] if half == 0 else None]
                for dj in range(ND):
                    em.op("tensor",
                          lambda e, dj=dj, ci=ci, bkx=bkx, half=half, par=par: e.matmul(
                              psb[bkx][:, half * TCM:(half + 1) * TCM],
                              win_bf[:, dj * 4096 + ci * P: dj * 4096 + (ci + 1) * P],
                              xnt[par][:, dj * 256:(dj + 1) * 256],
                              start=(dj == 0), stop=(dj == ND - 1)),
                          wait=(wait_mm if dj == 0 else ()),
                          inc=([(s_pe, 1)] if (dj == ND - 1 and half == 1) else ()))
            mk_mm = s_pe.mark()
            w_sil = [mk_mm]
            if kk >= 1:
                w_sil.append(mk_gp_hist.get((kk - 1, zp)))
            elif zp == 0:
                w_sil.append(mk_startup)
            em.op("scalar",
                  lambda e, zp=zp, bkx=bkx: e.activation(
                      sz_bf[:, 2 * zp * TCM: 2 * (zp + 1) * TCM],
                      psb[bkx], AF.Silu),
                  wait=w_sil, inc=[(s_se, 1)], drain=False)
            xzbank[bkx] = mk_sz[(kk, zp)] = s_se.mark()

        def xp_sec(kk):
            em.op("vector",
                  lambda e: e.tensor_copy(
                      ap3(xc, 0, 264, NCI, 3),
                      ap3(xc, 256, 264, NCI, 3)),
                  wait=[mk_r_hist[(kk, 7)]])
            for ci in range(NCI):
                em.op("tensor",
                      lambda e, ci=ci: e.matmul(
                          psb[5][0:16, 0:TCM],
                          wx_bf[:, ci * 32: ci * 32 + 16],
                          u_bf[:, ci * TCM:(ci + 1) * TCM],
                          start=(ci == 0), stop=(ci == NCI - 1)),
                      wait=[mk_u[(kk, ci // 2)]])
                em.op("tensor",
                      lambda e, ci=ci: e.matmul(
                          psb[4][0:16, 0:TCM],
                          wx_bf[:, ci * 32 + 16: ci * 32 + 32],
                          u_bf[:, ci * TCM:(ci + 1) * TCM],
                          start=(ci == 0), stop=(ci == NCI - 1)),
                      inc=([(s_pe, 1)] if ci == NCI - 1 else ()))
            mk_xp = s_pe.mark()
            mk_xp_prev[0] = mk_xp
            em.op("vector", lambda e: e.tensor_copy(
                xpd_bf[0:16, :], psb[5][0:16, 0:TCM]), wait=[mk_xp],
                drain=False)
            em.op("vector", lambda e: e.tensor_copy(
                xpb_bf[0:16, :], psb[4][0:16, 0:TCM]), inc=[(s_ve, 1)],
                drain=False)
            mk_xpe_k[kk] = s_ve.mark()

        def wout_t0(k, pi):
            w0 = [mk_gp_hist[(k, pi)]]
            if pi == 0:
                w0.append(mk_c67[0])
            for j in range(2):
                ci = 2 * pi + j
                for half in range(2):
                    em.op("tensor",
                          lambda e, ci=ci, half=half, bk=6 + half: e.matmul(
                              psb[bk],
                              y_bf[:, ci * TCM: ci * TCM + P],
                              wout_bf[:, ci * 1024 + half * 512: ci * 1024 + (half + 1) * 512],
                              start=(ci == 0), stop=(ci == NCI - 1)),
                          wait=(w0 if (j == 0 and half == 0) else ()),
                          inc=([(s_pe, 1)] if (j == 1 and half == 1) else ()))

        def phaseB_pair(k, pi):
            ci0 = 2 * pi
            pp = pi % 2
            w0 = [mk_xpe_k[k], mk_sp30_hist.get((k, pi - 1)),
                  mk_ubm_hist.get((k, pi - 1))]
            for half in range(2):
                ci = ci0 + half
                em.op("tensor",
                      lambda e, ci=ci, half=half: e.matmul(
                          psb[2][:, half * TCM:(half + 1) * TCM],
                          wdt_bf[0:17, ci * P:(ci + 1) * P],
                          xpd_bf[0:17, :], start=True, stop=True),
                      wait=(w0 if half == 0 else ()))
                em.op("tensor",
                      lambda e, ci=ci, half=half: e.matmul(
                          psb[3][:, half * TCM:(half + 1) * TCM],
                          wbp_bf[0:16, ci * P:(ci + 1) * P],
                          xpb_bf[0:16, :], start=True, stop=True),
                      inc=([(s_pe, 1)] if half == 1 else ()))
            mk_mm = s_pe.mark()
            if pi >= 1:
                wout_t0(k, pi - 1)
            em.op("vector",
                  lambda e, pp=pp: e.tensor_scalar(
                      sp[pp][:], psb[2][:], 30.0, None, op0=ALU.min),
                  wait=[mk_mm, mk_sp2_hist.get((k, pi - 2))],
                  inc=[(s_ve, 1)], drain=False)
            mk_sp30 = mk_sp30_hist[(k, pi)] = s_ve.mark()
            em.op("scalar", lambda e, pp=pp: e.activation(
                sp[pp][:], sp[pp][:], AF.Exp),
                wait=[mk_sp30, mk_scan_hist.get((k, pi - 2))], drain=False)
            em.op("scalar", lambda e, pp=pp: e.activation(
                sp[pp][:], sp[pp][:], AF.Ln, bias=1.0))
            em.op("scalar", lambda e, pp=pp: e.activation(
                a_f[pp][:], sp[pp][:], AF.Relu,
                scale=-1.0, bias=1.0), inc=[(s_se, 1)])
            mk_sp2 = mk_sp2_hist[(k, pi)] = s_se.mark()
            prev_gp = mk_gp_hist.get((k, pi - 2)) or \
                (mk_gp_hist.get((k - 1, 6 + pi)) if k > 0 else None)
            em.op("vector",
                  lambda e, ci0=ci0, pp=pp: e.tensor_mul(
                      ub_f[pp][:],
                      u_bf[:, ci0 * TCM:(ci0 + 2) * TCM],
                      psb[3][:]),
                  wait=[mk_u[(k, pi)], prev_gp], inc=[(s_ve, 1)], drain=False)
            mk_ubm_hist[(k, pi)] = s_ve.mark()
            em.op("vector", lambda e, pp=pp: e.tensor_scalar(
                a_f[pp][:], a_f[pp][:], 0.9999, None,
                op0=ALU.min), wait=[mk_sp2], drain=False)
            for half in range(2):
                ci = ci0 + half
                em.op("vector",
                      lambda e, ci=ci, half=half, pp=pp: e.tensor_tensor_scan(
                          ys_f[pp][:, half * TCM:(half + 1) * TCM],
                          a_f[pp][:, half * TCM:(half + 1) * TCM],
                          ub_f[pp][:, half * TCM:(half + 1) * TCM],
                          g_carry[:, ci:ci + 1],
                          op0=ALU.mult, op1=ALU.add),
                      wait=([prev_gp] if half == 0 else ()),
                      drain=(half == 0),
                      inc=([(s_ve, 1)] if half == 1 else ()))
            mk_scan = mk_scan_hist[(k, pi)] = s_ve.mark()
            for half in range(2):
                ci = ci0 + half
                em.op("gpsimd", lambda e, ci=ci, half=half, pp=pp: e.tensor_copy(
                    g_carry[:, ci:ci + 1],
                    ys_f[pp][:, (half + 1) * TCM - 1:(half + 1) * TCM]),
                    wait=([mk_scan] if half == 0 else ()), drain=False)
            for half in range(2):
                ci = ci0 + half
                em.op("vector",
                      lambda e, ci=ci, half=half, pp=pp: e.scalar_tensor_tensor(
                          ub_f[pp][:, half * TCM:(half + 1) * TCM],
                          u_bf[:, ci * TCM:(ci + 1) * TCM],
                          g_dpar[:, ci:ci + 1],
                          ys_f[pp][:, half * TCM:(half + 1) * TCM],
                          op0=ALU.mult, op1=ALU.add),
                      drain=(half == 0),
                      inc=([(s_ve, 1)] if half == 1 else ()))
            mk_stt = s_ve.mark()
            w_y = [mk_stt, mk_sz.get((k, pi))]
            if k >= 1 and pi == 0:
                w_y.append(mk_t1[k - 1])
            em.op("gpsimd",
                  lambda e, ci0=ci0, pp=pp: e.tensor_mul(
                      y_bf[:, ci0 * TCM:(ci0 + 2) * TCM],
                      ub_f[pp][:],
                      sz_bf[:, ci0 * TCM:(ci0 + 2) * TCM]),
                  wait=w_y, inc=[(s_gp, 1)], drain=False)
            mk_gp_hist[(k, pi)] = s_gp.mark()

        def mamba_tail(k):
            par = k % 2
            mk_t0_mm = s_pe.mark()
            for half in range(2):
                em.op("vector",
                      lambda e, half=half, par=par, bk=6 + half: e.tensor_add(
                          xin[par * 2][:, half * 512:(half + 1) * 512],
                          xin[par * 2][:, half * 512:(half + 1) * 512],
                          psb[bk]),
                      wait=[mk_t0_mm], inc=[(s_ve, 1)], drain=False)
            ev0 = s_ve.mark()
            st_sem = sd_m0 if k == 0 else (sd_m1 if k == 1 else sd_st[par])
            mk_store[k] = dma("scalar",
                              dst1[(k * NTM) * P:(k * NTM + 1) * P, :],
                              xin[par * 2][:], st_sem, wait=[ev0])
            for ci in range(NCI):
                for half in range(2):
                    em.op("tensor",
                          lambda e, ci=ci, half=half, bk=6 + half: e.matmul(
                              psb[bk],
                              y_bf[:, ci * TCM + P: ci * TCM + 2 * P],
                              wout_bf[:, ci * 1024 + half * 512: ci * 1024 + (half + 1) * 512],
                              start=(ci == 0), stop=(ci == NCI - 1)),
                          wait=([ev0] if (ci == 0 and half == 0) else ()),
                          inc=([(s_pe, 1)] if (ci == NCI - 1 and half == 1) else ()))
            mk_t1[k] = s_pe.mark()
            for half in range(2):
                em.op("vector",
                      lambda e, half=half, par=par, bk=6 + half: e.tensor_add(
                          xin[par * 2 + 1][:, half * 512:(half + 1) * 512],
                          xin[par * 2 + 1][:, half * 512:(half + 1) * 512],
                          psb[bk]),
                      wait=[mk_t1[k]], inc=[(s_ve, 1)], drain=False)
            ev1 = s_ve.mark()
            mk_c67[0] = ev1
            mk_store[k] = dma("scalar",
                              dst1[(k * NTM + 1) * P:(k * NTM + 2) * P, :],
                              xin[par * 2 + 1][:], st_sem, wait=[ev1])
            mk_cevax[par] = mk_store[k]

        for p in range(8):
            phaseA_xc_pair(0, p)
        for zp in range(8):
            phaseA_z_pair(0, zp)
        mk_xz_last = s_pe.mark()
        xp_sec(0)
        mamba_load_ln(1)
        mamba_transpose(1, [mk_xpe_k[0], mk_xpe_k[0]])

        wg_bf_g = VA(F_WG, 49152, BF16)
        wu_bf_g = VA(F_WU, 49152, BF16)
        wstg = [VA(F_STG, 6144, I32), VA(F_STG2, 6144, I32),
                VA(F_XIN2 + 2 * 4096, 6144, I32)]
        wsem = [sd_wa, sd_wb, sd_wc]
        w_prev = {0: None, 1: None, 2: None}
        w_mk = {}

        def emit_w_unit(gi):
            m2, r = divmod(gi, 16)
            dj, hh = divmod(r, 2)
            slot = (r % 2) if m2 == 0 else (r % 3)
            srcw = wg_t if m2 == 0 else wu_t
            dstw = wg_bf_g if m2 == 0 else wu_bf_g
            w = [mk_xz_last]
            if slot == 2:
                w.append(mk_mamba_pe_end)
            if w_prev[slot]:
                w.append(w_prev[slot])
            mkd = dma("sync", wstg[slot][:],
                      srcw[dj * P:(dj + 1) * P, hh * 1536:(hh + 1) * 1536],
                      wsem[slot], wait=w)
            cw = [mkd]
            if m2 == 1 and r == 0:
                cw += [mk_mamba_pe_end, mk_mamba_ve_end, mk_mamba_sta,
                       mk_mamba_stb]
            em.op("gpsimd",
                  lambda e, dj=dj, hh=hh, dstw=dstw, slot=slot: e.tensor_copy(
                      dstw[:, dj * H + hh * 1536: dj * H + (hh + 1) * 1536],
                      wstg[slot][:]),
                  wait=cw, inc=[(s_gp, 1)], drain=False)
            w_prev[slot] = w_mk[gi] = s_gp.mark()

        ffn1_pre = {}
        fxA_g = [VA(F_XIN2 + i * 4096, 4096, F32) for i in range(2)]
        fxB_g = [VA(F_XINB + i * 4096, 4096, F32) for i in range(2)]

        def emit_ffn1_preamble():
            mk_m01 = [(sd_m0, 32), (sd_m1, 32)]
            for i in range(2):
                mk = dma("scalar", fxB_g[i][:],
                         xs1[(2 + i) * P:(3 + i) * P, :], sd_sm,
                         wait=[mk_xz_last] + mk_m01)
            for i in range(2):
                mk = dma("scalar", fxA_g[i][:],
                         xs1[i * P:(1 + i) * P, :], sd_sm,
                         wait=[mk_xpe_k[7], mk_r_hist[(7, 7)]] + mk_m01)
            return {"mk_loads": mk}

        for k in range(NKM):
            for pi in range(8):
                phaseB_pair(k, pi)
                if k + 1 < NKM:
                    phaseA_xc_pair(k + 1, pi)
                    phaseA_z_pair(k + 1, pi)
                if k == NKM - 1 and N_SUBLAYERS >= 2:
                    emit_w_unit(2 * pi)
                    emit_w_unit(2 * pi + 1)
            if k + 1 == NKM - 1:
                mk_xz_last = s_pe.mark()
            if k == NKM - 1:
                mk_b7_pe = s_pe.mark()
            wout_t0(k, 7)
            mamba_tail(k)
            if k + 1 < NKM:
                xp_sec(k + 1)
                if k + 2 < NKM:
                    mamba_load_ln(k + 2)
                    mamba_transpose(k + 2, [mk_xpe_k[k + 1], mk_xpe_k[k + 1]])
            if k == NKM - 2 and N_SUBLAYERS >= 2:
                ffn1_pre.update(emit_ffn1_preamble())

        mk_mamba_pe_end = s_pe.mark()
        mk_mamba_ve_end = s_ve.mark()
        mk_mamba_gp_end = s_gp.mark()
        mk_mamba_sta = sd_sta.mark()
        mk_mamba_stb = sd_stb.mark()

        def ffn(src, dst, first, region, store_mark_fn, pre=None):
            wg_bf = VA(F_WG, 49152, BF16)
            wu_bf = VA(F_WU, 49152, BF16)
            wd_sb = [VA(F_WD + i * 2048, 2048, BF16) for i in range(2)]
            fxA = [VA(F_XIN2 + i * 4096, 4096, F32) for i in range(NTF)]
            fxB = [VA(F_XINB + i * 4096, 4096, F32) for i in range(2)]

            def fxt(c, tt):
                if pre and c == 0 and tt >= 2:
                    return fxB[tt - 2]
                if c % 2 == 1 and tt < 2:
                    return fxB[tt]
                return fxA[tt]
            fxn = [VA(F_XN + i * 2048, 2048, BF16) for i in range(2)]
            fxnt = [VA(F_XNT + i * 8192, 8192, BF16) for i in range(2)]
            h_sb = VA(F_H, 24576, BF16)
            gs = [VA(F_GS + i * 2048, 2048, F32) for i in range(2)]
            stg = [VA(F_STG, 6144, I32), VA(F_STG2, 6144, I32)]
            stg_wd = [VA(F_STG, 4096, I32), VA(F_STG2, 4096, I32)]

            mk_ln_f = [None] * NKF
            mk_xnt_f = [None] * NKF
            mk_fstore = [None] * NT
            mk_fst_chunk = [None] * NKF
            tr_banks = {4: None, 5: None}
            gu_mark = {0: None, 1: None}
            dn_ev = {i: None for i in range(8)}

            def f_ln_tiles(c, tts, col0):
                tiles = [fxt(c, tt) for tt in tts]
                if pre and c == 0:
                    mk_x = pre["mk_loads"]
                else:
                    for i, tt in enumerate(tts):
                        gtile = c * NTF + tt
                        w = list(store_mark_fn(gtile))
                        if tt < 2:
                            if c <= 1:
                                w += region["xin"]
                            else:
                                w.append(mk_fst_chunk[c - 2])
                            if first and c == 1:
                                w.append(mk_fst_chunk[0])
                        else:
                            if c == 0:
                                w += region["xin"]
                            else:
                                w.append(mk_fst_chunk[c - 1])
                        mk_x = dma("scalar", tiles[i][:],
                                   src[gtile * P:(gtile + 1) * P, :],
                                   sd_x[c % 2], wait=w)
                mst = ln_stats_group([t[:] for t in tiles], col0, [mk_x])
                for i, tt in enumerate(tts):
                    col = col0 + i
                    ew = trdone.get(("f", tt % 2))
                    if pre and c == 0 and i == 0 and tts[0] == 0:
                        ew = [mk_b7_pe, mk_mamba_gp_end,
                              mk_mamba_ve_end, ew]
                    mkn = ln_norm(fxn[tt % 2][:], tiles[i][:], col, mst,
                                  ew, first=(i == 0))
                    bank = 4 + tt % 2
                    mk_tr = transpose_tile(fxn[tt % 2][:], bank, mkn,
                                           tr_banks[bank])
                    trdone[("f", tt % 2)] = mk_tr
                    tr_banks[bank] = evac_xnt(fxnt[c % 2], 512, tt, bank,
                                              mk_tr)
                mk_xnt_f[c] = tr_banks[4 + tts[-1] % 2]

            if pre:
                f_ln_tiles(0, [0, 1, 2, 3], 8)
            else:
                f_ln_tiles(0, [0, 1], 8)
                f_ln_tiles(0, [2, 3], 10)

            mk_wg = [None] * ND
            mk_wu = [None] * ND
            prev_cast = [None]
            if first:
                for gi in range(16, 32):
                    emit_w_unit(gi)
                for dj in range(ND):
                    mk_wg[dj] = w_mk[2 * dj + 1]
                    mk_wu[dj] = w_mk[16 + 2 * dj + 1]
                prev_cast[0] = w_mk[31]
                for dstw, cache, mkk in ((wg_bf, wg_c, mk_wg),
                                         (wu_bf, wu_c, mk_wu)):
                    for dj in range(ND):
                        em.op("scalar",
                              lambda e, dj=dj, dstw=dstw, cache=cache: e.dma_start(
                                  cache[dj * P:(dj + 1) * P, :],
                                  dstw[:, dj * H:(dj + 1) * H]),
                              wait=[mkk[dj]], inc=[(sd_cache, 16)],
                              drain=False)
            else:
                for m, (cache, dstw, mkk, gsem) in enumerate(
                        ((wg_c, wg_bf, mk_wg, sd_g1), (wu_c, wu_bf, mk_wu, sd_g2))):
                    mkreg = region["wg"] if m == 0 else region["wu"]
                    for dj in range(ND):
                        mkk[dj] = dma("sync", dstw[:, dj * H:(dj + 1) * H],
                                      cache[dj * P:(dj + 1) * P, :], gsem,
                                      wait=mkreg)
                    mkk[ND - 1] = gsem.mark()

            gu_banks = [(0, 1), (2, 3)]
            cache_mk = {}
            mk_cache_all = []
            prev_wd_cons = {0: None, 1: None}
            for c in range(NKF):
                par = c % 2
                mk_h = {}
                if c == 0:
                    silu_mk = {}
                    for phase in range(2):
                        mkw = mk_wg if phase == 0 else mk_wu
                        for hi in range(NH):
                            bk = gu_banks[hi % 2][phase]
                            if phase == 0:
                                w = [mk_xnt_f[c], mkw[ND - 1],
                                     gu_mark[hi % 2]]
                            else:
                                w = [mkw[ND - 1], silu_mk[hi],
                                     mk_h.get(hi - 2)]
                            wsrc = wg_bf if phase == 0 else wu_bf
                            for dj in range(ND):
                                em.op("tensor",
                                      lambda e, hi=hi, dj=dj, bk=bk, par=par, wsrc=wsrc: e.matmul(
                                          psb[bk],
                                          wsrc[:, dj * H + hi * P: dj * H + (hi + 1) * P],
                                          fxnt[par][:, dj * 512:(dj + 1) * 512],
                                          start=(dj == 0), stop=(dj == ND - 1)),
                                      wait=(w if dj == 0 else ()),
                                      inc=([(s_pe, 1)] if dj == ND - 1 else ()))
                            mk_mm = s_pe.mark()
                            if phase == 0:
                                em.op("scalar",
                                      lambda e, hi=hi, bk=bk: e.activation(
                                          h_sb[:, hi * 512:(hi + 1) * 512],
                                          psb[bk], AF.Silu, scale=1.0 / 127.0),
                                      wait=[mk_mm] + (region["h"] if hi == 0 else []),
                                      inc=[(s_se, 1)], drain=False)
                                gu_mark[hi % 2] = silu_mk[hi] = s_se.mark()
                            else:
                                em.op("vector",
                                      lambda e, hi=hi, bk=bk: e.tensor_mul(
                                          h_sb[:, hi * 512:(hi + 1) * 512],
                                          psb[bk],
                                          h_sb[:, hi * 512:(hi + 1) * 512]),
                                      wait=[mk_mm], inc=[(s_ve, 1)],
                                      drain=False)
                                gu_mark[hi % 2] = mk_h[hi] = s_ve.mark()
                else:
                    for hi in range(NH):
                        bks = gu_banks[hi % 2]
                        w = [mk_xnt_f[c], gu_mark[hi % 2]]
                        for phase in range(2):
                            wsrc = wg_bf if phase == 0 else wu_bf
                            bk = bks[phase]
                            for dj in range(ND):
                                em.op("tensor",
                                      lambda e, hi=hi, dj=dj, bk=bk, par=par, wsrc=wsrc: e.matmul(
                                          psb[bk],
                                          wsrc[:, dj * H + hi * P: dj * H + (hi + 1) * P],
                                          fxnt[par][:, dj * 512:(dj + 1) * 512],
                                          start=(dj == 0), stop=(dj == ND - 1)),
                                      wait=(w if (dj == 0 and phase == 0) else ()),
                                      inc=([(s_pe, 1)] if (dj == ND - 1 and phase == 1) else ()))
                        mk_mm = s_pe.mark()
                        em.op("scalar",
                              lambda e, hi=hi, bk=bks[0]: e.activation(
                                  gs[hi % 2][:], psb[bk], AF.Silu,
                                  scale=1.0 / 127.0),
                              wait=[mk_mm], inc=[(s_se, 1)], drain=False)
                        mk_sil = s_se.mark()
                        em.op("vector",
                              lambda e, hi=hi, bk=bks[1]: e.tensor_mul(
                                  h_sb[:, hi * 512:(hi + 1) * 512],
                                  gs[hi % 2][:], psb[bk]),
                              wait=[mk_sil], inc=[(s_ve, 1)], drain=False)
                        gu_mark[hi % 2] = mk_h[hi] = s_ve.mark()
                mk_h_all = mk_h[NH - 1]
                if c + 1 < NKF and not (first and c == 0):
                    f_ln_tiles(c + 1, [0, 1], 8)

                dn_bk = [(0, 1), (2, 3), (6, 7), (4, 5)]
                mk_wd = [None] * NH
                wd_cons = {}
                for hi in range(NH):
                    slot = hi % 2
                    w = []
                    if hi >= 2:
                        w.append(wd_cons[hi - 2])
                        if cache_mk.get(hi - 2):
                            w.append(cache_mk[hi - 2])
                    else:
                        w.append(prev_wd_cons[slot])
                        if c == 0:
                            w += region["wd"]
                    wdsem = sd_wc if slot == 0 else sd_w4
                    wdq = "sync" if slot == 0 else "scalar"
                    if first and c == 0:
                        mkd = dma(wdq, stg_wd[slot][:],
                                  wd_t[hi * P:(hi + 1) * P, :], wdsem,
                                  wait=w + ([prev_cast[0]] if (hi < 2 and prev_cast[0]) else []))
                        em.op("gpsimd",
                              lambda e, slot=slot: e.tensor_copy(
                                  wd_sb[slot][:], stg_wd[slot][:]),
                              wait=[mkd], inc=[(s_gp, 1)], drain=False)
                        mk_wd[hi] = s_gp.mark()
                        cache_mk[hi] = dma("gpsimd",
                                           wd_c[hi * P:(hi + 1) * P, :],
                                           wd_sb[slot][:],
                                           sd_ca0 if slot == 0 else sd_ca1,
                                           wait=[mk_wd[hi]])
                    else:
                        if first:
                            w += mk_cache_all
                        mk_wd[hi] = dma(wdq, wd_sb[slot][:],
                                        wd_c[hi * P:(hi + 1) * P, :], wdsem,
                                        wait=w)
                    for tt in range(NTF):
                        for half in range(2):
                            bk = dn_bk[tt][half]
                            w2 = [mk_h_all, mk_wd[hi]]
                            if hi == 0:
                                w2.append(gu_mark[0] if bk in (0, 1, 2, 3) else None)
                                w2.append(gu_mark[1] if bk in (2, 3) else None)
                                w2.append(tr_banks.get(bk) if bk in (4, 5) else None)
                                w2.append(dn_ev.get(bk) if bk in (6, 7) else None)
                            em.op("tensor",
                                  lambda e, hi=hi, tt=tt, half=half, bk=bk, slot=slot: e.matmul(
                                      psb[bk],
                                      h_sb[:, hi * 512 + tt * P: hi * 512 + (tt + 1) * P],
                                      wd_sb[slot][:, half * 512:(half + 1) * 512],
                                      start=(hi == 0), stop=(hi == NH - 1)),
                                  wait=(w2 if hi == 0 or (tt == 0 and half == 0) else ()),
                                  inc=([(s_pe, 1)] if (tt == NTF - 1 and half == 1) else ()))
                    wd_cons[hi] = s_pe.mark()
                prev_wd_cons[0] = wd_cons[NH - 2]
                prev_wd_cons[1] = wd_cons[NH - 1]
                mk_dn = s_pe.mark()
                for tt in range(NTF):
                    fxc = fxt(c, tt)
                    for half in range(2):
                        bk = dn_bk[tt][half]
                        em.op("vector",
                              lambda e, fxc=fxc, half=half, bk=bk: e.scalar_tensor_tensor(
                                  fxc[:, half * 512:(half + 1) * 512],
                                  psb[bk], 1.0 / (127.0 * 127.0),
                                  fxc[:, half * 512:(half + 1) * 512],
                                  op0=ALU.mult, op1=ALU.add),
                              wait=[mk_dn], inc=[(s_ve, 1)], drain=False)
                    mk_ev = s_ve.mark()
                    for bk in dn_bk[tt]:
                        dn_ev[bk] = mk_ev
                    gtile = c * NTF + tt
                    mk_fstore[gtile] = dma(
                        "scalar", dst[gtile * P:(gtile + 1) * P, :],
                        fxc[:], sd_st[c % 2], wait=[mk_ev])
                    mk_fst_chunk[c] = mk_fstore[gtile]
                if c == 0:
                    mk_cache_all.append(sd_cache.mark())
                    mk_cache_all.append(sd_ca0.mark())
                    mk_cache_all.append(sd_ca1.mark())
                    mk_cache_all.append(sd_wc.mark())
                    mk_cache_all.append(sd_w4.mark())
                if c + 1 < NKF:
                    if first and c == 0:
                        f_ln_tiles(c + 1, [0, 1], 8)
                    f_ln_tiles(c + 1, [2, 3], 10)
                gu_mark[0] = dn_ev[0]
                gu_mark[1] = dn_ev[2]
                tr_banks[4] = dn_ev[4]
                tr_banks[5] = dn_ev[4]
            return mk_fstore

        if N_SUBLAYERS >= 2:
            region1 = {
                "wg": [mk_xz_last],
                "wu": [mk_mamba_pe_end, mk_mamba_ve_end, mk_mamba_sta,
                       mk_mamba_stb],
                "wd": [mk_mamba_pe_end, mk_mamba_ve_end, mk_mamba_sta,
                       mk_mamba_stb],
                "xin": [mk_mamba_gp_end, mk_mamba_ve_end],
                "xnt": [mk_mamba_gp_end, mk_mamba_ve_end],
                "h": [mk_mamba_pe_end, mk_mamba_gp_end],
            }
            dst2 = xs2 if N_SUBLAYERS > 2 else out_ext
            mk_f1_store = ffn(xs1, dst2, True, region1,
                              lambda gt: [mk_mamba_sta, mk_mamba_stb],
                              pre=ffn1_pre)
            mk_f1_pe = s_pe.mark()
            mk_f1_ve = s_ve.mark()
            mk_f1_sta = sd_sta.mark()
            mk_f1_stb = sd_stb.mark()
            mk_f1_cache = sd_cache.mark()
            mk_f1_ca0 = sd_ca0.mark()
            mk_f1_ca1 = sd_ca1.mark()

        if N_SUBLAYERS >= 3:
            a_xnt = VA(T_XNT, 32768, BF16)
            a_kt = VA(T_KT, 32768, BF16)
            a_v = VA(T_V, 65536 - 32768, BF16)
            a_wt = [VA(T_WT + i * 16384, 16384, BF16) for i in range(2)]
            a_probs = [VA(T_PROBS + i * 4096, 4096, BF16) for i in range(2)]
            a_probt = VA(T_PROBT, 4096, BF16)
            a_x = [VA(T_X + i * 4096, 4096, F32) for i in range(2)]
            a_xn = [VA(T_XN + i * 2048, 2048, BF16) for i in range(2)]
            a_xpno = [VA(T_XPNO + i * 4096, 4096, F32) for i in range(2)]
            a_xpni = [VA(T_XPNI + i * 4096, 4096, F32) for i in range(2)]
            a_xout = [VA(T_XOUT + i * 4096, 4096, F32) for i in range(2)]
            wt_stage = [V(A0 + T_PROBS, 4096, F32),
                        V(A0 + T_PROBS + 4096, 4096, F32)]

            mk_xpn_st = [None] * NT
            prev_x = {0: None, 1: None}
            prev_xpno = {0: None, 1: None}
            a_trb = {4: mk_f1_ve, 5: mk_f1_ve}
            for tpair in range(NT // 2):
                mk_x = []
                for slot in range(2):
                    t = 2 * tpair + slot
                    w = [mk_f1_sta, mk_f1_stb, prev_x[slot]]
                    if t < 2:
                        w.append(mk_f1_ve)
                    mk_x.append(dma("scalar", a_x[slot][:],
                                    xs2[t * P:(t + 1) * P, :],
                                    sd_x[slot], wait=w))
                mst = ln_stats_group([a_x[0][:], a_x[1][:]], 12, mk_x,
                                     extra_1pr=True)
                for slot in range(2):
                    t = 2 * tpair + slot
                    col = 12 + slot
                    mkn = ln_norm(a_xn[slot][:], a_x[slot][:], col, mst,
                                  trdone.get(("a", slot)),
                                  first=(slot == 0))
                    em.op("vector",
                          lambda e, slot=slot, j=col: e.tensor_scalar(
                              a_xpno[slot][:], a_x[slot][:],
                              g_1pr[:, j:j + 1], g_nmr[:, j:j + 1],
                              op0=ALU.mult, op1=ALU.add),
                          wait=[prev_xpno[slot]], inc=[(s_ve, 1)],
                          drain=False)
                    prev_x[slot] = s_ve.mark()
                    mk_xpn_st[t] = dma("gpsimd", xpn_d[t * P:(t + 1) * P, :],
                                       a_xpno[slot][:],
                                       sd_stp if slot == 0 else sd_stq,
                                       wait=[prev_x[slot]])
                    prev_xpno[slot] = mk_xpn_st[t]
                    bank = 4 + slot
                    mk_tr = transpose_tile(a_xn[slot][:], bank, mkn,
                                           a_trb[bank])
                    trdone[("a", slot)] = mk_tr
                    a_trb[bank] = evac_xnt(a_xnt, 2048, t, bank, mk_tr)
            mk_xnt_a = a_trb[4 + (NT - 1) % 2]
            mk_xnt_a2 = s_ve.mark()

            mk_wt = [[None] * ND, [None] * ND]
            prev_stg = {0: None, 1: None}
            wi = 0
            for halfw in range(2):
                for dj in range(ND):
                    slot = wi % 2
                    wi += 1
                    w = [prev_stg[slot]]
                    if halfw == 0 and dj < 2:
                        w += [mk_f1_ve, mk_f1_sta, mk_f1_stb, mk_f1_pe]
                    mkd = dma("sync", wt_stage[slot][:, 0:1024],
                              wt_at[dj * P:(dj + 1) * P,
                                    halfw * 1024:(halfw + 1) * 1024],
                              sd_wa if slot == 0 else sd_wb, wait=w)
                    em.op("vector",
                          lambda e, halfw=halfw, dj=dj, slot=slot: e.tensor_copy(
                              a_wt[halfw][:, dj * 1024:(dj + 1) * 1024],
                              wt_stage[slot][:, 0:1024]),
                          wait=[mkd] + ([mk_f1_pe] if (halfw == 0 and dj < 2) else []),
                          inc=[(s_ve, 1)], drain=False)
                    prev_stg[slot] = mk_wt[halfw][dj] = s_ve.mark()

            kv_ev = {b: None for b in range(4)}
            bi = 0
            for jc in range(ND):
                for n in range(4):
                    bk = bi % 4
                    bi += 1
                    w = [mk_xnt_a2, mk_wt[0][ND - 1], kv_ev[bk]]
                    for dj in range(ND):
                        em.op("tensor",
                              lambda e, jc=jc, n=n, dj=dj, bk=bk: e.matmul(
                                  psb[bk],
                                  a_wt[0][:, dj * 1024 + jc * P: dj * 1024 + (jc + 1) * P],
                                  a_xnt[:, dj * 2048 + n * 512: dj * 2048 + (n + 1) * 512],
                                  start=(dj == 0), stop=(dj == ND - 1)),
                              wait=(w if dj == 0 else ()),
                              inc=([(s_pe, 1)] if dj == ND - 1 else ()))
                    mk_mm = s_pe.mark()
                    em.op("vector",
                          lambda e, jc=jc, n=n, bk=bk: e.tensor_copy(
                              a_kt[:, jc * 2048 + n * 512: jc * 2048 + (n + 1) * 512],
                              psb[bk]),
                          wait=[mk_mm], inc=[(s_ve, 1)], drain=False)
                    kv_ev[bk] = s_ve.mark()
            for tt in range(NT):
                for half in range(2):
                    bk = bi % 4
                    bi += 1
                    w = [mk_wt[1][ND - 1], kv_ev[bk]]
                    for dj in range(ND):
                        em.op("tensor",
                              lambda e, tt=tt, half=half, dj=dj, bk=bk: e.matmul(
                                  psb[bk],
                                  a_xnt[:, dj * 2048 + tt * P: dj * 2048 + (tt + 1) * P],
                                  a_wt[1][:, dj * 1024 + half * 512: dj * 1024 + (half + 1) * 512],
                                  start=(dj == 0), stop=(dj == ND - 1)),
                              wait=(w if dj == 0 else ()),
                              inc=([(s_pe, 1)] if dj == ND - 1 else ()))
                    mk_mm = s_pe.mark()
                    em.op("vector",
                          lambda e, tt=tt, half=half, bk=bk: e.tensor_copy(
                              a_v[:, tt * 1024 + half * 512: tt * 1024 + (half + 1) * 512],
                              psb[bk]),
                          wait=[mk_mm], inc=[(s_ve, 1)], drain=False)
                    kv_ev[bk] = s_ve.mark()
            mk_kv_done = s_ve.mark()
            mk_xpn_all = sd_stp.mark()
            mk_xpn_all2 = sd_stq.mark()

            dst3 = xs3 if N_SUBLAYERS > 3 else out_ext
            mk_probt_ev = None
            mk_av_ev = [None]
            mk_xout_free = [None, None]
            prev_xpni = [None, None]
            mk_exp_hist = {-1: mk_kv_done, -2: mk_kv_done}
            qstate = {}

            def attn_av(qj):
                slotj = qj % 2
                mk_pt, mk_rcj, mk_xpnij = qstate[qj]
                for half in range(2):
                    bk = 6 + half
                    for ci in range(qj + 1):
                        em.op("tensor",
                              lambda e, ci=ci, half=half, bk=bk, qj=qj: e.matmul(
                                  psb[bk],
                                  a_probt[:, ci * P:(ci + 1) * P],
                                  a_v[:, ci * 1024 + half * 512: ci * 1024 + (half + 1) * 512],
                                  start=(ci == 0), stop=(ci == qj)),
                              wait=([mk_pt, mk_av_ev[0]] if (ci == 0 and half == 0) else ()),
                              inc=([(s_pe, 1)] if (ci == qj and half == 1) else ()))
                mk_av = s_pe.mark()
                for half in range(2):
                    em.op("vector",
                          lambda e, half=half, slotj=slotj, bk=6 + half: e.scalar_tensor_tensor(
                              a_xout[slotj][:, half * 512:(half + 1) * 512],
                              psb[bk], g_recip[:, qj:qj + 1],
                              a_xpni[slotj][:, half * 512:(half + 1) * 512],
                              op0=ALU.mult, op1=ALU.add),
                          wait=[mk_av, mk_rcj, mk_xpnij, mk_xout_free[slotj]],
                          inc=[(s_ve, 1)], drain=(half == 0))
                mk_av_ev[0] = s_ve.mark()
                prev_xpni[slotj] = mk_av_ev[0]
                mk_xout_free[slotj] = dma(
                    "scalar", dst3[qj * P:(qj + 1) * P, :], a_xout[slotj][:],
                    sd_st[slotj], wait=[mk_av_ev[0]])

            for qi in range(NT):
                width = (qi + 1) * P
                sbase = (qi % 2) * 1024 if qi <= 7 else 0
                nsl = (width + 511) // 512
                slot = qi % 2
                mk_xpni = dma("scalar", a_xpni[slot][:],
                              xpn_d[qi * P:(qi + 1) * P, :], sd_x[slot],
                              wait=[mk_xpn_all, mk_xpn_all2, prev_xpni[slot]])
                if qi <= 7:
                    w0 = [mk_xnt_a2, mk_exp_hist[qi - 2]]
                elif qi == 8:
                    w0 = [mk_xnt_a2, mk_exp_hist[6], mk_exp_hist[7]]
                else:
                    w0 = [mk_xnt_a2, mk_exp_hist[qi - 1]]
                for n in range(nsl):
                    ncols = min(width, (n + 1) * 512) - n * 512
                    for dj in range(ND):
                        em.op("tensor",
                              lambda e, qi=qi, n=n, dj=dj, ncols=ncols, sbase=sbase: e.matmul(
                                  ps_s[:, sbase + n * 512: sbase + n * 512 + ncols],
                                  a_xnt[:, dj * 2048 + qi * P: dj * 2048 + (qi + 1) * P],
                                  a_kt[:, dj * 2048 + n * 512: dj * 2048 + n * 512 + ncols],
                                  start=(dj == 0), stop=(dj == ND - 1)),
                              wait=(w0 if (dj == 0 and n == 0) else ()),
                              inc=([(s_pe, 1)] if (dj == ND - 1 and n == nsl - 1) else ()))
                mk_sc = s_pe.mark()
                em.op("vector",
                      lambda e, qi=qi, sbase=sbase: e.tensor_add(
                          ps_s[:, sbase + qi * P: sbase + (qi + 1) * P],
                          ps_s[:, sbase + qi * P: sbase + (qi + 1) * P],
                          g_mask[:]),
                      wait=[mk_sc])
                em.op("vector",
                      lambda e, width=width, sbase=sbase: e.reduce_max(
                          g_nmax[:, 0:1], ps_s[:, sbase: sbase + width],
                          axis=AX.X, negate=True),
                      inc=[(s_ve, 1)])
                mk_nm = s_ve.mark()
                em.op("scalar",
                      lambda e, qi=qi, width=width, slot=slot, sbase=sbase: e.activation(
                          a_probs[slot][:, 0:width],
                          ps_s[:, sbase: sbase + width],
                          AF.Exp, bias=g_nmax[:, 0:1], accum_out=g_rsum[:, 0:1]),
                      wait=[mk_nm, mk_probt_ev], inc=[(s_se, 1)])
                mk_exp = s_se.mark()
                mk_exp_hist[qi] = mk_exp
                em.op("vector",
                      lambda e, qi=qi: e.reciprocal(
                          g_recip[:, qi:qi + 1], g_rsum[:, 0:1]),
                      wait=[mk_exp], inc=[(s_ve, 1)], drain=False)
                mk_rc = s_ve.mark()
                if qi >= 1:
                    attn_av(qi - 1)
                ngr = (qi + 1 + 7) // 8
                mk_ptr_ev = mk_probt_ev
                for g in range(ngr):
                    cis = list(range(g * 8, min(qi + 1, g * 8 + 8)))
                    pbf = psb[4].bitcast(BF16)
                    for i, ci in enumerate(cis):
                        em.op("tensor",
                              lambda e, ci=ci, i=i, slot=slot, pbf=pbf: e.transpose(
                                  pbf[:, i * P:(i + 1) * P],
                                  a_probs[slot][:, ci * P:(ci + 1) * P],
                                  g_ident),
                              wait=([mk_exp, mk_ptr_ev] if i == 0 else ()),
                              inc=([(s_pe, 1)] if ci == cis[-1] else ()))
                    mk_tr = s_pe.mark()
                    dstp = ap3(a_probt, g * 8 * P, P, len(cis), P)
                    em.op("vector",
                          lambda e, dstp=dstp, pbf=pbf, ncis=len(cis): e.tensor_copy(
                              dstp, pbf[:, 0:ncis * P]),
                          wait=[mk_tr], inc=[(s_ve, 1)], drain=False)
                    mk_ptr_ev = s_ve.mark()
                mk_probt_ev = mk_ptr_ev
                qstate[qi] = (mk_probt_ev, mk_rc, mk_xpni)
            attn_av(NT - 1)
            mk_attn_ve = s_ve.mark()
            mk_attn_sta = sd_sta.mark()
            mk_attn_stb = sd_stb.mark()

        if N_SUBLAYERS >= 4:
            region2 = {
                "wg": [mk_attn_ve, mk_f1_cache, mk_f1_ca0, mk_f1_ca1],
                "wu": [mk_attn_ve, mk_f1_cache, mk_f1_ca0, mk_f1_ca1],
                "wd": [mk_attn_ve, mk_f1_cache, mk_f1_ca0, mk_f1_ca1],
                "xin": [mk_attn_ve, mk_attn_sta, mk_attn_stb],
                "xnt": [mk_attn_ve],
                "h": [mk_attn_ve, mk_attn_sta, mk_attn_stb],
            }
            ffn(xs3, out_ext, False, region2,
                lambda gt: [mk_attn_sta, mk_attn_stb])

        all_sems = [sd_sm, sd_wa, sd_wb, sd_wc, sd_w4, sd_cache, sd_ca0,
                    sd_ca1, sd_g1, sd_g2, sd_xa, sd_xb, sd_sta, sd_stb,
                    sd_stp, sd_stq, sd_stg, sd_m0, sd_m1]
        em.op("sync", lambda e: None,
              wait=[(sg, sg.n) for sg in all_sems])

        block = ctx.enter_context(nc.Block())

        @block.sync
        def _(e):
            for f in em.q["sync"]:
                f(e)

        @block.tensor
        def _(e):
            for f in em.q["tensor"]:
                f(e)

        @block.vector
        def _(e):
            for f in em.q["vector"]:
                f(e)

        @block.scalar
        def _(e):
            for f in em.q["scalar"]:
                f(e)

        @block.gpsimd
        def _(e):
            for f in em.q["gpsimd"]:
                f(e)

    return nc


_NC_CACHE = None


def kernel(L0_attn_a_x, L0_attn_a_W_in, L0_attn_a_conv_w, L0_attn_a_W_x,
           L0_attn_a_W_dt, L0_attn_a_b_dt, L0_attn_a_W_Bp, L0_attn_a_D_param,
           L0_attn_a_W_out, L0_attn_b_W, L0_ffn_w_gate, L0_ffn_w_up,
           L0_ffn_w_down):
    global _NC_CACHE
    x = np.ascontiguousarray(np.asarray(L0_attn_a_x, dtype=np.float32))
    w_in = np.ascontiguousarray(np.asarray(L0_attn_a_W_in, np.float32))
    conv_w = np.asarray(L0_attn_a_conv_w, np.float32).reshape(DI, 4)
    conv_wt = np.ascontiguousarray(
        conv_w.reshape(NCI, P, 4).transpose(1, 0, 2).reshape(P, NCI * 4))
    w_x = np.asarray(L0_attn_a_W_x, np.float32)
    w_x_pad = np.ascontiguousarray(
        w_x.reshape(NCI, P, 32).transpose(1, 0, 2).reshape(P, NCI * 32))
    w_dt = np.ascontiguousarray(np.concatenate(
        [np.asarray(L0_attn_a_W_dt, np.float32),
         np.asarray(L0_attn_a_b_dt, np.float32).reshape(1, DI)], axis=0))
    b_dt = np.ascontiguousarray(
        np.asarray(L0_attn_a_b_dt, np.float32).reshape(NCI, P).T)
    w_bp = np.ascontiguousarray(np.asarray(L0_attn_a_W_Bp, np.float32))
    d_par = np.ascontiguousarray(
        np.asarray(L0_attn_a_D_param, np.float32).reshape(NCI, P).T)
    w_out = np.ascontiguousarray(np.asarray(L0_attn_a_W_out, np.float32))
    wt_at = np.ascontiguousarray(np.asarray(L0_attn_b_W, np.float32).T)
    wg_t = np.ascontiguousarray(np.asarray(L0_ffn_w_gate, np.int32).T)
    wu_t = np.ascontiguousarray(np.asarray(L0_ffn_w_up, np.int32).T)
    wd_t = np.ascontiguousarray(np.asarray(L0_ffn_w_down, np.int32).T)
    ident = np.eye(P).astype(ml_dtypes.bfloat16)
    mask = np.where(np.tril(np.ones((P, P), bool)), 0.0,
                    -1e30).astype(np.float32)

    if _NC_CACHE is None:
        _NC_CACHE = build_nc()
    nc = _NC_CACHE

    common = {
        "w_in": w_in, "conv_wt": conv_wt, "w_x": w_x_pad, "w_dt": w_dt,
        "b_dt": b_dt, "w_bp": w_bp, "d_par": d_par, "w_out": w_out,
        "wt_at": wt_at, "wg_t": wg_t, "wu_t": wu_t, "wd_t": wd_t,
        "ident": ident, "mask": mask,
    }
    in_maps = [dict(common, x=np.ascontiguousarray(x[c])) for c in range(8)]
    res = run_bass_kernel_spmd(nc, in_maps, core_ids=list(range(8)))
    return np.stack([r["out"] for r in res.results]).astype(np.float32)



# version 30 (final)
# speedup vs baseline: 1.0293x; 1.0293x over previous
import numpy as np
import ml_dtypes
from contextlib import ExitStack

import concourse.bass as bass
import concourse.mybir as mybir
from concourse.bass_utils import run_bass_kernel_spmd

F32 = mybir.dt.float32
BF16 = mybir.dt.bfloat16
I32 = mybir.dt.int32
I8 = mybir.dt.int8
AF = mybir.ActivationFunctionType
ALU = mybir.AluOpType
AX = mybir.AxisListType

P = 128
D = 1024
ND = 8
DI = 2048
NCI = 16
L = 2048
NT = 16
H = 3072
NH = 24
TCM = 256
NKM = L // TCM
NTM = TCM // P
TCF = 512
NKF = L // TCF
NTF = TCF // P
EPS = 1e-5

N_SUBLAYERS = 4
CUT = None

ENGINES = ("sync", "tensor", "vector", "scalar", "gpsimd")

AB = 203584
G_IDENT = 0
G_MASK = 256
G_CONVW = 768
G_BDT = 1024
G_DPAR = 1088
G_NSUM = 1152
G_SSQ = 1216
G_VAR = 1280
G_RSTD = 1344
G_NMR = 1408
G_1PR = 1472
G_CARRY = 1536
G_MSQ = 1600
G_NMAX = 1664
G_RSUM = 1728
G_RECIP = 1792
G_SD = 1856
G_DUMP = 4096
A0 = 8192

M_WIN = 0
M_WOUT = 65536
M_WX = 98304
M_XIN = 100352
M_XN = 116736
M_XNT = 120832
M_XC = 129024
M_U = 137472
M_SZ = 145664
M_ACC = 153856
M_R2 = 155904
M_R3 = 160000
M_SP = 164096
M_A2 = 168192
M_UB = 172288
M_YS = 176384
M_Y = 180480
M_STAGE = 188672
M_END = 192768

F_WG = 0
F_STG = 49152
F_STG2 = 55296
F_WU = 65536
F_WD = 114688
F_XINB = 118784
F_XIN2 = 129024
F_XN = 145408
F_XNT = 149504
F_H = 165888
F_GS = 190464

T_XNT = 0
T_KT = 32768
T_V = 65536
T_WT = 98304
T_PROBS = 131072
T_PROBT = 139264
T_X = 143360
T_XN = 151552
T_XPNO = 155648
T_XPNI = 163840
T_XOUT = 172032
T_END = 180224


class Sem:
    __slots__ = ("h", "n")

    def __init__(self, h):
        self.h = h
        self.n = 0

    def mark(self):
        return (self, self.n)


class Em:
    def __init__(self):
        self.q = {e: [] for e in ENGINES}
        self.n_ops = 0

    def op(self, eng, fn, wait=(), inc=(), drain=True):
        self.n_ops += 1
        if CUT is not None and self.n_ops > CUT:
            return
        waits = []
        for m in wait:
            if m is None:
                continue
            s, v = m
            assert v <= s.n, f"forward wait on {eng}: {v} > {s.n}"
            if v > 0:
                waits.append((s.h, v))
        incs = []
        for s, d in inc:
            s.n += d
            incs.append((s.h, d))

        need_drain = drain and eng in ("vector", "scalar", "gpsimd")

        def run(e, fn=fn, waits=waits, incs=incs, nd=need_drain):
            if nd:
                e.drain()
            for h, v in waits:
                e.wait_ge(h, v)
            r = fn(e)
            for h, d in incs:
                r = r.then_inc(h, d)

        self.q[eng].append(run)


def ap3(view, off, blk_step, blk_n, inner_n):
    return bass.AP(view.tensor, view.offset + off,
                   [list(view.ap[0]), [blk_step, blk_n], [1, inner_n]])


def build_nc():
    nc = bass.Bass("TRN2", target_bir_lowering=False)

    x_ext = nc.dram_tensor("x", [L, D], F32, kind="ExternalInput")
    w_in = nc.dram_tensor("w_in", [D, 4096], F32, kind="ExternalInput")
    conv_wt = nc.dram_tensor("conv_wt", [P, 64], F32, kind="ExternalInput")
    w_x = nc.dram_tensor("w_x", [P, NCI * 32], F32, kind="ExternalInput")
    w_dt = nc.dram_tensor("w_dt", [17, DI], F32, kind="ExternalInput")
    b_dt = nc.dram_tensor("b_dt", [P, 16], F32, kind="ExternalInput")
    w_bp = nc.dram_tensor("w_bp", [16, DI], F32, kind="ExternalInput")
    d_par = nc.dram_tensor("d_par", [P, 16], F32, kind="ExternalInput")
    w_out = nc.dram_tensor("w_out", [DI, D], F32, kind="ExternalInput")
    wt_at = nc.dram_tensor("wt_at", [D, 2048], F32, kind="ExternalInput")
    wg_t = nc.dram_tensor("wg_t", [D, H], I32, kind="ExternalInput")
    wu_t = nc.dram_tensor("wu_t", [D, H], I32, kind="ExternalInput")
    wd_t = nc.dram_tensor("wd_t", [H, D], I32, kind="ExternalInput")
    ident_e = nc.dram_tensor("ident", [P, P], BF16, kind="ExternalInput")
    mask_e = nc.dram_tensor("mask", [P, P], F32, kind="ExternalInput")
    out_ext = nc.dram_tensor("out", [L, D], F32, kind="ExternalOutput")

    xs1 = nc.dram_tensor("xs1", [L, D], F32)
    xs2 = nc.dram_tensor("xs2", [L, D], F32)
    xs3 = nc.dram_tensor("xs3", [L, D], F32)
    xpn_d = nc.dram_tensor("xpn_d", [L, D], F32)
    wg_c = nc.dram_tensor("wg_c", [D, H], BF16)
    wu_c = nc.dram_tensor("wu_c", [D, H], BF16)
    wd_c = nc.dram_tensor("wd_c", [H, D], BF16)

    em = Em()

    with ExitStack() as ctx:
        arena = ctx.enter_context(nc.sbuf_tensor("arena", [P, AB], I8))
        wdt_bf = ctx.enter_context(nc.sbuf_tensor("wdt_bf", [17, DI], BF16))
        wbp_bf = ctx.enter_context(nc.sbuf_tensor("wbp_bf", [16, DI], BF16))
        xpd_bf = ctx.enter_context(nc.sbuf_tensor("xpd_bf", [32, TCM], BF16))
        xpb_bf = ctx.enter_context(nc.sbuf_tensor("xpb_bf", [16, TCM], BF16))

        def V(off, nbytes, dt):
            return arena[:, off:off + nbytes].bitcast(dt)

        def VA(off, nbytes, dt):
            return V(A0 + off, nbytes, dt)

        g_ident = V(G_IDENT, 256, BF16)
        g_mask = V(G_MASK, 512, F32)
        g_convw = V(G_CONVW, 256, F32)
        g_bdt = V(G_BDT, 64, F32)
        g_dpar = V(G_DPAR, 64, F32)
        g_nsum = V(G_NSUM, 64, F32)
        g_ssq = V(G_SSQ, 64, F32)
        g_var = V(G_VAR, 64, F32)
        g_rstd = V(G_RSTD, 64, F32)
        g_nmr = V(G_NMR, 64, F32)
        g_1pr = V(G_1PR, 64, F32)
        g_carry = V(G_CARRY, 64, F32)
        g_msq = V(G_MSQ, 64, F32)
        g_nmax = V(G_NMAX, 64, F32)
        g_rsum = V(G_RSUM, 64, F32)
        g_recip = V(G_RECIP, 64, F32)
        g_sd = V(G_SD, 64, F32)
        g_dump = V(G_DUMP, 2048, BF16)

        ps_s = ctx.enter_context(nc.psum_tensor("ps_s", [P, 2048], F32))
        ps4 = ctx.enter_context(nc.psum_tensor("ps4", [P, 512], F32))
        ps5 = ctx.enter_context(nc.psum_tensor("ps5", [P, 512], F32))
        ps6 = ctx.enter_context(nc.psum_tensor("ps6", [P, 512], F32))
        ps7 = ctx.enter_context(nc.psum_tensor("ps7", [P, 512], F32))
        psb = [ps_s[:, i * 512:(i + 1) * 512] for i in range(4)] + \
              [ps4[:], ps5[:], ps6[:], ps7[:]]

        sd_sm = Sem(ctx.enter_context(nc.semaphore("sd_sm")))
        sd_wa = Sem(ctx.enter_context(nc.semaphore("sd_wa")))
        sd_wb = Sem(ctx.enter_context(nc.semaphore("sd_wb")))
        sd_wc = Sem(ctx.enter_context(nc.semaphore("sd_wc")))
        sd_w4 = Sem(ctx.enter_context(nc.semaphore("sd_w4")))
        sd_cache = Sem(ctx.enter_context(nc.semaphore("sd_cache")))
        sd_ca0 = Sem(ctx.enter_context(nc.semaphore("sd_ca0")))
        sd_ca1 = Sem(ctx.enter_context(nc.semaphore("sd_ca1")))
        sd_g1 = Sem(ctx.enter_context(nc.semaphore("sd_g1")))
        sd_g2 = Sem(ctx.enter_context(nc.semaphore("sd_g2")))
        sd_xa = Sem(ctx.enter_context(nc.semaphore("sd_xa")))
        sd_xb = Sem(ctx.enter_context(nc.semaphore("sd_xb")))
        sd_sta = Sem(ctx.enter_context(nc.semaphore("sd_sta")))
        sd_stb = Sem(ctx.enter_context(nc.semaphore("sd_stb")))
        sd_stp = Sem(ctx.enter_context(nc.semaphore("sd_stp")))
        sd_stq = Sem(ctx.enter_context(nc.semaphore("sd_stq")))
        sd_stg = Sem(ctx.enter_context(nc.semaphore("sd_stg")))
        sd_m0 = Sem(ctx.enter_context(nc.semaphore("sd_m0")))
        sd_m1 = Sem(ctx.enter_context(nc.semaphore("sd_m1")))
        sd_x = [sd_xa, sd_xb]
        sd_st = [sd_sta, sd_stb]
        s_pe = Sem(ctx.enter_context(nc.semaphore("s_pe")))
        s_ve = Sem(ctx.enter_context(nc.semaphore("s_ve")))
        s_se = Sem(ctx.enter_context(nc.semaphore("s_se")))
        s_gp = Sem(ctx.enter_context(nc.semaphore("s_gp")))

        col_free = {}
        trdone = {}
        mk_eps = None

        def dma(queue, dst, src, sem, wait=()):
            em.op(queue, lambda e, d=dst, s=src: e.dma_start(d, s),
                  wait=wait, inc=[(sem, 16)])
            return sem.mark()

        def ln_stats_group(tiles, col0, wait_x, extra_1pr=False):
            n = len(tiles)
            J = slice(col0, col0 + n)
            cf = [col_free.get(col0 + i) for i in range(n)]
            for i, t in enumerate(tiles):
                em.op("scalar",
                      lambda e, s=t, j=col0 + i: e.activation(
                          g_dump[:, 0:1024], s, AF.Square,
                          accum_out=g_ssq[:, j:j + 1]),
                      wait=(list(wait_x) + cf) if i == 0 else (),
                      drain=(i > 0),
                      inc=([(s_se, 1)] if i == n - 1 else ()))
            mk_ssq = s_se.mark()
            for i, t in enumerate(tiles):
                em.op("vector",
                      lambda e, s=t, j=col0 + i: e.reduce_sum(
                          g_nsum[:, j:j + 1], s, axis=AX.X, negate=True),
                      wait=(list(wait_x) + cf) if i == 0 else (), drain=False,
                      inc=([(s_ve, 1)] if i == n - 1 else ()))
            em.op("vector", lambda e, J=J: e.tensor_scalar_mul(
                g_nsum[:, J], g_nsum[:, J], 1.0 / D))
            em.op("vector", lambda e, J=J: e.tensor_mul(
                g_msq[:, J], g_nsum[:, J], g_nsum[:, J]))
            em.op("vector", lambda e, J=J: e.scalar_tensor_tensor(
                g_var[:, J], g_ssq[:, J], 1.0 / D,
                g_msq[:, J], op0=ALU.mult, op1=ALU.subtract),
                wait=[mk_ssq], inc=[(s_ve, 1)])
            mk_var = s_ve.mark()
            em.op("scalar", lambda e, J=J: e.activation(
                g_sd[:, J], g_var[:, J], AF.Sqrt,
                bias=g_sd[:, 15:16]),
                wait=[mk_var, mk_eps], inc=[(s_se, 1)])
            mk_sd = s_se.mark()
            em.op("vector", lambda e, J=J: e.reciprocal(
                g_rstd[:, J], g_sd[:, J]), wait=[mk_sd], drain=False)
            em.op("vector", lambda e, J=J: e.tensor_mul(
                g_nmr[:, J], g_nsum[:, J], g_rstd[:, J]),
                inc=[(s_ve, 1)])
            if extra_1pr:
                em.op("vector", lambda e, J=J: e.tensor_scalar_add(
                    g_1pr[:, J], g_rstd[:, J], 1.0),
                    inc=[(s_ve, 1)], drain=False)
            return s_ve.mark()

        def ln_norm(dst_bf, src_tile, col, mk_stats, extra_wait=None,
                    first=True):
            ew = extra_wait if isinstance(extra_wait, list) else [extra_wait]
            em.op("vector", lambda e, d=dst_bf, s=src_tile, j=col:
                  e.tensor_scalar(d, s, g_rstd[:, j:j + 1],
                                  g_nmr[:, j:j + 1],
                                  op0=ALU.mult, op1=ALU.add),
                  wait=[mk_stats] + ew, inc=[(s_ve, 1)], drain=first)
            col_free[col] = s_ve.mark()
            return col_free[col]

        def transpose_tile(xn_bf_tile, bank, mk_xn, mk_bank_free):
            pbf = psb[bank].bitcast(BF16)
            for dj in range(ND):
                em.op("tensor",
                      lambda e, s=xn_bf_tile, dj=dj, pbf=pbf: e.transpose(
                          pbf[:, dj * P:(dj + 1) * P],
                          s[:, dj * P:(dj + 1) * P], g_ident),
                      wait=([mk_xn, mk_bank_free] if dj == 0 else ()),
                      inc=([(s_pe, 1)] if dj == ND - 1 else ()))
            return s_pe.mark()

        def evac_xnt(xnt_view, blk_step, tt, bank, mk_tr):
            pbf = psb[bank].bitcast(BF16)
            dst = ap3(xnt_view, tt * P, blk_step, ND, P)
            em.op("vector", lambda e, d=dst, s=pbf: e.tensor_copy(d, s),
                  wait=[mk_tr], inc=[(s_ve, 1)], drain=False)
            return s_ve.mark()

        xc = VA(M_XC, 8448, BF16)
        em.op("vector", lambda e: e.memset(ap3(xc, 0, 264, NCI, 3), 0.0),
              inc=[(s_ve, 1)], drain=False)
        em.op("vector", lambda e: e.memset(g_carry[:], 0.0), inc=[(s_ve, 1)],
              drain=False)
        em.op("vector", lambda e: e.memset(g_sd[:, 15:16], EPS),
              inc=[(s_ve, 1)], drain=False)
        mk_init = s_ve.mark()
        mk_eps = mk_init

        xin = [VA(M_XIN + i * 4096, 4096, F32) for i in range(4)]
        xnbf = [VA(M_XN + i * 2048, 2048, BF16) for i in range(2)]
        xnt = [VA(M_XNT + i * 4096, 4096, BF16) for i in range(2)]
        u_bf = VA(M_U, 8192, BF16)
        sz_bf = VA(M_SZ, 8192, BF16)
        acc_p = VA(M_ACC, 2048, F32)
        r2_p = [VA(M_R2 + i * 2048, 2048, F32) for i in range(2)]
        r3_p = [VA(M_R3 + i * 2048, 2048, F32) for i in range(2)]
        sp = [VA(M_SP + i * 2048, 2048, F32) for i in range(2)]
        a_f = [VA(M_A2 + i * 2048, 2048, F32) for i in range(2)]
        ub_f = [VA(M_UB + i * 2048, 2048, F32) for i in range(2)]
        ys_f = [VA(M_YS + i * 2048, 2048, F32) for i in range(2)]
        y_bf = VA(M_Y, 8192, BF16)

        mk_ln = [None] * (NKM + 1)
        mk_xnt = [None] * (NKM + 1)
        mk_cevax = [None, None]
        mk_store = [None] * NKM
        mk_y_done = [None]
        mk_gp_hist = {}

        def mamba_load_ln(k):
            par = k % 2
            for tt in range(NTM):
                if k >= 2:
                    w = [mk_cevax[par]]
                elif k == 1:
                    w = [mk_wx_stage]
                else:
                    w = []
                mk_x = dma("scalar", xin[par * 2 + tt][:],
                           x_ext[(k * NTM + tt) * P:(k * NTM + tt + 1) * P, :],
                           sd_x[par], wait=w)
            mst = ln_stats_group([xin[par * 2 + tt][:] for tt in range(NTM)],
                                 par * 2, [mk_x])
            mks = []
            for tt in range(NTM):
                col = par * 2 + tt
                mks.append(ln_norm(xnbf[tt][:], xin[par * 2 + tt][:], col,
                                   mst, trdone.get(("m", tt)),
                                   first=(tt == 0)))
            mk_ln[k] = mks

        def mamba_transpose(k, mk_free):
            par = k % 2
            banks = [4, 5]
            for tt in range(NTM):
                mk_tr = transpose_tile(xnbf[tt][:], banks[tt],
                                       mk_ln[k][tt], mk_free[tt])
                trdone[("m", tt)] = mk_tr
                mk_xnt[k] = evac_xnt(xnt[par], 256, tt, banks[tt], mk_tr)

        stage = VA(M_STAGE, 4096, F32)
        wx_bf = VA(M_WX, 1024, BF16)
        wx_stage = VA(M_Y, 2048, F32)
        dma("sync", g_ident, ident_e[:], sd_sm)
        dma("sync", g_mask, mask_e[:], sd_sm)
        dma("sync", g_convw, conv_wt[:], sd_sm)
        dma("sync", g_bdt, b_dt[:], sd_sm)
        dma("sync", g_dpar, d_par[:], sd_sm)
        mk_sm = dma("sync", wx_stage[:, 0:512], w_x[:], sd_sm)
        mk_sm_all = mk_sm
        mamba_load_ln(0)
        mamba_transpose(0, [mk_sm_all, mk_sm_all])

        em.op("vector", lambda e: e.tensor_copy(
            wx_bf[:], wx_stage[:, 0:512]), wait=[mk_sm], inc=[(s_ve, 1)],
            drain=False)
        mk_wx = s_ve.mark()
        mk_wx_stage = mk_wx
        prevc = None
        for m, (srcw, dstw, nrow) in enumerate(
                ((w_dt, wdt_bf, 17), (w_bp, wbp_bf, 16))):
            for hh in range(2):
                w = [prevc] if prevc else []
                mkd = dma("sync", stage[0:nrow, 0:1024],
                          srcw[:, hh * 1024:(hh + 1) * 1024], sd_g1, wait=w)
                em.op("vector",
                      lambda e, dstw=dstw, hh=hh, nrow=nrow: e.tensor_copy(
                          dstw[0:nrow, hh * 1024:(hh + 1) * 1024],
                          stage[0:nrow, 0:1024]),
                      wait=[mkd], inc=[(s_ve, 1)], drain=False)
                prevc = s_ve.mark()
        mk_wdtbp = prevc
        em.op("vector", lambda e: e.memset(xpd_bf[0:32, :], 1.0),
              inc=[(s_ve, 1)], drain=False)
        mk_ones = s_ve.mark()

        win_bf = VA(M_WIN, 65536, BF16)
        win_stage = [VA(M_U, 8192, F32), VA(M_SZ, 8192, F32),
                     VA(M_ACC, 8192, F32), VA(M_SP, 8192, F32)]
        win_sem = [sd_wa, sd_wb, sd_wc, sd_w4]
        mk_win_cast = [None] * ND
        chain = {i: None for i in range(4)}
        gi = 0
        for dj in range(ND):
            for hh in range(2):
                slot = gi % 4
                gi += 1
                mkd = dma("sync" if slot % 2 == 0 else "scalar",
                          win_stage[slot][:, 0:2048],
                          w_in[dj * P:(dj + 1) * P,
                               hh * 2048:(hh + 1) * 2048], win_sem[slot],
                          wait=[chain[slot]] if chain[slot] else [])
                em.op("vector",
                      lambda e, dj=dj, hh=hh, slot=slot: e.tensor_copy(
                          win_bf[:, dj * 4096 + hh * 2048:
                                 dj * 4096 + (hh + 1) * 2048],
                          win_stage[slot][:, 0:2048]),
                      wait=[mkd], inc=[(s_ve, 1)], drain=False)
                mk_win_cast[dj] = chain[slot] = s_ve.mark()

        wout_bf = VA(M_WOUT, 32768, BF16)
        wo_stage = [VA(M_UB, 4096, F32), VA(M_YS, 4096, F32)]
        wo_sem = [sd_wa, sd_wb]
        last_cast = {0: chain[0], 1: chain[1]}
        for ci in range(NCI):
            slot = ci % 2
            mkd = dma("sync", wo_stage[slot][:, 0:1024],
                      w_out[ci * P:(ci + 1) * P, :], wo_sem[slot],
                      wait=[last_cast[slot]] if last_cast[slot] else [])
            em.op("vector",
                  lambda e, ci=ci, slot=slot: e.tensor_copy(
                      wout_bf[:, ci * 1024:(ci + 1) * 1024],
                      wo_stage[slot][:, 0:1024]),
                  wait=[mkd], inc=[(s_ve, 1)], drain=False)
            last_cast[slot] = s_ve.mark()
        mk_startup = s_ve.mark()

        dst1 = xs1 if N_SUBLAYERS > 1 else out_ext

        mk_xz_last = None
        xzbank = {0: None, 1: None}
        mk_c67 = [None]
        mk_gp_acc = [None]
        mk_gp_conv = {0: None, 1: None}
        mk_xp_prev = [None]
        mk_scan_hist = {}
        mk_u = {}
        mk_sz = {}
        mk_sp30_hist = {}
        mk_sp2_hist = {}
        mk_ubm_hist = {}
        mk_xpe_k = {}
        mk_t1 = {-1: None}
        mk_r_hist = {}

        def phaseA_xc_pair(kk, p):
            par = kk % 2
            bkx = p % 2
            for half in range(2):
                ci = 2 * p + half
                wait_mm = [mk_xnt[kk], xzbank[bkx] if half == 0 else None]
                if kk == 0 and ci == 0:
                    wait_mm += [mk_win_cast[-1], mk_wdtbp, mk_wx, mk_ones]
                for dj in range(ND):
                    em.op("tensor",
                          lambda e, dj=dj, ci=ci, bkx=bkx, half=half, par=par: e.matmul(
                              psb[bkx][:, half * TCM:(half + 1) * TCM],
                              win_bf[:, dj * 4096 + ci * P: dj * 4096 + (ci + 1) * P],
                              xnt[par][:, dj * 256:(dj + 1) * 256],
                              start=(dj == 0), stop=(dj == ND - 1)),
                          wait=(wait_mm if dj == 0 else ()),
                          inc=([(s_pe, 1)] if (dj == ND - 1 and half == 1) else ()))
            mk_mm = s_pe.mark()
            em.op("scalar",
                  lambda e, p=p, bkx=bkx: e.copy(
                      ap3(xc, 2 * p * 264 + 3, 264, 2, TCM),
                      psb[bkx]),
                  wait=[mk_mm], inc=[(s_se, 1)], drain=False)
            xzbank[bkx] = s_se.mark()
            for half in range(2):
                ci = 2 * p + half
                em.op("vector",
                      lambda e, ci=ci, half=half: e.tensor_scalar(
                          acc_p[:, half * TCM:(half + 1) * TCM],
                          xc[:, ci * 264 + 0: ci * 264 + TCM],
                          g_convw[:, ci * 4: ci * 4 + 1], None,
                          op0=ALU.mult),
                      wait=([mk_gp_acc[0], xzbank[bkx]] if half == 0 else ()),
                      drain=(half == 0))
            for half in range(2):
                ci = 2 * p + half
                em.op("vector",
                      lambda e, ci=ci, half=half: e.scalar_tensor_tensor(
                          acc_p[:, half * TCM:(half + 1) * TCM],
                          xc[:, ci * 264 + 1: ci * 264 + 1 + TCM],
                          g_convw[:, ci * 4 + 1: ci * 4 + 2],
                          acc_p[:, half * TCM:(half + 1) * TCM],
                          op0=ALU.mult, op1=ALU.add),
                      drain=(half == 0),
                      inc=([(s_ve, 1)] if half == 1 else ()))
            mk_acc = s_ve.mark()
            pp = p % 2
            for half in range(2):
                ci = 2 * p + half
                em.op("scalar",
                      lambda e, ci=ci, half=half, pp=pp: e.mul(
                          r2_p[pp][:, half * TCM:(half + 1) * TCM],
                          xc[:, ci * 264 + 2: ci * 264 + 2 + TCM],
                          g_convw[:, ci * 4 + 2: ci * 4 + 3]),
                      wait=([xzbank[bkx], mk_gp_conv[pp]] if half == 0 else ()),
                      drain=False)
                em.op("scalar",
                      lambda e, ci=ci, half=half, pp=pp: e.mul(
                          r3_p[pp][:, half * TCM:(half + 1) * TCM],
                          xc[:, ci * 264 + 3: ci * 264 + 3 + TCM],
                          g_convw[:, ci * 4 + 3: ci * 4 + 4]),
                      drain=False,
                      inc=([(s_se, 1)] if half == 1 else ()))
            mk_r = s_se.mark()
            mk_r_hist[(kk, p)] = mk_r
            em.op("gpsimd",
                  lambda e, pp=pp: e.tensor_add(
                      r2_p[pp][:], r2_p[pp][:], acc_p[:]),
                  wait=[mk_acc, mk_r], inc=[(s_gp, 1)], drain=False)
            mk_gp_acc[0] = s_gp.mark()
            em.op("gpsimd",
                  lambda e, pp=pp: e.tensor_add(
                      r2_p[pp][:], r2_p[pp][:], r3_p[pp][:]),
                  inc=[(s_gp, 1)])
            mk_gp_conv[pp] = s_gp.mark()
            w_sil = [mk_gp_conv[pp]]
            if kk >= 1:
                w_sil += [mk_gp_hist.get((kk - 1, p)), mk_xp_prev[0]]
            else:
                w_sil.append(mk_startup)
            em.op("scalar",
                  lambda e, p=p, pp=pp: e.activation(
                      u_bf[:, 2 * p * TCM: 2 * (p + 1) * TCM],
                      r2_p[pp][:], AF.Silu),
                  wait=w_sil, inc=[(s_se, 1)], drain=False)
            mk_u[(kk, p)] = s_se.mark()

        def phaseA_z_pair(kk, zp):
            par = kk % 2
            bkx = zp % 2
            for half in range(2):
                ci = NCI + 2 * zp + half
                wait_mm = [mk_xnt[kk], xzbank[bkx] if half == 0 else None]
                for dj in range(ND):
                    em.op("tensor",
                          lambda e, dj=dj, ci=ci, bkx=bkx, half=half, par=par: e.matmul(
                              psb[bkx][:, half * TCM:(half + 1) * TCM],
                              win_bf[:, dj * 4096 + ci * P: dj * 4096 + (ci + 1) * P],
                              xnt[par][:, dj * 256:(dj + 1) * 256],
                              start=(dj == 0), stop=(dj == ND - 1)),
                          wait=(wait_mm if dj == 0 else ()),
                          inc=([(s_pe, 1)] if (dj == ND - 1 and half == 1) else ()))
            mk_mm = s_pe.mark()
            w_sil = [mk_mm]
            if kk >= 1:
                w_sil.append(mk_gp_hist.get((kk - 1, zp)))
            elif zp == 0:
                w_sil.append(mk_startup)
            em.op("scalar",
                  lambda e, zp=zp, bkx=bkx: e.activation(
                      sz_bf[:, 2 * zp * TCM: 2 * (zp + 1) * TCM],
                      psb[bkx], AF.Silu),
                  wait=w_sil, inc=[(s_se, 1)], drain=False)
            xzbank[bkx] = mk_sz[(kk, zp)] = s_se.mark()

        def xp_sec(kk):
            em.op("vector",
                  lambda e: e.tensor_copy(
                      ap3(xc, 0, 264, NCI, 3),
                      ap3(xc, 256, 264, NCI, 3)),
                  wait=[mk_r_hist[(kk, 7)]])
            for ci in range(NCI):
                em.op("tensor",
                      lambda e, ci=ci: e.matmul(
                          psb[5][0:16, 0:TCM],
                          wx_bf[:, ci * 32: ci * 32 + 16],
                          u_bf[:, ci * TCM:(ci + 1) * TCM],
                          start=(ci == 0), stop=(ci == NCI - 1)),
                      wait=[mk_u[(kk, ci // 2)]])
                em.op("tensor",
                      lambda e, ci=ci: e.matmul(
                          psb[4][0:16, 0:TCM],
                          wx_bf[:, ci * 32 + 16: ci * 32 + 32],
                          u_bf[:, ci * TCM:(ci + 1) * TCM],
                          start=(ci == 0), stop=(ci == NCI - 1)),
                      inc=([(s_pe, 1)] if ci == NCI - 1 else ()))
            mk_xp = s_pe.mark()
            mk_xp_prev[0] = mk_xp
            em.op("vector", lambda e: e.tensor_copy(
                xpd_bf[0:16, :], psb[5][0:16, 0:TCM]), wait=[mk_xp],
                drain=False)
            em.op("vector", lambda e: e.tensor_copy(
                xpb_bf[0:16, :], psb[4][0:16, 0:TCM]), inc=[(s_ve, 1)],
                drain=False)
            mk_xpe_k[kk] = s_ve.mark()

        def wout_t0(k, pi):
            w0 = [mk_gp_hist[(k, pi)]]
            if pi == 0:
                w0.append(mk_c67[0])
            for j in range(2):
                ci = 2 * pi + j
                for half in range(2):
                    em.op("tensor",
                          lambda e, ci=ci, half=half, bk=6 + half: e.matmul(
                              psb[bk],
                              y_bf[:, ci * TCM: ci * TCM + P],
                              wout_bf[:, ci * 1024 + half * 512: ci * 1024 + (half + 1) * 512],
                              start=(ci == 0), stop=(ci == NCI - 1)),
                          wait=(w0 if (j == 0 and half == 0) else ()),
                          inc=([(s_pe, 1)] if (j == 1 and half == 1) else ()))

        def phaseB_pair(k, pi):
            ci0 = 2 * pi
            pp = pi % 2
            w0 = [mk_xpe_k[k], mk_sp30_hist.get((k, pi - 1)),
                  mk_ubm_hist.get((k, pi - 1))]
            for half in range(2):
                ci = ci0 + half
                em.op("tensor",
                      lambda e, ci=ci, half=half: e.matmul(
                          psb[2][:, half * TCM:(half + 1) * TCM],
                          wdt_bf[0:17, ci * P:(ci + 1) * P],
                          xpd_bf[0:17, :], start=True, stop=True),
                      wait=(w0 if half == 0 else ()))
                em.op("tensor",
                      lambda e, ci=ci, half=half: e.matmul(
                          psb[3][:, half * TCM:(half + 1) * TCM],
                          wbp_bf[0:16, ci * P:(ci + 1) * P],
                          xpb_bf[0:16, :], start=True, stop=True),
                      inc=([(s_pe, 1)] if half == 1 else ()))
            mk_mm = s_pe.mark()
            if pi >= 1:
                wout_t0(k, pi - 1)
            em.op("vector",
                  lambda e, pp=pp: e.tensor_scalar(
                      sp[pp][:], psb[2][:], 30.0, None, op0=ALU.min),
                  wait=[mk_mm, mk_sp2_hist.get((k, pi - 2))],
                  inc=[(s_ve, 1)], drain=False)
            mk_sp30 = mk_sp30_hist[(k, pi)] = s_ve.mark()
            em.op("scalar", lambda e, pp=pp: e.activation(
                sp[pp][:], sp[pp][:], AF.Exp),
                wait=[mk_sp30, mk_scan_hist.get((k, pi - 2))], drain=False)
            em.op("scalar", lambda e, pp=pp: e.activation(
                sp[pp][:], sp[pp][:], AF.Ln, bias=1.0))
            em.op("scalar", lambda e, pp=pp: e.activation(
                a_f[pp][:], sp[pp][:], AF.Relu,
                scale=-1.0, bias=1.0), inc=[(s_se, 1)])
            mk_sp2 = mk_sp2_hist[(k, pi)] = s_se.mark()
            prev_gp = mk_gp_hist.get((k, pi - 2)) or \
                (mk_gp_hist.get((k - 1, 6 + pi)) if k > 0 else None)
            em.op("vector",
                  lambda e, ci0=ci0, pp=pp: e.tensor_mul(
                      ub_f[pp][:],
                      u_bf[:, ci0 * TCM:(ci0 + 2) * TCM],
                      psb[3][:]),
                  wait=[mk_u[(k, pi)], prev_gp], inc=[(s_ve, 1)], drain=False)
            mk_ubm_hist[(k, pi)] = s_ve.mark()
            em.op("vector", lambda e, pp=pp: e.tensor_scalar(
                a_f[pp][:], a_f[pp][:], 0.9999, None,
                op0=ALU.min), wait=[mk_sp2], drain=False)
            for half in range(2):
                ci = ci0 + half
                em.op("vector",
                      lambda e, ci=ci, half=half, pp=pp: e.tensor_tensor_scan(
                          ys_f[pp][:, half * TCM:(half + 1) * TCM],
                          a_f[pp][:, half * TCM:(half + 1) * TCM],
                          ub_f[pp][:, half * TCM:(half + 1) * TCM],
                          g_carry[:, ci:ci + 1],
                          op0=ALU.mult, op1=ALU.add),
                      wait=([prev_gp] if half == 0 else ()),
                      drain=(half == 0),
                      inc=([(s_ve, 1)] if half == 1 else ()))
            mk_scan = mk_scan_hist[(k, pi)] = s_ve.mark()
            for half in range(2):
                ci = ci0 + half
                em.op("gpsimd", lambda e, ci=ci, half=half, pp=pp: e.tensor_copy(
                    g_carry[:, ci:ci + 1],
                    ys_f[pp][:, (half + 1) * TCM - 1:(half + 1) * TCM]),
                    wait=([mk_scan] if half == 0 else ()), drain=False)
            for half in range(2):
                ci = ci0 + half
                em.op("vector",
                      lambda e, ci=ci, half=half, pp=pp: e.scalar_tensor_tensor(
                          ub_f[pp][:, half * TCM:(half + 1) * TCM],
                          u_bf[:, ci * TCM:(ci + 1) * TCM],
                          g_dpar[:, ci:ci + 1],
                          ys_f[pp][:, half * TCM:(half + 1) * TCM],
                          op0=ALU.mult, op1=ALU.add),
                      drain=(half == 0),
                      inc=([(s_ve, 1)] if half == 1 else ()))
            mk_stt = s_ve.mark()
            w_y = [mk_stt, mk_sz.get((k, pi))]
            if k >= 1 and pi == 0:
                w_y.append(mk_t1[k - 1])
            em.op("gpsimd",
                  lambda e, ci0=ci0, pp=pp: e.tensor_mul(
                      y_bf[:, ci0 * TCM:(ci0 + 2) * TCM],
                      ub_f[pp][:],
                      sz_bf[:, ci0 * TCM:(ci0 + 2) * TCM]),
                  wait=w_y, inc=[(s_gp, 1)], drain=False)
            mk_gp_hist[(k, pi)] = s_gp.mark()

        def mamba_tail(k):
            par = k % 2
            mk_t0_mm = s_pe.mark()
            for half in range(2):
                em.op("vector",
                      lambda e, half=half, par=par, bk=6 + half: e.tensor_add(
                          xin[par * 2][:, half * 512:(half + 1) * 512],
                          xin[par * 2][:, half * 512:(half + 1) * 512],
                          psb[bk]),
                      wait=[mk_t0_mm], inc=[(s_ve, 1)], drain=False)
            ev0 = s_ve.mark()
            st_sem = sd_m0 if k == 0 else (sd_m1 if k == 1 else sd_st[par])
            mk_store[k] = dma("scalar",
                              dst1[(k * NTM) * P:(k * NTM + 1) * P, :],
                              xin[par * 2][:], st_sem, wait=[ev0])
            for ci in range(NCI):
                for half in range(2):
                    em.op("tensor",
                          lambda e, ci=ci, half=half, bk=6 + half: e.matmul(
                              psb[bk],
                              y_bf[:, ci * TCM + P: ci * TCM + 2 * P],
                              wout_bf[:, ci * 1024 + half * 512: ci * 1024 + (half + 1) * 512],
                              start=(ci == 0), stop=(ci == NCI - 1)),
                          wait=([ev0] if (ci == 0 and half == 0) else ()),
                          inc=([(s_pe, 1)] if (ci == NCI - 1 and half == 1) else ()))
            mk_t1[k] = s_pe.mark()
            for half in range(2):
                em.op("vector",
                      lambda e, half=half, par=par, bk=6 + half: e.tensor_add(
                          xin[par * 2 + 1][:, half * 512:(half + 1) * 512],
                          xin[par * 2 + 1][:, half * 512:(half + 1) * 512],
                          psb[bk]),
                      wait=[mk_t1[k]], inc=[(s_ve, 1)], drain=False)
            ev1 = s_ve.mark()
            mk_c67[0] = ev1
            mk_store[k] = dma("scalar",
                              dst1[(k * NTM + 1) * P:(k * NTM + 2) * P, :],
                              xin[par * 2 + 1][:], st_sem, wait=[ev1])
            mk_cevax[par] = mk_store[k]

        for p in range(8):
            phaseA_xc_pair(0, p)
        for zp in range(8):
            phaseA_z_pair(0, zp)
        mk_xz_last = s_pe.mark()
        xp_sec(0)
        mamba_load_ln(1)
        mamba_transpose(1, [mk_xpe_k[0], mk_xpe_k[0]])

        wg_bf_g = VA(F_WG, 49152, BF16)
        wu_bf_g = VA(F_WU, 49152, BF16)
        wstg = [VA(F_STG, 6144, I32), VA(F_STG2, 6144, I32),
                VA(F_XIN2 + 2 * 4096, 6144, I32)]
        wsem = [sd_wa, sd_wb, sd_wc]
        w_prev = {0: None, 1: None, 2: None}
        w_mk = {}

        def emit_w_unit(gi):
            m2, r = divmod(gi, 16)
            dj, hh = divmod(r, 2)
            slot = (r % 2) if m2 == 0 else (r % 3)
            srcw = wg_t if m2 == 0 else wu_t
            dstw = wg_bf_g if m2 == 0 else wu_bf_g
            w = [mk_xz_last]
            if slot == 2:
                w.append(mk_mamba_pe_end)
            if w_prev[slot]:
                w.append(w_prev[slot])
            mkd = dma("sync", wstg[slot][:],
                      srcw[dj * P:(dj + 1) * P, hh * 1536:(hh + 1) * 1536],
                      wsem[slot], wait=w)
            cw = [mkd]
            if m2 == 1 and r == 0:
                cw += [mk_mamba_pe_end, mk_mamba_ve_end, mk_mamba_sta,
                       mk_mamba_stb]
            em.op("gpsimd",
                  lambda e, dj=dj, hh=hh, dstw=dstw, slot=slot: e.tensor_copy(
                      dstw[:, dj * H + hh * 1536: dj * H + (hh + 1) * 1536],
                      wstg[slot][:]),
                  wait=cw, inc=[(s_gp, 1)], drain=False)
            w_prev[slot] = w_mk[gi] = s_gp.mark()

        ffn1_pre = {}
        fxA_g = [VA(F_XIN2 + i * 4096, 4096, F32) for i in range(2)]
        fxB_g = [VA(F_XINB + i * 4096, 4096, F32) for i in range(2)]

        def emit_ffn1_preamble():
            mk_m01 = [(sd_m0, 32), (sd_m1, 32)]
            for i in range(2):
                mk = dma("scalar", fxB_g[i][:],
                         xs1[(2 + i) * P:(3 + i) * P, :], sd_sm,
                         wait=[mk_xz_last] + mk_m01)
            for i in range(2):
                mk = dma("scalar", fxA_g[i][:],
                         xs1[i * P:(1 + i) * P, :], sd_sm,
                         wait=[mk_xpe_k[7], mk_r_hist[(7, 7)]] + mk_m01)
            return {"mk_loads": mk}

        for k in range(NKM):
            for pi in range(8):
                phaseB_pair(k, pi)
                if k + 1 < NKM:
                    phaseA_xc_pair(k + 1, pi)
                    phaseA_z_pair(k + 1, pi)
                if k == NKM - 1 and N_SUBLAYERS >= 2:
                    emit_w_unit(2 * pi)
                    emit_w_unit(2 * pi + 1)
            if k + 1 == NKM - 1:
                mk_xz_last = s_pe.mark()
            if k == NKM - 1:
                mk_b7_pe = s_pe.mark()
            wout_t0(k, 7)
            mamba_tail(k)
            if k + 1 < NKM:
                xp_sec(k + 1)
                if k + 2 < NKM:
                    mamba_load_ln(k + 2)
                    mamba_transpose(k + 2, [mk_xpe_k[k + 1], mk_xpe_k[k + 1]])
            if k == NKM - 2 and N_SUBLAYERS >= 2:
                ffn1_pre.update(emit_ffn1_preamble())

        mk_mamba_pe_end = s_pe.mark()
        mk_mamba_ve_end = s_ve.mark()
        mk_mamba_gp_end = s_gp.mark()
        mk_mamba_sta = sd_sta.mark()
        mk_mamba_stb = sd_stb.mark()

        def ffn(src, dst, first, region, store_mark_fn, pre=None):
            wg_bf = VA(F_WG, 49152, BF16)
            wu_bf = VA(F_WU, 49152, BF16)
            wd_sb = [VA(F_WD + i * 2048, 2048, BF16) for i in range(2)]
            fxA = [VA(F_XIN2 + i * 4096, 4096, F32) for i in range(NTF)]
            fxB = [VA(F_XINB + i * 4096, 4096, F32) for i in range(2)]

            fxC = [VA(F_STG, 4096, F32), VA(F_STG + 4096, 4096, F32)]

            def fxt(c, tt):
                if tt >= 2:
                    if c == 0:
                        return fxB[tt - 2] if pre else fxA[tt]
                    if pre:
                        return fxA[tt] if c % 2 == 1 else fxC[tt - 2]
                    return fxC[tt - 2] if c % 2 == 1 else fxA[tt]
                if c % 2 == 1:
                    return fxB[tt]
                return fxA[tt]
            fxn = [VA(F_XN + i * 2048, 2048, BF16) for i in range(2)]
            fxnt = [VA(F_XNT + i * 8192, 8192, BF16) for i in range(2)]
            h_sb = VA(F_H, 24576, BF16)
            gs = [VA(F_GS + i * 2048, 2048, F32) for i in range(2)]
            stg = [VA(F_STG, 6144, I32), VA(F_STG2, 6144, I32)]
            stg_wd = [VA(F_STG, 4096, I32), VA(F_STG2, 4096, I32)]

            mk_ln_f = [None] * NKF
            mk_xnt_f = [None] * NKF
            mk_fstore = [None] * NT
            mk_fst_chunk = [None] * NKF
            tr_banks = {4: None, 5: None}
            gu_mark = {0: None, 1: None}
            dn_ev = {i: None for i in range(8)}

            def f_ln_tiles(c, tts, col0):
                tiles = [fxt(c, tt) for tt in tts]
                if pre and c == 0:
                    mk_x = pre["mk_loads"]
                else:
                    for i, tt in enumerate(tts):
                        gtile = c * NTF + tt
                        w = list(store_mark_fn(gtile))
                        if tt < 2:
                            if c <= 1:
                                w += region["xin"]
                            else:
                                w.append(mk_fst_chunk[c - 2])
                            if first and c == 1:
                                w.append(mk_fst_chunk[0])
                        else:
                            if c == 0:
                                w += region["xin"]
                            elif c == 1:
                                if first:
                                    w.append(prev_cast[0])
                                else:
                                    w += region["xin"]
                            else:
                                w.append(mk_fst_chunk[c - 2])
                        mk_x = dma("scalar", tiles[i][:],
                                   src[gtile * P:(gtile + 1) * P, :],
                                   sd_x[c % 2], wait=w)
                mst = ln_stats_group([t[:] for t in tiles], col0, [mk_x])
                for i, tt in enumerate(tts):
                    col = col0 + i
                    ew = trdone.get(("f", tt % 2))
                    if pre and c == 0 and i == 0 and tts[0] == 0:
                        ew = [mk_b7_pe, mk_mamba_gp_end,
                              mk_mamba_ve_end, ew]
                    mkn = ln_norm(fxn[tt % 2][:], tiles[i][:], col, mst,
                                  ew, first=(i == 0))
                    bank = 4 + tt % 2
                    mk_tr = transpose_tile(fxn[tt % 2][:], bank, mkn,
                                           tr_banks[bank])
                    trdone[("f", tt % 2)] = mk_tr
                    tr_banks[bank] = evac_xnt(fxnt[c % 2], 512, tt, bank,
                                              mk_tr)
                mk_xnt_f[c] = tr_banks[4 + tts[-1] % 2]

            if pre:
                f_ln_tiles(0, [0, 1, 2, 3], 8)
            else:
                f_ln_tiles(0, [0, 1], 8)
                f_ln_tiles(0, [2, 3], 10)

            mk_wg = [None] * ND
            mk_wu = [None] * ND
            prev_cast = [None]
            if first:
                for gi in range(16, 32):
                    emit_w_unit(gi)
                for dj in range(ND):
                    mk_wg[dj] = w_mk[2 * dj + 1]
                    mk_wu[dj] = w_mk[16 + 2 * dj + 1]
                prev_cast[0] = w_mk[31]
                for dstw, cache, mkk in ((wg_bf, wg_c, mk_wg),
                                         (wu_bf, wu_c, mk_wu)):
                    for dj in range(ND):
                        em.op("scalar",
                              lambda e, dj=dj, dstw=dstw, cache=cache: e.dma_start(
                                  cache[dj * P:(dj + 1) * P, :],
                                  dstw[:, dj * H:(dj + 1) * H]),
                              wait=[mkk[dj]], inc=[(sd_cache, 16)],
                              drain=False)
            else:
                for m, (cache, dstw, mkk, gsem) in enumerate(
                        ((wg_c, wg_bf, mk_wg, sd_g1), (wu_c, wu_bf, mk_wu, sd_g2))):
                    mkreg = region["wg"] if m == 0 else region["wu"]
                    for dj in range(ND):
                        mkk[dj] = dma("sync", dstw[:, dj * H:(dj + 1) * H],
                                      cache[dj * P:(dj + 1) * P, :], gsem,
                                      wait=mkreg)
                    mkk[ND - 1] = gsem.mark()

            gu_banks = [(0, 1), (2, 3)]
            cache_mk = {}
            mk_cache_all = []
            prev_wd_cons = {0: None, 1: None}
            for c in range(NKF):
                par = c % 2
                mk_h = {}
                if c == 0:
                    silu_mk = {}
                    for phase in range(2):
                        mkw = mk_wg if phase == 0 else mk_wu
                        for hi in range(NH):
                            bk = gu_banks[hi % 2][phase]
                            if phase == 0:
                                w = [mk_xnt_f[c], mkw[ND - 1],
                                     gu_mark[hi % 2]]
                            else:
                                w = [mkw[ND - 1], silu_mk[hi],
                                     mk_h.get(hi - 2)]
                            wsrc = wg_bf if phase == 0 else wu_bf
                            for dj in range(ND):
                                em.op("tensor",
                                      lambda e, hi=hi, dj=dj, bk=bk, par=par, wsrc=wsrc: e.matmul(
                                          psb[bk],
                                          wsrc[:, dj * H + hi * P: dj * H + (hi + 1) * P],
                                          fxnt[par][:, dj * 512:(dj + 1) * 512],
                                          start=(dj == 0), stop=(dj == ND - 1)),
                                      wait=(w if dj == 0 else ()),
                                      inc=([(s_pe, 1)] if dj == ND - 1 else ()))
                            mk_mm = s_pe.mark()
                            if phase == 0:
                                em.op("scalar",
                                      lambda e, hi=hi, bk=bk: e.activation(
                                          h_sb[:, hi * 512:(hi + 1) * 512],
                                          psb[bk], AF.Silu, scale=1.0 / 127.0),
                                      wait=[mk_mm] + (region["h"] if hi == 0 else []),
                                      inc=[(s_se, 1)], drain=False)
                                gu_mark[hi % 2] = silu_mk[hi] = s_se.mark()
                            else:
                                em.op("vector",
                                      lambda e, hi=hi, bk=bk: e.tensor_mul(
                                          h_sb[:, hi * 512:(hi + 1) * 512],
                                          psb[bk],
                                          h_sb[:, hi * 512:(hi + 1) * 512]),
                                      wait=[mk_mm], inc=[(s_ve, 1)],
                                      drain=False)
                                gu_mark[hi % 2] = mk_h[hi] = s_ve.mark()
                else:
                    for hi in range(NH):
                        bks = gu_banks[hi % 2]
                        w = [mk_xnt_f[c], gu_mark[hi % 2]]
                        for phase in range(2):
                            wsrc = wg_bf if phase == 0 else wu_bf
                            bk = bks[phase]
                            for dj in range(ND):
                                em.op("tensor",
                                      lambda e, hi=hi, dj=dj, bk=bk, par=par, wsrc=wsrc: e.matmul(
                                          psb[bk],
                                          wsrc[:, dj * H + hi * P: dj * H + (hi + 1) * P],
                                          fxnt[par][:, dj * 512:(dj + 1) * 512],
                                          start=(dj == 0), stop=(dj == ND - 1)),
                                      wait=(w if (dj == 0 and phase == 0) else ()),
                                      inc=([(s_pe, 1)] if (dj == ND - 1 and phase == 1) else ()))
                        mk_mm = s_pe.mark()
                        em.op("scalar",
                              lambda e, hi=hi, bk=bks[0]: e.activation(
                                  gs[hi % 2][:], psb[bk], AF.Silu,
                                  scale=1.0 / 127.0),
                              wait=[mk_mm], inc=[(s_se, 1)], drain=False)
                        mk_sil = s_se.mark()
                        em.op("vector",
                              lambda e, hi=hi, bk=bks[1]: e.tensor_mul(
                                  h_sb[:, hi * 512:(hi + 1) * 512],
                                  gs[hi % 2][:], psb[bk]),
                              wait=[mk_sil], inc=[(s_ve, 1)], drain=False)
                        gu_mark[hi % 2] = mk_h[hi] = s_ve.mark()
                mk_h_all = mk_h[NH - 1]
                if c + 1 < NKF and not (first and c == 0):
                    f_ln_tiles(c + 1, [0, 1], 8)
                    f_ln_tiles(c + 1, [2, 3], 10)

                dn_bk = [(0, 1), (2, 3), (6, 7), (4, 5)]
                mk_wd = [None] * NH
                wd_cons = {}
                for hi in range(NH):
                    slot = hi % 2
                    w = []
                    if hi >= 2:
                        w.append(wd_cons[hi - 2])
                        if cache_mk.get(hi - 2):
                            w.append(cache_mk[hi - 2])
                    else:
                        w.append(prev_wd_cons[slot])
                        if c == 0:
                            w += region["wd"]
                    wdsem = sd_wc if slot == 0 else sd_w4
                    wdq = "sync" if slot == 0 else "scalar"
                    if first and c == 0:
                        mkd = dma(wdq, stg_wd[slot][:],
                                  wd_t[hi * P:(hi + 1) * P, :], wdsem,
                                  wait=w + ([prev_cast[0]] if (hi < 2 and prev_cast[0]) else []))
                        em.op("gpsimd",
                              lambda e, slot=slot: e.tensor_copy(
                                  wd_sb[slot][:], stg_wd[slot][:]),
                              wait=[mkd], inc=[(s_gp, 1)], drain=False)
                        mk_wd[hi] = s_gp.mark()
                        cache_mk[hi] = dma("gpsimd",
                                           wd_c[hi * P:(hi + 1) * P, :],
                                           wd_sb[slot][:],
                                           sd_ca0 if slot == 0 else sd_ca1,
                                           wait=[mk_wd[hi]])
                    else:
                        if first:
                            w += mk_cache_all
                        mk_wd[hi] = dma(wdq, wd_sb[slot][:],
                                        wd_c[hi * P:(hi + 1) * P, :], wdsem,
                                        wait=w)
                    for tt in range(NTF):
                        for half in range(2):
                            bk = dn_bk[tt][half]
                            w2 = [mk_h_all, mk_wd[hi]]
                            if hi == 0:
                                w2.append(gu_mark[0] if bk in (0, 1, 2, 3) else None)
                                w2.append(gu_mark[1] if bk in (2, 3) else None)
                                w2.append(tr_banks.get(bk) if bk in (4, 5) else None)
                                w2.append(dn_ev.get(bk) if bk in (6, 7) else None)
                            em.op("tensor",
                                  lambda e, hi=hi, tt=tt, half=half, bk=bk, slot=slot: e.matmul(
                                      psb[bk],
                                      h_sb[:, hi * 512 + tt * P: hi * 512 + (tt + 1) * P],
                                      wd_sb[slot][:, half * 512:(half + 1) * 512],
                                      start=(hi == 0), stop=(hi == NH - 1)),
                                  wait=(w2 if hi == 0 or (tt == 0 and half == 0) else ()),
                                  inc=([(s_pe, 1)] if (tt == NTF - 1 and half == 1) else ()))
                    wd_cons[hi] = s_pe.mark()
                prev_wd_cons[0] = wd_cons[NH - 2]
                prev_wd_cons[1] = wd_cons[NH - 1]
                mk_dn = s_pe.mark()
                for tt in range(NTF):
                    fxc = fxt(c, tt)
                    for half in range(2):
                        bk = dn_bk[tt][half]
                        em.op("vector",
                              lambda e, fxc=fxc, half=half, bk=bk: e.scalar_tensor_tensor(
                                  fxc[:, half * 512:(half + 1) * 512],
                                  psb[bk], 1.0 / (127.0 * 127.0),
                                  fxc[:, half * 512:(half + 1) * 512],
                                  op0=ALU.mult, op1=ALU.add),
                              wait=[mk_dn], inc=[(s_ve, 1)], drain=False)
                    mk_ev = s_ve.mark()
                    for bk in dn_bk[tt]:
                        dn_ev[bk] = mk_ev
                    gtile = c * NTF + tt
                    mk_fstore[gtile] = dma(
                        "scalar", dst[gtile * P:(gtile + 1) * P, :],
                        fxc[:], sd_st[c % 2], wait=[mk_ev])
                    mk_fst_chunk[c] = mk_fstore[gtile]
                if c == 0:
                    mk_cache_all.append(sd_cache.mark())
                    mk_cache_all.append(sd_ca0.mark())
                    mk_cache_all.append(sd_ca1.mark())
                    mk_cache_all.append(sd_wc.mark())
                    mk_cache_all.append(sd_w4.mark())
                if c + 1 < NKF and first and c == 0:
                    f_ln_tiles(c + 1, [0, 1], 8)
                    f_ln_tiles(c + 1, [2, 3], 10)
                gu_mark[0] = dn_ev[0]
                gu_mark[1] = dn_ev[2]
                tr_banks[4] = dn_ev[4]
                tr_banks[5] = dn_ev[4]
            return mk_fstore

        if N_SUBLAYERS >= 2:
            region1 = {
                "wg": [mk_xz_last],
                "wu": [mk_mamba_pe_end, mk_mamba_ve_end, mk_mamba_sta,
                       mk_mamba_stb],
                "wd": [mk_mamba_pe_end, mk_mamba_ve_end, mk_mamba_sta,
                       mk_mamba_stb],
                "xin": [mk_mamba_gp_end, mk_mamba_ve_end],
                "xnt": [mk_mamba_gp_end, mk_mamba_ve_end],
                "h": [mk_mamba_pe_end, mk_mamba_gp_end],
            }
            dst2 = xs2 if N_SUBLAYERS > 2 else out_ext
            mk_f1_store = ffn(xs1, dst2, True, region1,
                              lambda gt: [mk_mamba_sta, mk_mamba_stb],
                              pre=ffn1_pre)
            mk_f1_pe = s_pe.mark()
            mk_f1_ve = s_ve.mark()
            mk_f1_sta = sd_sta.mark()
            mk_f1_stb = sd_stb.mark()
            mk_f1_cache = sd_cache.mark()
            mk_f1_ca0 = sd_ca0.mark()
            mk_f1_ca1 = sd_ca1.mark()

        if N_SUBLAYERS >= 3:
            a_xnt = VA(T_XNT, 32768, BF16)
            a_kt = VA(T_KT, 32768, BF16)
            a_v = VA(T_V, 65536 - 32768, BF16)
            a_wt = [VA(T_WT + i * 16384, 16384, BF16) for i in range(2)]
            a_probs = [VA(T_PROBS + i * 4096, 4096, BF16) for i in range(2)]
            a_probt = VA(T_PROBT, 4096, BF16)
            a_x = [VA(T_X + i * 4096, 4096, F32) for i in range(2)]
            a_xn = [VA(T_XN + i * 2048, 2048, BF16) for i in range(2)]
            a_xpno = [VA(T_XPNO + i * 4096, 4096, F32) for i in range(2)]
            a_xpni = [VA(T_XPNI + i * 4096, 4096, F32) for i in range(2)]
            a_xout = [VA(T_XOUT + i * 4096, 4096, F32) for i in range(2)]
            wt_stage = [V(A0 + T_PROBS, 4096, F32),
                        V(A0 + T_PROBS + 4096, 4096, F32)]

            mk_xpn_st = [None] * NT
            prev_x = {0: None, 1: None}
            prev_xpno = {0: None, 1: None}
            a_trb = {4: mk_f1_ve, 5: mk_f1_ve}
            for tpair in range(NT // 2):
                mk_x = []
                for slot in range(2):
                    t = 2 * tpair + slot
                    w = [mk_f1_sta, mk_f1_stb, prev_x[slot]]
                    if t < 2:
                        w.append(mk_f1_ve)
                    mk_x.append(dma("scalar", a_x[slot][:],
                                    xs2[t * P:(t + 1) * P, :],
                                    sd_x[slot], wait=w))
                mst = ln_stats_group([a_x[0][:], a_x[1][:]], 12, mk_x,
                                     extra_1pr=True)
                for slot in range(2):
                    t = 2 * tpair + slot
                    col = 12 + slot
                    mkn = ln_norm(a_xn[slot][:], a_x[slot][:], col, mst,
                                  trdone.get(("a", slot)),
                                  first=(slot == 0))
                    em.op("vector",
                          lambda e, slot=slot, j=col: e.tensor_scalar(
                              a_xpno[slot][:], a_x[slot][:],
                              g_1pr[:, j:j + 1], g_nmr[:, j:j + 1],
                              op0=ALU.mult, op1=ALU.add),
                          wait=[prev_xpno[slot]], inc=[(s_ve, 1)],
                          drain=False)
                    prev_x[slot] = s_ve.mark()
                    mk_xpn_st[t] = dma("gpsimd", xpn_d[t * P:(t + 1) * P, :],
                                       a_xpno[slot][:],
                                       sd_stp if slot == 0 else sd_stq,
                                       wait=[prev_x[slot]])
                    prev_xpno[slot] = mk_xpn_st[t]
                    bank = 4 + slot
                    mk_tr = transpose_tile(a_xn[slot][:], bank, mkn,
                                           a_trb[bank])
                    trdone[("a", slot)] = mk_tr
                    a_trb[bank] = evac_xnt(a_xnt, 2048, t, bank, mk_tr)
            mk_xnt_a = a_trb[4 + (NT - 1) % 2]
            mk_xnt_a2 = s_ve.mark()

            mk_wt = [[None] * ND, [None] * ND]
            prev_stg = {0: None, 1: None}
            wi = 0
            for halfw in range(2):
                for dj in range(ND):
                    slot = wi % 2
                    wi += 1
                    w = [prev_stg[slot]]
                    if halfw == 0 and dj < 2:
                        w += [mk_f1_ve, mk_f1_sta, mk_f1_stb, mk_f1_pe]
                    mkd = dma("sync", wt_stage[slot][:, 0:1024],
                              wt_at[dj * P:(dj + 1) * P,
                                    halfw * 1024:(halfw + 1) * 1024],
                              sd_wa if slot == 0 else sd_wb, wait=w)
                    em.op("vector",
                          lambda e, halfw=halfw, dj=dj, slot=slot: e.tensor_copy(
                              a_wt[halfw][:, dj * 1024:(dj + 1) * 1024],
                              wt_stage[slot][:, 0:1024]),
                          wait=[mkd] + ([mk_f1_pe] if (halfw == 0 and dj < 2) else []),
                          inc=[(s_ve, 1)], drain=False)
                    prev_stg[slot] = mk_wt[halfw][dj] = s_ve.mark()

            kv_ev = {b: None for b in range(4)}
            bi = 0
            for jc in range(ND):
                for n in range(4):
                    bk = bi % 4
                    bi += 1
                    w = [mk_xnt_a2, mk_wt[0][ND - 1], kv_ev[bk]]
                    for dj in range(ND):
                        em.op("tensor",
                              lambda e, jc=jc, n=n, dj=dj, bk=bk: e.matmul(
                                  psb[bk],
                                  a_wt[0][:, dj * 1024 + jc * P: dj * 1024 + (jc + 1) * P],
                                  a_xnt[:, dj * 2048 + n * 512: dj * 2048 + (n + 1) * 512],
                                  start=(dj == 0), stop=(dj == ND - 1)),
                              wait=(w if dj == 0 else ()),
                              inc=([(s_pe, 1)] if dj == ND - 1 else ()))
                    mk_mm = s_pe.mark()
                    em.op("vector",
                          lambda e, jc=jc, n=n, bk=bk: e.tensor_copy(
                              a_kt[:, jc * 2048 + n * 512: jc * 2048 + (n + 1) * 512],
                              psb[bk]),
                          wait=[mk_mm], inc=[(s_ve, 1)], drain=False)
                    kv_ev[bk] = s_ve.mark()
            for tt in range(NT):
                for half in range(2):
                    bk = bi % 4
                    bi += 1
                    w = [mk_wt[1][ND - 1], kv_ev[bk]]
                    for dj in range(ND):
                        em.op("tensor",
                              lambda e, tt=tt, half=half, dj=dj, bk=bk: e.matmul(
                                  psb[bk],
                                  a_xnt[:, dj * 2048 + tt * P: dj * 2048 + (tt + 1) * P],
                                  a_wt[1][:, dj * 1024 + half * 512: dj * 1024 + (half + 1) * 512],
                                  start=(dj == 0), stop=(dj == ND - 1)),
                              wait=(w if dj == 0 else ()),
                              inc=([(s_pe, 1)] if dj == ND - 1 else ()))
                    mk_mm = s_pe.mark()
                    em.op("vector",
                          lambda e, tt=tt, half=half, bk=bk: e.tensor_copy(
                              a_v[:, tt * 1024 + half * 512: tt * 1024 + (half + 1) * 512],
                              psb[bk]),
                          wait=[mk_mm], inc=[(s_ve, 1)], drain=False)
                    kv_ev[bk] = s_ve.mark()
            mk_kv_done = s_ve.mark()
            mk_xpn_all = sd_stp.mark()
            mk_xpn_all2 = sd_stq.mark()

            dst3 = xs3 if N_SUBLAYERS > 3 else out_ext
            mk_probt_ev = None
            mk_av_ev = [None]
            mk_xout_free = [None, None]
            prev_xpni = [None, None]
            mk_exp_hist = {-1: mk_kv_done, -2: mk_kv_done}
            qstate = {}

            def attn_av(qj):
                slotj = qj % 2
                mk_pt, mk_rcj, mk_xpnij = qstate[qj]
                for half in range(2):
                    bk = 6 + half
                    for ci in range(qj + 1):
                        em.op("tensor",
                              lambda e, ci=ci, half=half, bk=bk, qj=qj: e.matmul(
                                  psb[bk],
                                  a_probt[:, ci * P:(ci + 1) * P],
                                  a_v[:, ci * 1024 + half * 512: ci * 1024 + (half + 1) * 512],
                                  start=(ci == 0), stop=(ci == qj)),
                              wait=([mk_pt, mk_av_ev[0]] if (ci == 0 and half == 0) else ()),
                              inc=([(s_pe, 1)] if (ci == qj and half == 1) else ()))
                mk_av = s_pe.mark()
                for half in range(2):
                    em.op("vector",
                          lambda e, half=half, slotj=slotj, bk=6 + half: e.scalar_tensor_tensor(
                              a_xout[slotj][:, half * 512:(half + 1) * 512],
                              psb[bk], g_recip[:, qj:qj + 1],
                              a_xpni[slotj][:, half * 512:(half + 1) * 512],
                              op0=ALU.mult, op1=ALU.add),
                          wait=[mk_av, mk_rcj, mk_xpnij, mk_xout_free[slotj]],
                          inc=[(s_ve, 1)], drain=(half == 0))
                mk_av_ev[0] = s_ve.mark()
                prev_xpni[slotj] = mk_av_ev[0]
                mk_xout_free[slotj] = dma(
                    "scalar", dst3[qj * P:(qj + 1) * P, :], a_xout[slotj][:],
                    sd_st[slotj], wait=[mk_av_ev[0]])

            for qi in range(NT):
                width = (qi + 1) * P
                sbase = (qi % 2) * 1024 if qi <= 7 else 0
                nsl = (width + 511) // 512
                slot = qi % 2
                mk_xpni = dma("scalar", a_xpni[slot][:],
                              xpn_d[qi * P:(qi + 1) * P, :], sd_x[slot],
                              wait=[mk_xpn_all, mk_xpn_all2, prev_xpni[slot]])
                if qi <= 7:
                    w0 = [mk_xnt_a2, mk_exp_hist[qi - 2]]
                elif qi == 8:
                    w0 = [mk_xnt_a2, mk_exp_hist[6], mk_exp_hist[7]]
                else:
                    w0 = [mk_xnt_a2, mk_exp_hist[qi - 1]]
                for n in range(nsl):
                    ncols = min(width, (n + 1) * 512) - n * 512
                    for dj in range(ND):
                        em.op("tensor",
                              lambda e, qi=qi, n=n, dj=dj, ncols=ncols, sbase=sbase: e.matmul(
                                  ps_s[:, sbase + n * 512: sbase + n * 512 + ncols],
                                  a_xnt[:, dj * 2048 + qi * P: dj * 2048 + (qi + 1) * P],
                                  a_kt[:, dj * 2048 + n * 512: dj * 2048 + n * 512 + ncols],
                                  start=(dj == 0), stop=(dj == ND - 1)),
                              wait=(w0 if (dj == 0 and n == 0) else ()),
                              inc=([(s_pe, 1)] if (dj == ND - 1 and n == nsl - 1) else ()))
                mk_sc = s_pe.mark()
                em.op("vector",
                      lambda e, qi=qi, sbase=sbase: e.tensor_add(
                          ps_s[:, sbase + qi * P: sbase + (qi + 1) * P],
                          ps_s[:, sbase + qi * P: sbase + (qi + 1) * P],
                          g_mask[:]),
                      wait=[mk_sc])
                em.op("vector",
                      lambda e, width=width, sbase=sbase: e.reduce_max(
                          g_nmax[:, 0:1], ps_s[:, sbase: sbase + width],
                          axis=AX.X, negate=True),
                      inc=[(s_ve, 1)])
                mk_nm = s_ve.mark()
                em.op("scalar",
                      lambda e, qi=qi, width=width, slot=slot, sbase=sbase: e.activation(
                          a_probs[slot][:, 0:width],
                          ps_s[:, sbase: sbase + width],
                          AF.Exp, bias=g_nmax[:, 0:1], accum_out=g_rsum[:, 0:1]),
                      wait=[mk_nm, mk_probt_ev], inc=[(s_se, 1)])
                mk_exp = s_se.mark()
                mk_exp_hist[qi] = mk_exp
                em.op("vector",
                      lambda e, qi=qi: e.reciprocal(
                          g_recip[:, qi:qi + 1], g_rsum[:, 0:1]),
                      wait=[mk_exp], inc=[(s_ve, 1)], drain=False)
                mk_rc = s_ve.mark()
                if qi >= 1:
                    attn_av(qi - 1)
                ngr = (qi + 1 + 7) // 8
                mk_ptr_ev = mk_probt_ev
                for g in range(ngr):
                    cis = list(range(g * 8, min(qi + 1, g * 8 + 8)))
                    pbf = psb[4].bitcast(BF16)
                    for i, ci in enumerate(cis):
                        em.op("tensor",
                              lambda e, ci=ci, i=i, slot=slot, pbf=pbf: e.transpose(
                                  pbf[:, i * P:(i + 1) * P],
                                  a_probs[slot][:, ci * P:(ci + 1) * P],
                                  g_ident),
                              wait=([mk_exp, mk_ptr_ev] if i == 0 else ()),
                              inc=([(s_pe, 1)] if ci == cis[-1] else ()))
                    mk_tr = s_pe.mark()
                    dstp = ap3(a_probt, g * 8 * P, P, len(cis), P)
                    em.op("vector",
                          lambda e, dstp=dstp, pbf=pbf, ncis=len(cis): e.tensor_copy(
                              dstp, pbf[:, 0:ncis * P]),
                          wait=[mk_tr], inc=[(s_ve, 1)], drain=False)
                    mk_ptr_ev = s_ve.mark()
                mk_probt_ev = mk_ptr_ev
                qstate[qi] = (mk_probt_ev, mk_rc, mk_xpni)
            attn_av(NT - 1)
            mk_attn_ve = s_ve.mark()
            mk_attn_sta = sd_sta.mark()
            mk_attn_stb = sd_stb.mark()

        if N_SUBLAYERS >= 4:
            region2 = {
                "wg": [mk_attn_ve, mk_f1_cache, mk_f1_ca0, mk_f1_ca1],
                "wu": [mk_attn_ve, mk_f1_cache, mk_f1_ca0, mk_f1_ca1],
                "wd": [mk_attn_ve, mk_f1_cache, mk_f1_ca0, mk_f1_ca1],
                "xin": [mk_attn_ve, mk_attn_sta, mk_attn_stb],
                "xnt": [mk_attn_ve],
                "h": [mk_attn_ve, mk_attn_sta, mk_attn_stb],
            }
            ffn(xs3, out_ext, False, region2,
                lambda gt: [mk_attn_sta, mk_attn_stb])

        all_sems = [sd_sm, sd_wa, sd_wb, sd_wc, sd_w4, sd_cache, sd_ca0,
                    sd_ca1, sd_g1, sd_g2, sd_xa, sd_xb, sd_sta, sd_stb,
                    sd_stp, sd_stq, sd_stg, sd_m0, sd_m1]
        em.op("sync", lambda e: None,
              wait=[(sg, sg.n) for sg in all_sems])

        block = ctx.enter_context(nc.Block())

        @block.sync
        def _(e):
            for f in em.q["sync"]:
                f(e)

        @block.tensor
        def _(e):
            for f in em.q["tensor"]:
                f(e)

        @block.vector
        def _(e):
            for f in em.q["vector"]:
                f(e)

        @block.scalar
        def _(e):
            for f in em.q["scalar"]:
                f(e)

        @block.gpsimd
        def _(e):
            for f in em.q["gpsimd"]:
                f(e)

    return nc


_NC_CACHE = None


def kernel(L0_attn_a_x, L0_attn_a_W_in, L0_attn_a_conv_w, L0_attn_a_W_x,
           L0_attn_a_W_dt, L0_attn_a_b_dt, L0_attn_a_W_Bp, L0_attn_a_D_param,
           L0_attn_a_W_out, L0_attn_b_W, L0_ffn_w_gate, L0_ffn_w_up,
           L0_ffn_w_down):
    global _NC_CACHE
    x = np.ascontiguousarray(np.asarray(L0_attn_a_x, dtype=np.float32))
    w_in = np.ascontiguousarray(np.asarray(L0_attn_a_W_in, np.float32))
    conv_w = np.asarray(L0_attn_a_conv_w, np.float32).reshape(DI, 4)
    conv_wt = np.ascontiguousarray(
        conv_w.reshape(NCI, P, 4).transpose(1, 0, 2).reshape(P, NCI * 4))
    w_x = np.asarray(L0_attn_a_W_x, np.float32)
    w_x_pad = np.ascontiguousarray(
        w_x.reshape(NCI, P, 32).transpose(1, 0, 2).reshape(P, NCI * 32))
    w_dt = np.ascontiguousarray(np.concatenate(
        [np.asarray(L0_attn_a_W_dt, np.float32),
         np.asarray(L0_attn_a_b_dt, np.float32).reshape(1, DI)], axis=0))
    b_dt = np.ascontiguousarray(
        np.asarray(L0_attn_a_b_dt, np.float32).reshape(NCI, P).T)
    w_bp = np.ascontiguousarray(np.asarray(L0_attn_a_W_Bp, np.float32))
    d_par = np.ascontiguousarray(
        np.asarray(L0_attn_a_D_param, np.float32).reshape(NCI, P).T)
    w_out = np.ascontiguousarray(np.asarray(L0_attn_a_W_out, np.float32))
    wt_at = np.ascontiguousarray(np.asarray(L0_attn_b_W, np.float32).T)
    wg_t = np.ascontiguousarray(np.asarray(L0_ffn_w_gate, np.int32).T)
    wu_t = np.ascontiguousarray(np.asarray(L0_ffn_w_up, np.int32).T)
    wd_t = np.ascontiguousarray(np.asarray(L0_ffn_w_down, np.int32).T)
    ident = np.eye(P).astype(ml_dtypes.bfloat16)
    mask = np.where(np.tril(np.ones((P, P), bool)), 0.0,
                    -1e30).astype(np.float32)

    if _NC_CACHE is None:
        _NC_CACHE = build_nc()
    nc = _NC_CACHE

    common = {
        "w_in": w_in, "conv_wt": conv_wt, "w_x": w_x_pad, "w_dt": w_dt,
        "b_dt": b_dt, "w_bp": w_bp, "d_par": d_par, "w_out": w_out,
        "wt_at": wt_at, "wg_t": wg_t, "wu_t": wu_t, "wd_t": wd_t,
        "ident": ident, "mask": mask,
    }
    in_maps = [dict(common, x=np.ascontiguousarray(x[c])) for c in range(8)]
    res = run_bass_kernel_spmd(nc, in_maps, core_ids=list(range(8)))
    return np.stack([r["out"] for r in res.results]).astype(np.float32)



# version 31
# speedup vs baseline: 1.0359x; 1.0064x over previous
import numpy as np
import ml_dtypes
from contextlib import ExitStack

import concourse.bass as bass
import concourse.mybir as mybir
from concourse.bass_utils import run_bass_kernel_spmd

F32 = mybir.dt.float32
BF16 = mybir.dt.bfloat16
I32 = mybir.dt.int32
I8 = mybir.dt.int8
AF = mybir.ActivationFunctionType
ALU = mybir.AluOpType
AX = mybir.AxisListType

P = 128
D = 1024
ND = 8
DI = 2048
NCI = 16
L = 2048
NT = 16
H = 3072
NH = 24
TCM = 256
NKM = L // TCM
NTM = TCM // P
TCF = 512
NKF = L // TCF
NTF = TCF // P
EPS = 1e-5

N_SUBLAYERS = 4
CUT = None

ENGINES = ("sync", "tensor", "vector", "scalar", "gpsimd")

AB = 203584
G_IDENT = 0
G_MASK = 256
G_CONVW = 768
G_BDT = 1024
G_DPAR = 1088
G_NSUM = 1152
G_SSQ = 1216
G_VAR = 1280
G_RSTD = 1344
G_NMR = 1408
G_1PR = 1472
G_CARRY = 1536
G_MSQ = 1600
G_NMAX = 1664
G_RSUM = 1728
G_RECIP = 1792
G_SD = 1856
G_DUMP = 4096
A0 = 8192

M_WIN = 0
M_WOUT = 65536
M_WX = 98304
M_XIN = 100352
M_XN = 116736
M_XNT = 120832
M_XC = 129024
M_U = 137472
M_SZ = 145664
M_ACC = 153856
M_R2 = 155904
M_R3 = 160000
M_SP = 164096
M_A2 = 168192
M_UB = 172288
M_YS = 176384
M_Y = 180480
M_STAGE = 188672
M_END = 192768

F_WG = 0
F_STG = 49152
F_STG2 = 55296
F_WU = 65536
F_WD = 114688
F_XINB = 118784
F_XIN2 = 129024
F_XN = 145408
F_XNT = 149504
F_H = 165888
F_GS = 190464

T_XNT = 0
T_KT = 32768
T_V = 65536
T_WT = 98304
T_PROBS = 131072
T_PROBT = 139264
T_X = 143360
T_XN = 151552
T_XPNO = 155648
T_XPNI = 163840
T_XOUT = 172032
T_END = 180224


class Sem:
    __slots__ = ("h", "n")

    def __init__(self, h):
        self.h = h
        self.n = 0

    def mark(self):
        return (self, self.n)


class Em:
    def __init__(self):
        self.q = {e: [] for e in ENGINES}
        self.n_ops = 0

    def op(self, eng, fn, wait=(), inc=(), drain=True):
        self.n_ops += 1
        if CUT is not None and self.n_ops > CUT:
            return
        waits = []
        for m in wait:
            if m is None:
                continue
            s, v = m
            assert v <= s.n, f"forward wait on {eng}: {v} > {s.n}"
            if v > 0:
                waits.append((s.h, v))
        incs = []
        for s, d in inc:
            s.n += d
            incs.append((s.h, d))

        need_drain = drain and eng in ("vector", "scalar", "gpsimd")

        def run(e, fn=fn, waits=waits, incs=incs, nd=need_drain):
            if nd:
                e.drain()
            for h, v in waits:
                e.wait_ge(h, v)
            r = fn(e)
            for h, d in incs:
                r = r.then_inc(h, d)

        self.q[eng].append(run)


def ap3(view, off, blk_step, blk_n, inner_n):
    return bass.AP(view.tensor, view.offset + off,
                   [list(view.ap[0]), [blk_step, blk_n], [1, inner_n]])


def build_nc():
    nc = bass.Bass("TRN2", target_bir_lowering=False)

    x_ext = nc.dram_tensor("x", [L, D], F32, kind="ExternalInput")
    w_in = nc.dram_tensor("w_in", [D, 4096], F32, kind="ExternalInput")
    conv_wt = nc.dram_tensor("conv_wt", [P, 64], F32, kind="ExternalInput")
    w_x = nc.dram_tensor("w_x", [P, NCI * 32], F32, kind="ExternalInput")
    w_dt = nc.dram_tensor("w_dt", [17, DI], F32, kind="ExternalInput")
    b_dt = nc.dram_tensor("b_dt", [P, 16], F32, kind="ExternalInput")
    w_bp = nc.dram_tensor("w_bp", [16, DI], F32, kind="ExternalInput")
    d_par = nc.dram_tensor("d_par", [P, 16], F32, kind="ExternalInput")
    w_out = nc.dram_tensor("w_out", [DI, D], F32, kind="ExternalInput")
    wt_at = nc.dram_tensor("wt_at", [D, 2048], F32, kind="ExternalInput")
    wg_t = nc.dram_tensor("wg_t", [D, H], I32, kind="ExternalInput")
    wu_t = nc.dram_tensor("wu_t", [D, H], I32, kind="ExternalInput")
    wd_t = nc.dram_tensor("wd_t", [H, D], I32, kind="ExternalInput")
    ident_e = nc.dram_tensor("ident", [P, P], BF16, kind="ExternalInput")
    mask_e = nc.dram_tensor("mask", [P, P], F32, kind="ExternalInput")
    out_ext = nc.dram_tensor("out", [L, D], F32, kind="ExternalOutput")

    xs1 = nc.dram_tensor("xs1", [L, D], F32)
    xs2 = nc.dram_tensor("xs2", [L, D], F32)
    xs3 = nc.dram_tensor("xs3", [L, D], F32)
    xpn_d = nc.dram_tensor("xpn_d", [L, D], F32)
    wg_c = nc.dram_tensor("wg_c", [D, H], BF16)
    wu_c = nc.dram_tensor("wu_c", [D, H], BF16)
    wd_c = nc.dram_tensor("wd_c", [H, D], BF16)

    em = Em()

    with ExitStack() as ctx:
        arena = ctx.enter_context(nc.sbuf_tensor("arena", [P, AB], I8))
        wdt_bf = ctx.enter_context(nc.sbuf_tensor("wdt_bf", [17, DI], BF16))
        wbp_bf = ctx.enter_context(nc.sbuf_tensor("wbp_bf", [16, DI], BF16))
        xpd_bf = ctx.enter_context(nc.sbuf_tensor("xpd_bf", [32, TCM], BF16))
        xpb_bf = ctx.enter_context(nc.sbuf_tensor("xpb_bf", [16, TCM], BF16))

        def V(off, nbytes, dt):
            return arena[:, off:off + nbytes].bitcast(dt)

        def VA(off, nbytes, dt):
            return V(A0 + off, nbytes, dt)

        g_ident = V(G_IDENT, 256, BF16)
        g_mask = V(G_MASK, 512, F32)
        g_convw = V(G_CONVW, 256, F32)
        g_bdt = V(G_BDT, 64, F32)
        g_dpar = V(G_DPAR, 64, F32)
        g_nsum = V(G_NSUM, 64, F32)
        g_ssq = V(G_SSQ, 64, F32)
        g_var = V(G_VAR, 64, F32)
        g_rstd = V(G_RSTD, 64, F32)
        g_nmr = V(G_NMR, 64, F32)
        g_1pr = V(G_1PR, 64, F32)
        g_carry = V(G_CARRY, 64, F32)
        g_msq = V(G_MSQ, 64, F32)
        g_nmax = V(G_NMAX, 64, F32)
        g_rsum = V(G_RSUM, 64, F32)
        g_recip = V(G_RECIP, 64, F32)
        g_sd = V(G_SD, 64, F32)
        g_dump = V(G_DUMP, 2048, BF16)

        ps_s = ctx.enter_context(nc.psum_tensor("ps_s", [P, 2048], F32))
        ps4 = ctx.enter_context(nc.psum_tensor("ps4", [P, 512], F32))
        ps5 = ctx.enter_context(nc.psum_tensor("ps5", [P, 512], F32))
        ps6 = ctx.enter_context(nc.psum_tensor("ps6", [P, 512], F32))
        ps7 = ctx.enter_context(nc.psum_tensor("ps7", [P, 512], F32))
        psb = [ps_s[:, i * 512:(i + 1) * 512] for i in range(4)] + \
              [ps4[:], ps5[:], ps6[:], ps7[:]]

        sd_sm = Sem(ctx.enter_context(nc.semaphore("sd_sm")))
        sd_wa = Sem(ctx.enter_context(nc.semaphore("sd_wa")))
        sd_wb = Sem(ctx.enter_context(nc.semaphore("sd_wb")))
        sd_wc = Sem(ctx.enter_context(nc.semaphore("sd_wc")))
        sd_w4 = Sem(ctx.enter_context(nc.semaphore("sd_w4")))
        sd_cache = Sem(ctx.enter_context(nc.semaphore("sd_cache")))
        sd_ca0 = Sem(ctx.enter_context(nc.semaphore("sd_ca0")))
        sd_ca1 = Sem(ctx.enter_context(nc.semaphore("sd_ca1")))
        sd_g1 = Sem(ctx.enter_context(nc.semaphore("sd_g1")))
        sd_g2 = Sem(ctx.enter_context(nc.semaphore("sd_g2")))
        sd_xa = Sem(ctx.enter_context(nc.semaphore("sd_xa")))
        sd_xb = Sem(ctx.enter_context(nc.semaphore("sd_xb")))
        sd_sta = Sem(ctx.enter_context(nc.semaphore("sd_sta")))
        sd_stb = Sem(ctx.enter_context(nc.semaphore("sd_stb")))
        sd_stp = Sem(ctx.enter_context(nc.semaphore("sd_stp")))
        sd_stq = Sem(ctx.enter_context(nc.semaphore("sd_stq")))
        sd_stg = Sem(ctx.enter_context(nc.semaphore("sd_stg")))
        sd_m0 = Sem(ctx.enter_context(nc.semaphore("sd_m0")))
        sd_m1 = Sem(ctx.enter_context(nc.semaphore("sd_m1")))
        sd_x = [sd_xa, sd_xb]
        sd_st = [sd_sta, sd_stb]
        s_pe = Sem(ctx.enter_context(nc.semaphore("s_pe")))
        s_ve = Sem(ctx.enter_context(nc.semaphore("s_ve")))
        s_se = Sem(ctx.enter_context(nc.semaphore("s_se")))
        s_gp = Sem(ctx.enter_context(nc.semaphore("s_gp")))

        col_free = {}
        trdone = {}
        mk_eps = None

        def dma(queue, dst, src, sem, wait=()):
            em.op(queue, lambda e, d=dst, s=src: e.dma_start(d, s),
                  wait=wait, inc=[(sem, 16)])
            return sem.mark()

        def ln_stats_group(tiles, col0, wait_x, extra_1pr=False):
            n = len(tiles)
            J = slice(col0, col0 + n)
            cf = [col_free.get(col0 + i) for i in range(n)]
            for i, t in enumerate(tiles):
                em.op("scalar",
                      lambda e, s=t, j=col0 + i: e.activation(
                          g_dump[:, 0:1024], s, AF.Square,
                          accum_out=g_ssq[:, j:j + 1]),
                      wait=(list(wait_x) + cf) if i == 0 else (),
                      drain=(i > 0),
                      inc=([(s_se, 1)] if i == n - 1 else ()))
            mk_ssq = s_se.mark()
            for i, t in enumerate(tiles):
                em.op("vector",
                      lambda e, s=t, j=col0 + i: e.reduce_sum(
                          g_nsum[:, j:j + 1], s, axis=AX.X, negate=True),
                      wait=(list(wait_x) + cf) if i == 0 else (), drain=False,
                      inc=([(s_ve, 1)] if i == n - 1 else ()))
            em.op("vector", lambda e, J=J: e.tensor_scalar_mul(
                g_nsum[:, J], g_nsum[:, J], 1.0 / D))
            em.op("vector", lambda e, J=J: e.tensor_mul(
                g_msq[:, J], g_nsum[:, J], g_nsum[:, J]))
            em.op("vector", lambda e, J=J: e.scalar_tensor_tensor(
                g_var[:, J], g_ssq[:, J], 1.0 / D,
                g_msq[:, J], op0=ALU.mult, op1=ALU.subtract),
                wait=[mk_ssq], inc=[(s_ve, 1)])
            mk_var = s_ve.mark()
            em.op("scalar", lambda e, J=J: e.activation(
                g_sd[:, J], g_var[:, J], AF.Sqrt,
                bias=g_sd[:, 15:16]),
                wait=[mk_var, mk_eps], inc=[(s_se, 1)])
            mk_sd = s_se.mark()
            em.op("vector", lambda e, J=J: e.reciprocal(
                g_rstd[:, J], g_sd[:, J]), wait=[mk_sd], drain=False)
            em.op("vector", lambda e, J=J: e.tensor_mul(
                g_nmr[:, J], g_nsum[:, J], g_rstd[:, J]),
                inc=[(s_ve, 1)])
            if extra_1pr:
                em.op("vector", lambda e, J=J: e.tensor_scalar_add(
                    g_1pr[:, J], g_rstd[:, J], 1.0),
                    inc=[(s_ve, 1)], drain=False)
            return s_ve.mark()

        def ln_norm(dst_bf, src_tile, col, mk_stats, extra_wait=None,
                    first=True):
            ew = extra_wait if isinstance(extra_wait, list) else [extra_wait]
            em.op("vector", lambda e, d=dst_bf, s=src_tile, j=col:
                  e.tensor_scalar(d, s, g_rstd[:, j:j + 1],
                                  g_nmr[:, j:j + 1],
                                  op0=ALU.mult, op1=ALU.add),
                  wait=[mk_stats] + ew, inc=[(s_ve, 1)], drain=first)
            col_free[col] = s_ve.mark()
            return col_free[col]

        def transpose_tile(xn_bf_tile, bank, mk_xn, mk_bank_free):
            pbf = psb[bank].bitcast(BF16)
            for dj in range(ND):
                em.op("tensor",
                      lambda e, s=xn_bf_tile, dj=dj, pbf=pbf: e.transpose(
                          pbf[:, dj * P:(dj + 1) * P],
                          s[:, dj * P:(dj + 1) * P], g_ident),
                      wait=([mk_xn, mk_bank_free] if dj == 0 else ()),
                      inc=([(s_pe, 1)] if dj == ND - 1 else ()))
            return s_pe.mark()

        def evac_xnt(xnt_view, blk_step, tt, bank, mk_tr):
            pbf = psb[bank].bitcast(BF16)
            dst = ap3(xnt_view, tt * P, blk_step, ND, P)
            em.op("vector", lambda e, d=dst, s=pbf: e.tensor_copy(d, s),
                  wait=[mk_tr], inc=[(s_ve, 1)], drain=False)
            return s_ve.mark()

        xc = VA(M_XC, 8448, BF16)
        em.op("vector", lambda e: e.memset(ap3(xc, 0, 264, NCI, 3), 0.0),
              inc=[(s_ve, 1)], drain=False)
        em.op("vector", lambda e: e.memset(g_carry[:], 0.0), inc=[(s_ve, 1)],
              drain=False)
        em.op("vector", lambda e: e.memset(g_sd[:, 15:16], EPS),
              inc=[(s_ve, 1)], drain=False)
        mk_init = s_ve.mark()
        mk_eps = mk_init

        xin = [VA(M_XIN + i * 4096, 4096, F32) for i in range(4)]
        xnbf = [VA(M_XN + i * 2048, 2048, BF16) for i in range(2)]
        xnt = [VA(M_XNT + i * 4096, 4096, BF16) for i in range(2)]
        u_bf = VA(M_U, 8192, BF16)
        sz_bf = VA(M_SZ, 8192, BF16)
        acc_p = VA(M_ACC, 2048, F32)
        r2_p = [VA(M_R2 + i * 2048, 2048, F32) for i in range(2)]
        r3_p = [VA(M_R3 + i * 2048, 2048, F32) for i in range(2)]
        sp = [VA(M_SP + i * 2048, 2048, F32) for i in range(2)]
        a_f = [VA(M_A2 + i * 2048, 2048, F32) for i in range(2)]
        ub_f = [VA(M_UB + i * 2048, 2048, F32) for i in range(2)]
        ys_f = [VA(M_YS + i * 2048, 2048, F32) for i in range(2)]
        y_bf = VA(M_Y, 8192, BF16)

        mk_ln = [None] * (NKM + 1)
        mk_xnt = [None] * (NKM + 1)
        mk_cevax = [None, None]
        mk_store = [None] * NKM
        mk_y_done = [None]
        mk_gp_hist = {}

        def mamba_load_ln(k):
            par = k % 2
            for tt in range(NTM):
                if k >= 2:
                    w = [mk_cevax[par]]
                elif k == 1:
                    w = [mk_wx_stage]
                else:
                    w = []
                mk_x = dma("scalar", xin[par * 2 + tt][:],
                           x_ext[(k * NTM + tt) * P:(k * NTM + tt + 1) * P, :],
                           sd_x[par], wait=w)
            mst = ln_stats_group([xin[par * 2 + tt][:] for tt in range(NTM)],
                                 par * 2, [mk_x])
            mks = []
            for tt in range(NTM):
                col = par * 2 + tt
                mks.append(ln_norm(xnbf[tt][:], xin[par * 2 + tt][:], col,
                                   mst, trdone.get(("m", tt)),
                                   first=(tt == 0)))
            mk_ln[k] = mks

        def mamba_transpose(k, mk_free):
            par = k % 2
            banks = [4, 5]
            for tt in range(NTM):
                mk_tr = transpose_tile(xnbf[tt][:], banks[tt],
                                       mk_ln[k][tt], mk_free[tt])
                trdone[("m", tt)] = mk_tr
                mk_xnt[k] = evac_xnt(xnt[par], 256, tt, banks[tt], mk_tr)

        stage = VA(M_STAGE, 4096, F32)
        wx_bf = VA(M_WX, 1024, BF16)
        wx_stage = VA(M_Y, 2048, F32)
        dma("sync", g_ident, ident_e[:], sd_sm)
        dma("sync", g_mask, mask_e[:], sd_sm)
        dma("sync", g_convw, conv_wt[:], sd_sm)
        dma("sync", g_bdt, b_dt[:], sd_sm)
        dma("sync", g_dpar, d_par[:], sd_sm)
        mk_sm = dma("sync", wx_stage[:, 0:512], w_x[:], sd_sm)
        mk_sm_all = mk_sm
        mamba_load_ln(0)
        mamba_transpose(0, [mk_sm_all, mk_sm_all])

        em.op("vector", lambda e: e.tensor_copy(
            wx_bf[:], wx_stage[:, 0:512]), wait=[mk_sm], inc=[(s_ve, 1)],
            drain=False)
        mk_wx = s_ve.mark()
        mk_wx_stage = mk_wx
        prevc = None
        for m, (srcw, dstw, nrow) in enumerate(
                ((w_dt, wdt_bf, 17), (w_bp, wbp_bf, 16))):
            for hh in range(2):
                w = [prevc] if prevc else []
                mkd = dma("sync", stage[0:nrow, 0:1024],
                          srcw[:, hh * 1024:(hh + 1) * 1024], sd_g1, wait=w)
                em.op("vector",
                      lambda e, dstw=dstw, hh=hh, nrow=nrow: e.tensor_copy(
                          dstw[0:nrow, hh * 1024:(hh + 1) * 1024],
                          stage[0:nrow, 0:1024]),
                      wait=[mkd], inc=[(s_ve, 1)], drain=False)
                prevc = s_ve.mark()
        mk_wdtbp = prevc
        em.op("vector", lambda e: e.memset(xpd_bf[0:32, :], 1.0),
              inc=[(s_ve, 1)], drain=False)
        mk_ones = s_ve.mark()

        win_bf = VA(M_WIN, 65536, BF16)
        win_stage = [VA(M_U, 8192, F32), VA(M_SZ, 8192, F32),
                     VA(M_ACC, 8192, F32), VA(M_SP, 8192, F32)]
        win_sem = [sd_wa, sd_wb, sd_wc, sd_w4]
        mk_win_cast = [None] * ND
        chain = {i: None for i in range(4)}
        gi = 0
        for dj in range(ND):
            for hh in range(2):
                slot = gi % 4
                gi += 1
                mkd = dma("sync" if slot % 2 == 0 else "scalar",
                          win_stage[slot][:, 0:2048],
                          w_in[dj * P:(dj + 1) * P,
                               hh * 2048:(hh + 1) * 2048], win_sem[slot],
                          wait=[chain[slot]] if chain[slot] else [])
                em.op("vector",
                      lambda e, dj=dj, hh=hh, slot=slot: e.tensor_copy(
                          win_bf[:, dj * 4096 + hh * 2048:
                                 dj * 4096 + (hh + 1) * 2048],
                          win_stage[slot][:, 0:2048]),
                      wait=[mkd], inc=[(s_ve, 1)], drain=False)
                mk_win_cast[dj] = chain[slot] = s_ve.mark()

        wout_bf = VA(M_WOUT, 32768, BF16)
        wo_stage = [VA(M_UB, 4096, F32), VA(M_YS, 4096, F32)]
        wo_sem = [sd_wa, sd_wb]
        last_cast = {0: chain[0], 1: chain[1]}
        for ci in range(NCI):
            slot = ci % 2
            mkd = dma("sync", wo_stage[slot][:, 0:1024],
                      w_out[ci * P:(ci + 1) * P, :], wo_sem[slot],
                      wait=[last_cast[slot]] if last_cast[slot] else [])
            em.op("vector",
                  lambda e, ci=ci, slot=slot: e.tensor_copy(
                      wout_bf[:, ci * 1024:(ci + 1) * 1024],
                      wo_stage[slot][:, 0:1024]),
                  wait=[mkd], inc=[(s_ve, 1)], drain=False)
            last_cast[slot] = s_ve.mark()
        mk_startup = s_ve.mark()

        dst1 = xs1 if N_SUBLAYERS > 1 else out_ext

        mk_xz_last = None
        xzbank = {0: None, 1: None}
        mk_c67 = [None]
        mk_gp_acc = [None]
        mk_gp_conv = {0: None, 1: None}
        mk_xp_prev = [None]
        mk_scan_hist = {}
        mk_u = {}
        mk_sz = {}
        mk_sp30_hist = {}
        mk_sp2_hist = {}
        mk_ubm_hist = {}
        mk_xpe_k = {}
        mk_t1 = {-1: None}
        mk_r_hist = {}

        def phaseA_xc_pair(kk, p):
            par = kk % 2
            bkx = p % 2
            for half in range(2):
                ci = 2 * p + half
                wait_mm = [mk_xnt[kk], xzbank[bkx] if half == 0 else None]
                if kk == 0 and ci == 0:
                    wait_mm += [mk_win_cast[-1], mk_wdtbp, mk_wx, mk_ones]
                for dj in range(ND):
                    em.op("tensor",
                          lambda e, dj=dj, ci=ci, bkx=bkx, half=half, par=par: e.matmul(
                              psb[bkx][:, half * TCM:(half + 1) * TCM],
                              win_bf[:, dj * 4096 + ci * P: dj * 4096 + (ci + 1) * P],
                              xnt[par][:, dj * 256:(dj + 1) * 256],
                              start=(dj == 0), stop=(dj == ND - 1)),
                          wait=(wait_mm if dj == 0 else ()),
                          inc=([(s_pe, 1)] if (dj == ND - 1 and half == 1) else ()))
            mk_mm = s_pe.mark()
            em.op("scalar",
                  lambda e, p=p, bkx=bkx: e.copy(
                      ap3(xc, 2 * p * 264 + 3, 264, 2, TCM),
                      psb[bkx]),
                  wait=[mk_mm], inc=[(s_se, 1)], drain=False)
            xzbank[bkx] = s_se.mark()
            for half in range(2):
                ci = 2 * p + half
                em.op("vector",
                      lambda e, ci=ci, half=half: e.tensor_scalar(
                          acc_p[:, half * TCM:(half + 1) * TCM],
                          xc[:, ci * 264 + 0: ci * 264 + TCM],
                          g_convw[:, ci * 4: ci * 4 + 1], None,
                          op0=ALU.mult),
                      wait=([mk_gp_acc[0], xzbank[bkx]] if half == 0 else ()),
                      drain=(half == 0))
            for half in range(2):
                ci = 2 * p + half
                em.op("vector",
                      lambda e, ci=ci, half=half: e.scalar_tensor_tensor(
                          acc_p[:, half * TCM:(half + 1) * TCM],
                          xc[:, ci * 264 + 1: ci * 264 + 1 + TCM],
                          g_convw[:, ci * 4 + 1: ci * 4 + 2],
                          acc_p[:, half * TCM:(half + 1) * TCM],
                          op0=ALU.mult, op1=ALU.add),
                      drain=(half == 0),
                      inc=([(s_ve, 1)] if half == 1 else ()))
            mk_acc = s_ve.mark()
            pp = p % 2
            for half in range(2):
                ci = 2 * p + half
                em.op("scalar",
                      lambda e, ci=ci, half=half, pp=pp: e.mul(
                          r2_p[pp][:, half * TCM:(half + 1) * TCM],
                          xc[:, ci * 264 + 2: ci * 264 + 2 + TCM],
                          g_convw[:, ci * 4 + 2: ci * 4 + 3]),
                      wait=([xzbank[bkx], mk_gp_conv[pp]] if half == 0 else ()),
                      drain=False)
                em.op("scalar",
                      lambda e, ci=ci, half=half, pp=pp: e.mul(
                          r3_p[pp][:, half * TCM:(half + 1) * TCM],
                          xc[:, ci * 264 + 3: ci * 264 + 3 + TCM],
                          g_convw[:, ci * 4 + 3: ci * 4 + 4]),
                      drain=False,
                      inc=([(s_se, 1)] if half == 1 else ()))
            mk_r = s_se.mark()
            mk_r_hist[(kk, p)] = mk_r
            em.op("gpsimd",
                  lambda e, pp=pp: e.tensor_add(
                      r2_p[pp][:], r2_p[pp][:], acc_p[:]),
                  wait=[mk_acc, mk_r], inc=[(s_gp, 1)], drain=False)
            mk_gp_acc[0] = s_gp.mark()
            em.op("gpsimd",
                  lambda e, pp=pp: e.tensor_add(
                      r2_p[pp][:], r2_p[pp][:], r3_p[pp][:]),
                  inc=[(s_gp, 1)])
            mk_gp_conv[pp] = s_gp.mark()
            w_sil = [mk_gp_conv[pp]]
            if kk >= 1:
                w_sil += [mk_gp_hist.get((kk - 1, p)), mk_xp_prev[0]]
            else:
                w_sil.append(mk_startup)
            em.op("scalar",
                  lambda e, p=p, pp=pp: e.activation(
                      u_bf[:, 2 * p * TCM: 2 * (p + 1) * TCM],
                      r2_p[pp][:], AF.Silu),
                  wait=w_sil, inc=[(s_se, 1)], drain=False)
            mk_u[(kk, p)] = s_se.mark()

        def phaseA_z_pair(kk, zp):
            par = kk % 2
            bkx = zp % 2
            for half in range(2):
                ci = NCI + 2 * zp + half
                wait_mm = [mk_xnt[kk], xzbank[bkx] if half == 0 else None]
                for dj in range(ND):
                    em.op("tensor",
                          lambda e, dj=dj, ci=ci, bkx=bkx, half=half, par=par: e.matmul(
                              psb[bkx][:, half * TCM:(half + 1) * TCM],
                              win_bf[:, dj * 4096 + ci * P: dj * 4096 + (ci + 1) * P],
                              xnt[par][:, dj * 256:(dj + 1) * 256],
                              start=(dj == 0), stop=(dj == ND - 1)),
                          wait=(wait_mm if dj == 0 else ()),
                          inc=([(s_pe, 1)] if (dj == ND - 1 and half == 1) else ()))
            mk_mm = s_pe.mark()
            w_sil = [mk_mm]
            if kk >= 1:
                w_sil.append(mk_gp_hist.get((kk - 1, zp)))
            elif zp == 0:
                w_sil.append(mk_startup)
            em.op("scalar",
                  lambda e, zp=zp, bkx=bkx: e.activation(
                      sz_bf[:, 2 * zp * TCM: 2 * (zp + 1) * TCM],
                      psb[bkx], AF.Silu),
                  wait=w_sil, inc=[(s_se, 1)], drain=False)
            xzbank[bkx] = mk_sz[(kk, zp)] = s_se.mark()

        def xp_sec(kk):
            em.op("vector",
                  lambda e: e.tensor_copy(
                      ap3(xc, 0, 264, NCI, 3),
                      ap3(xc, 256, 264, NCI, 3)),
                  wait=[mk_r_hist[(kk, 7)]])
            for ci in range(NCI):
                em.op("tensor",
                      lambda e, ci=ci: e.matmul(
                          psb[5][0:16, 0:TCM],
                          wx_bf[:, ci * 32: ci * 32 + 16],
                          u_bf[:, ci * TCM:(ci + 1) * TCM],
                          start=(ci == 0), stop=(ci == NCI - 1)),
                      wait=[mk_u[(kk, ci // 2)]])
                em.op("tensor",
                      lambda e, ci=ci: e.matmul(
                          psb[4][0:16, 0:TCM],
                          wx_bf[:, ci * 32 + 16: ci * 32 + 32],
                          u_bf[:, ci * TCM:(ci + 1) * TCM],
                          start=(ci == 0), stop=(ci == NCI - 1)),
                      inc=([(s_pe, 1)] if ci == NCI - 1 else ()))
            mk_xp = s_pe.mark()
            mk_xp_prev[0] = mk_xp
            em.op("vector", lambda e: e.tensor_copy(
                xpd_bf[0:16, :], psb[5][0:16, 0:TCM]), wait=[mk_xp],
                drain=False)
            em.op("vector", lambda e: e.tensor_copy(
                xpb_bf[0:16, :], psb[4][0:16, 0:TCM]), inc=[(s_ve, 1)],
                drain=False)
            mk_xpe_k[kk] = s_ve.mark()

        def wout_t0(k, pi):
            w0 = [mk_gp_hist[(k, pi)]]
            if pi == 0:
                w0.append(mk_c67[0])
            for j in range(2):
                ci = 2 * pi + j
                for half in range(2):
                    em.op("tensor",
                          lambda e, ci=ci, half=half, bk=6 + half: e.matmul(
                              psb[bk],
                              y_bf[:, ci * TCM: ci * TCM + P],
                              wout_bf[:, ci * 1024 + half * 512: ci * 1024 + (half + 1) * 512],
                              start=(ci == 0), stop=(ci == NCI - 1)),
                          wait=(w0 if (j == 0 and half == 0) else ()),
                          inc=([(s_pe, 1)] if (j == 1 and half == 1) else ()))

        def phaseB_pair(k, pi):
            ci0 = 2 * pi
            pp = pi % 2
            w0 = [mk_xpe_k[k], mk_sp30_hist.get((k, pi - 1)),
                  mk_ubm_hist.get((k, pi - 1))]
            for half in range(2):
                ci = ci0 + half
                em.op("tensor",
                      lambda e, ci=ci, half=half: e.matmul(
                          psb[2][:, half * TCM:(half + 1) * TCM],
                          wdt_bf[0:17, ci * P:(ci + 1) * P],
                          xpd_bf[0:17, :], start=True, stop=True),
                      wait=(w0 if half == 0 else ()))
                em.op("tensor",
                      lambda e, ci=ci, half=half: e.matmul(
                          psb[3][:, half * TCM:(half + 1) * TCM],
                          wbp_bf[0:16, ci * P:(ci + 1) * P],
                          xpb_bf[0:16, :], start=True, stop=True),
                      inc=([(s_pe, 1)] if half == 1 else ()))
            mk_mm = s_pe.mark()
            if pi >= 1:
                wout_t0(k, pi - 1)
            em.op("vector",
                  lambda e, pp=pp: e.tensor_scalar(
                      sp[pp][:], psb[2][:], 30.0, None, op0=ALU.min),
                  wait=[mk_mm, mk_sp2_hist.get((k, pi - 2))],
                  inc=[(s_ve, 1)], drain=False)
            mk_sp30 = mk_sp30_hist[(k, pi)] = s_ve.mark()
            em.op("scalar", lambda e, pp=pp: e.activation(
                sp[pp][:], sp[pp][:], AF.Exp),
                wait=[mk_sp30, mk_scan_hist.get((k, pi - 2))], drain=False)
            em.op("scalar", lambda e, pp=pp: e.activation(
                sp[pp][:], sp[pp][:], AF.Ln, bias=1.0))
            em.op("scalar", lambda e, pp=pp: e.activation(
                a_f[pp][:], sp[pp][:], AF.Relu,
                scale=-1.0, bias=1.0), inc=[(s_se, 1)])
            mk_sp2 = mk_sp2_hist[(k, pi)] = s_se.mark()
            prev_gp = mk_gp_hist.get((k, pi - 2)) or \
                (mk_gp_hist.get((k - 1, 6 + pi)) if k > 0 else None)
            em.op("vector",
                  lambda e, ci0=ci0, pp=pp: e.tensor_mul(
                      ub_f[pp][:],
                      u_bf[:, ci0 * TCM:(ci0 + 2) * TCM],
                      psb[3][:]),
                  wait=[mk_u[(k, pi)], prev_gp], inc=[(s_ve, 1)], drain=False)
            mk_ubm_hist[(k, pi)] = s_ve.mark()
            em.op("vector", lambda e, pp=pp: e.tensor_scalar(
                a_f[pp][:], a_f[pp][:], 0.9999, None,
                op0=ALU.min), wait=[mk_sp2], drain=False)
            for half in range(2):
                ci = ci0 + half
                em.op("vector",
                      lambda e, ci=ci, half=half, pp=pp: e.tensor_tensor_scan(
                          ys_f[pp][:, half * TCM:(half + 1) * TCM],
                          a_f[pp][:, half * TCM:(half + 1) * TCM],
                          ub_f[pp][:, half * TCM:(half + 1) * TCM],
                          g_carry[:, ci:ci + 1],
                          op0=ALU.mult, op1=ALU.add),
                      wait=([prev_gp] if half == 0 else ()),
                      drain=(half == 0),
                      inc=([(s_ve, 1)] if half == 1 else ()))
            mk_scan = mk_scan_hist[(k, pi)] = s_ve.mark()
            for half in range(2):
                ci = ci0 + half
                em.op("gpsimd", lambda e, ci=ci, half=half, pp=pp: e.tensor_copy(
                    g_carry[:, ci:ci + 1],
                    ys_f[pp][:, (half + 1) * TCM - 1:(half + 1) * TCM]),
                    wait=([mk_scan] if half == 0 else ()), drain=False)
            for half in range(2):
                ci = ci0 + half
                em.op("vector",
                      lambda e, ci=ci, half=half, pp=pp: e.scalar_tensor_tensor(
                          ub_f[pp][:, half * TCM:(half + 1) * TCM],
                          u_bf[:, ci * TCM:(ci + 1) * TCM],
                          g_dpar[:, ci:ci + 1],
                          ys_f[pp][:, half * TCM:(half + 1) * TCM],
                          op0=ALU.mult, op1=ALU.add),
                      drain=(half == 0),
                      inc=([(s_ve, 1)] if half == 1 else ()))
            mk_stt = s_ve.mark()
            w_y = [mk_stt, mk_sz.get((k, pi))]
            if k >= 1 and pi == 0:
                w_y.append(mk_t1[k - 1])
            em.op("gpsimd",
                  lambda e, ci0=ci0, pp=pp: e.tensor_mul(
                      y_bf[:, ci0 * TCM:(ci0 + 2) * TCM],
                      ub_f[pp][:],
                      sz_bf[:, ci0 * TCM:(ci0 + 2) * TCM]),
                  wait=w_y, inc=[(s_gp, 1)], drain=False)
            mk_gp_hist[(k, pi)] = s_gp.mark()

        def mamba_tail(k):
            par = k % 2
            mk_t0_mm = s_pe.mark()
            for half in range(2):
                em.op("vector",
                      lambda e, half=half, par=par, bk=6 + half: e.tensor_add(
                          xin[par * 2][:, half * 512:(half + 1) * 512],
                          xin[par * 2][:, half * 512:(half + 1) * 512],
                          psb[bk]),
                      wait=[mk_t0_mm], inc=[(s_ve, 1)], drain=False)
            ev0 = s_ve.mark()
            st_sem = sd_m0 if k == 0 else (sd_m1 if k == 1 else sd_st[par])
            mk_store[k] = dma("scalar",
                              dst1[(k * NTM) * P:(k * NTM + 1) * P, :],
                              xin[par * 2][:], st_sem, wait=[ev0])
            for ci in range(NCI):
                for half in range(2):
                    em.op("tensor",
                          lambda e, ci=ci, half=half, bk=6 + half: e.matmul(
                              psb[bk],
                              y_bf[:, ci * TCM + P: ci * TCM + 2 * P],
                              wout_bf[:, ci * 1024 + half * 512: ci * 1024 + (half + 1) * 512],
                              start=(ci == 0), stop=(ci == NCI - 1)),
                          wait=([ev0] if (ci == 0 and half == 0) else ()),
                          inc=([(s_pe, 1)] if (ci == NCI - 1 and half == 1) else ()))
            mk_t1[k] = s_pe.mark()
            for half in range(2):
                em.op("vector",
                      lambda e, half=half, par=par, bk=6 + half: e.tensor_add(
                          xin[par * 2 + 1][:, half * 512:(half + 1) * 512],
                          xin[par * 2 + 1][:, half * 512:(half + 1) * 512],
                          psb[bk]),
                      wait=[mk_t1[k]], inc=[(s_ve, 1)], drain=False)
            ev1 = s_ve.mark()
            mk_c67[0] = ev1
            mk_store[k] = dma("scalar",
                              dst1[(k * NTM + 1) * P:(k * NTM + 2) * P, :],
                              xin[par * 2 + 1][:], st_sem, wait=[ev1])
            mk_cevax[par] = mk_store[k]

        for p in range(8):
            phaseA_xc_pair(0, p)
        for zp in range(8):
            phaseA_z_pair(0, zp)
        mk_xz_last = s_pe.mark()
        xp_sec(0)
        mamba_load_ln(1)
        mamba_transpose(1, [mk_xpe_k[0], mk_xpe_k[0]])

        wg_bf_g = VA(F_WG, 49152, BF16)
        wu_bf_g = VA(F_WU, 49152, BF16)
        wstg = [VA(F_STG, 6144, I32), VA(F_STG2, 6144, I32),
                VA(F_XIN2 + 2 * 4096, 6144, I32)]
        wsem = [sd_wa, sd_wb, sd_wc]
        w_prev = {0: None, 1: None, 2: None}
        w_mk = {}

        def emit_w_unit(gi):
            m2, r = divmod(gi, 16)
            dj, hh = divmod(r, 2)
            slot = (r % 2) if m2 == 0 else (r % 3)
            srcw = wg_t if m2 == 0 else wu_t
            dstw = wg_bf_g if m2 == 0 else wu_bf_g
            w = [mk_xz_last]
            if slot == 2:
                w.append(mk_mamba_pe_end)
            if w_prev[slot]:
                w.append(w_prev[slot])
            mkd = dma("sync", wstg[slot][:],
                      srcw[dj * P:(dj + 1) * P, hh * 1536:(hh + 1) * 1536],
                      wsem[slot], wait=w)
            cw = [mkd]
            if m2 == 1 and r == 0:
                cw += [mk_mamba_pe_end, mk_mamba_ve_end, mk_mamba_sta,
                       mk_mamba_stb]
            em.op("gpsimd",
                  lambda e, dj=dj, hh=hh, dstw=dstw, slot=slot: e.tensor_copy(
                      dstw[:, dj * H + hh * 1536: dj * H + (hh + 1) * 1536],
                      wstg[slot][:]),
                  wait=cw, inc=[(s_gp, 1)], drain=False)
            w_prev[slot] = w_mk[gi] = s_gp.mark()

        ffn1_pre = {}
        fxA_g = [VA(F_XIN2 + i * 4096, 4096, F32) for i in range(2)]
        fxB_g = [VA(F_XINB + i * 4096, 4096, F32) for i in range(2)]

        def emit_ffn1_preamble():
            mk_m01 = [(sd_m0, 32), (sd_m1, 32)]
            for i in range(2):
                mk = dma("scalar", fxB_g[i][:],
                         xs1[(2 + i) * P:(3 + i) * P, :], sd_sm,
                         wait=[mk_xz_last] + mk_m01)
            for i in range(2):
                mk = dma("scalar", fxA_g[i][:],
                         xs1[i * P:(1 + i) * P, :], sd_sm,
                         wait=[mk_xpe_k[7], mk_r_hist[(7, 7)]] + mk_m01)
            return {"mk_loads": mk}

        for k in range(NKM):
            for pi in range(8):
                phaseB_pair(k, pi)
                if k + 1 < NKM:
                    phaseA_xc_pair(k + 1, pi)
                    phaseA_z_pair(k + 1, pi)
                if k == NKM - 1 and N_SUBLAYERS >= 2:
                    emit_w_unit(2 * pi)
                    emit_w_unit(2 * pi + 1)
            if k + 1 == NKM - 1:
                mk_xz_last = s_pe.mark()
            if k == NKM - 1:
                mk_b7_pe = s_pe.mark()
            wout_t0(k, 7)
            mamba_tail(k)
            if k + 1 < NKM:
                xp_sec(k + 1)
                if k + 2 < NKM:
                    mamba_load_ln(k + 2)
                    mamba_transpose(k + 2, [mk_xpe_k[k + 1], mk_xpe_k[k + 1]])
            if k == NKM - 2 and N_SUBLAYERS >= 2:
                ffn1_pre.update(emit_ffn1_preamble())

        mk_mamba_pe_end = s_pe.mark()
        mk_mamba_ve_end = s_ve.mark()
        mk_mamba_gp_end = s_gp.mark()
        mk_mamba_sta = sd_sta.mark()
        mk_mamba_stb = sd_stb.mark()

        def ffn(src, dst, first, region, store_mark_fn, pre=None):
            wg_bf = VA(F_WG, 49152, BF16)
            wu_bf = VA(F_WU, 49152, BF16)
            wd_sb = [VA(F_WD + i * 2048, 2048, BF16) for i in range(2)]
            fxA = [VA(F_XIN2 + i * 4096, 4096, F32) for i in range(NTF)]
            fxB = [VA(F_XINB + i * 4096, 4096, F32) for i in range(2)]

            fxC = [VA(F_STG, 4096, F32), VA(F_STG + 4096, 4096, F32)]

            def fxt(c, tt):
                if tt >= 2:
                    if c == 0:
                        return fxB[tt - 2] if pre else fxA[tt]
                    if pre:
                        return fxA[tt] if c % 2 == 1 else fxC[tt - 2]
                    return fxC[tt - 2] if c % 2 == 1 else fxA[tt]
                if c % 2 == 1:
                    return fxB[tt]
                return fxA[tt]
            fxn = [VA(F_XN + i * 2048, 2048, BF16) for i in range(2)]
            fxnt = [VA(F_XNT + i * 8192, 8192, BF16) for i in range(2)]
            h_sb = VA(F_H, 24576, BF16)
            gs = [VA(F_GS + i * 2048, 2048, F32) for i in range(2)]
            stg = [VA(F_STG, 6144, I32), VA(F_STG2, 6144, I32)]
            stg_wd = [VA(F_STG, 4096, I32), VA(F_STG2, 4096, I32)]

            mk_ln_f = [None] * NKF
            mk_xnt_f = [None] * NKF
            mk_fstore = [None] * NT
            mk_fst_chunk = [None] * NKF
            tr_banks = {4: None, 5: None}
            gu_mark = {0: None, 1: None}
            dn_ev = {i: None for i in range(8)}

            def f_ln_tiles(c, tts, col0):
                tiles = [fxt(c, tt) for tt in tts]
                if pre and c == 0:
                    mk_x = pre["mk_loads"]
                else:
                    for i, tt in enumerate(tts):
                        gtile = c * NTF + tt
                        w = list(store_mark_fn(gtile))
                        if tt < 2:
                            if c <= 1:
                                w += region["xin"]
                            else:
                                w.append(mk_fst_chunk[c - 2])
                            if first and c == 1:
                                w.append(mk_fst_chunk[0])
                        else:
                            if c == 0:
                                w += region["xin"]
                            elif c == 1:
                                if first:
                                    w.append(prev_cast[0])
                                else:
                                    w += region["xin"]
                            else:
                                w.append(mk_fst_chunk[c - 2])
                        mk_x = dma("scalar", tiles[i][:],
                                   src[gtile * P:(gtile + 1) * P, :],
                                   sd_x[c % 2], wait=w)
                mst = ln_stats_group([t[:] for t in tiles], col0, [mk_x])
                for i, tt in enumerate(tts):
                    col = col0 + i
                    ew = trdone.get(("f", tt % 2))
                    if pre and c == 0 and i == 0 and tts[0] == 0:
                        ew = [mk_b7_pe, mk_mamba_gp_end,
                              mk_mamba_ve_end, ew]
                    mkn = ln_norm(fxn[tt % 2][:], tiles[i][:], col, mst,
                                  ew, first=(i == 0))
                    bank = 4 + tt % 2
                    mk_tr = transpose_tile(fxn[tt % 2][:], bank, mkn,
                                           tr_banks[bank])
                    trdone[("f", tt % 2)] = mk_tr
                    tr_banks[bank] = evac_xnt(fxnt[c % 2], 512, tt, bank,
                                              mk_tr)
                mk_xnt_f[c] = tr_banks[4 + tts[-1] % 2]

            if pre:
                f_ln_tiles(0, [0, 1, 2, 3], 8)
            else:
                f_ln_tiles(0, [0, 1], 8)
                f_ln_tiles(0, [2, 3], 10)

            mk_wg = [None] * ND
            mk_wu = [None] * ND
            prev_cast = [None]
            if first:
                for gi in range(16, 32):
                    emit_w_unit(gi)
                for dj in range(ND):
                    mk_wg[dj] = w_mk[2 * dj + 1]
                    mk_wu[dj] = w_mk[16 + 2 * dj + 1]
                prev_cast[0] = w_mk[31]
                for dstw, cache, mkk in ((wg_bf, wg_c, mk_wg),
                                         (wu_bf, wu_c, mk_wu)):
                    for dj in range(ND):
                        em.op("scalar",
                              lambda e, dj=dj, dstw=dstw, cache=cache: e.dma_start(
                                  cache[dj * P:(dj + 1) * P, :],
                                  dstw[:, dj * H:(dj + 1) * H]),
                              wait=[mkk[dj]], inc=[(sd_cache, 16)],
                              drain=False)
            else:
                for m, (cache, dstw, mkk, gsem) in enumerate(
                        ((wg_c, wg_bf, mk_wg, sd_g1), (wu_c, wu_bf, mk_wu, sd_g2))):
                    mkreg = region["wg"] if m == 0 else region["wu"]
                    for dj in range(ND):
                        mkk[dj] = dma("sync", dstw[:, dj * H:(dj + 1) * H],
                                      cache[dj * P:(dj + 1) * P, :], gsem,
                                      wait=mkreg)
                    mkk[ND - 1] = gsem.mark()

            gu_banks = [(0, 1), (2, 3)]
            cache_mk = {}
            mk_cache_all = []
            prev_wd_cons = {0: None, 1: None}
            for c in range(NKF):
                par = c % 2
                mk_h = {}
                if c == 0:
                    silu_mk = {}
                    for phase in range(2):
                        mkw = mk_wg if phase == 0 else mk_wu
                        for hi in range(NH):
                            bk = gu_banks[hi % 2][phase]
                            if phase == 0:
                                w = [mk_xnt_f[c], mkw[ND - 1],
                                     gu_mark[hi % 2]]
                            else:
                                w = [mkw[0] if first else mkw[ND - 1],
                                     silu_mk[hi], mk_h.get(hi - 2)]
                            wsrc = wg_bf if phase == 0 else wu_bf
                            for dj in range(ND):
                                em.op("tensor",
                                      lambda e, hi=hi, dj=dj, bk=bk, par=par, wsrc=wsrc: e.matmul(
                                          psb[bk],
                                          wsrc[:, dj * H + hi * P: dj * H + (hi + 1) * P],
                                          fxnt[par][:, dj * 512:(dj + 1) * 512],
                                          start=(dj == 0), stop=(dj == ND - 1)),
                                      wait=(w if dj == 0 else
                                            ([mkw[dj]] if (phase == 1 and
                                                           first) else ())),
                                      inc=([(s_pe, 1)] if dj == ND - 1 else ()))
                            mk_mm = s_pe.mark()
                            if phase == 0:
                                em.op("scalar",
                                      lambda e, hi=hi, bk=bk: e.activation(
                                          h_sb[:, hi * 512:(hi + 1) * 512],
                                          psb[bk], AF.Silu, scale=1.0 / 127.0),
                                      wait=[mk_mm] + (region["h"] if hi == 0 else []),
                                      inc=[(s_se, 1)], drain=False)
                                gu_mark[hi % 2] = silu_mk[hi] = s_se.mark()
                            else:
                                em.op("vector",
                                      lambda e, hi=hi, bk=bk: e.tensor_mul(
                                          h_sb[:, hi * 512:(hi + 1) * 512],
                                          psb[bk],
                                          h_sb[:, hi * 512:(hi + 1) * 512]),
                                      wait=[mk_mm], inc=[(s_ve, 1)],
                                      drain=False)
                                gu_mark[hi % 2] = mk_h[hi] = s_ve.mark()
                else:
                    for hi in range(NH):
                        bks = gu_banks[hi % 2]
                        w = [mk_xnt_f[c], gu_mark[hi % 2]]
                        for phase in range(2):
                            wsrc = wg_bf if phase == 0 else wu_bf
                            bk = bks[phase]
                            for dj in range(ND):
                                em.op("tensor",
                                      lambda e, hi=hi, dj=dj, bk=bk, par=par, wsrc=wsrc: e.matmul(
                                          psb[bk],
                                          wsrc[:, dj * H + hi * P: dj * H + (hi + 1) * P],
                                          fxnt[par][:, dj * 512:(dj + 1) * 512],
                                          start=(dj == 0), stop=(dj == ND - 1)),
                                      wait=(w if (dj == 0 and phase == 0) else ()),
                                      inc=([(s_pe, 1)] if (dj == ND - 1 and phase == 1) else ()))
                        mk_mm = s_pe.mark()
                        em.op("scalar",
                              lambda e, hi=hi, bk=bks[0]: e.activation(
                                  gs[hi % 2][:], psb[bk], AF.Silu,
                                  scale=1.0 / 127.0),
                              wait=[mk_mm], inc=[(s_se, 1)], drain=False)
                        mk_sil = s_se.mark()
                        em.op("vector",
                              lambda e, hi=hi, bk=bks[1]: e.tensor_mul(
                                  h_sb[:, hi * 512:(hi + 1) * 512],
                                  gs[hi % 2][:], psb[bk]),
                              wait=[mk_sil], inc=[(s_ve, 1)], drain=False)
                        gu_mark[hi % 2] = mk_h[hi] = s_ve.mark()
                mk_h_all = mk_h[NH - 1]
                if c + 1 < NKF and not (first and c == 0):
                    f_ln_tiles(c + 1, [0, 1], 8)
                    f_ln_tiles(c + 1, [2, 3], 10)

                dn_bk = [(0, 1), (2, 3), (6, 7), (4, 5)]
                mk_wd = [None] * NH
                wd_cons = {}
                for hi in range(NH):
                    slot = hi % 2
                    w = []
                    if hi >= 2:
                        w.append(wd_cons[hi - 2])
                        if cache_mk.get(hi - 2):
                            w.append(cache_mk[hi - 2])
                    else:
                        w.append(prev_wd_cons[slot])
                        if c == 0:
                            w += region["wd"]
                    wdsem = sd_wc if slot == 0 else sd_w4
                    wdq = "sync" if slot == 0 else "scalar"
                    if first and c == 0:
                        mkd = dma(wdq, stg_wd[slot][:],
                                  wd_t[hi * P:(hi + 1) * P, :], wdsem,
                                  wait=w + ([prev_cast[0]] if (hi < 2 and prev_cast[0]) else []))
                        em.op("gpsimd",
                              lambda e, slot=slot: e.tensor_copy(
                                  wd_sb[slot][:], stg_wd[slot][:]),
                              wait=[mkd], inc=[(s_gp, 1)], drain=False)
                        mk_wd[hi] = s_gp.mark()
                        cache_mk[hi] = dma("gpsimd",
                                           wd_c[hi * P:(hi + 1) * P, :],
                                           wd_sb[slot][:],
                                           sd_ca0 if slot == 0 else sd_ca1,
                                           wait=[mk_wd[hi]])
                    else:
                        if first:
                            w += mk_cache_all
                        mk_wd[hi] = dma(wdq, wd_sb[slot][:],
                                        wd_c[hi * P:(hi + 1) * P, :], wdsem,
                                        wait=w)
                    for tt in range(NTF):
                        for half in range(2):
                            bk = dn_bk[tt][half]
                            w2 = [mk_h_all, mk_wd[hi]]
                            if hi == 0:
                                w2.append(gu_mark[0] if bk in (0, 1, 2, 3) else None)
                                w2.append(gu_mark[1] if bk in (2, 3) else None)
                                w2.append(tr_banks.get(bk) if bk in (4, 5) else None)
                                w2.append(dn_ev.get(bk) if bk in (6, 7) else None)
                            em.op("tensor",
                                  lambda e, hi=hi, tt=tt, half=half, bk=bk, slot=slot: e.matmul(
                                      psb[bk],
                                      h_sb[:, hi * 512 + tt * P: hi * 512 + (tt + 1) * P],
                                      wd_sb[slot][:, half * 512:(half + 1) * 512],
                                      start=(hi == 0), stop=(hi == NH - 1)),
                                  wait=(w2 if hi == 0 or (tt == 0 and half == 0) else ()),
                                  inc=([(s_pe, 1)] if (tt == NTF - 1 and half == 1) else ()))
                    wd_cons[hi] = s_pe.mark()
                prev_wd_cons[0] = wd_cons[NH - 2]
                prev_wd_cons[1] = wd_cons[NH - 1]
                mk_dn = s_pe.mark()
                for tt in range(NTF):
                    fxc = fxt(c, tt)
                    for half in range(2):
                        bk = dn_bk[tt][half]
                        em.op("vector",
                              lambda e, fxc=fxc, half=half, bk=bk: e.scalar_tensor_tensor(
                                  fxc[:, half * 512:(half + 1) * 512],
                                  psb[bk], 1.0 / (127.0 * 127.0),
                                  fxc[:, half * 512:(half + 1) * 512],
                                  op0=ALU.mult, op1=ALU.add),
                              wait=[mk_dn], inc=[(s_ve, 1)], drain=False)
                    mk_ev = s_ve.mark()
                    for bk in dn_bk[tt]:
                        dn_ev[bk] = mk_ev
                    gtile = c * NTF + tt
                    mk_fstore[gtile] = dma(
                        "scalar", dst[gtile * P:(gtile + 1) * P, :],
                        fxc[:], sd_st[c % 2], wait=[mk_ev])
                    mk_fst_chunk[c] = mk_fstore[gtile]
                if c == 0:
                    mk_cache_all.append(sd_cache.mark())
                    mk_cache_all.append(sd_ca0.mark())
                    mk_cache_all.append(sd_ca1.mark())
                    mk_cache_all.append(sd_wc.mark())
                    mk_cache_all.append(sd_w4.mark())
                if c + 1 < NKF and first and c == 0:
                    f_ln_tiles(c + 1, [0, 1], 8)
                    f_ln_tiles(c + 1, [2, 3], 10)
                gu_mark[0] = dn_ev[0]
                gu_mark[1] = dn_ev[2]
                tr_banks[4] = dn_ev[4]
                tr_banks[5] = dn_ev[4]
            return mk_fstore

        if N_SUBLAYERS >= 2:
            region1 = {
                "wg": [mk_xz_last],
                "wu": [mk_mamba_pe_end, mk_mamba_ve_end, mk_mamba_sta,
                       mk_mamba_stb],
                "wd": [mk_mamba_pe_end, mk_mamba_ve_end, mk_mamba_sta,
                       mk_mamba_stb],
                "xin": [mk_mamba_gp_end, mk_mamba_ve_end],
                "xnt": [mk_mamba_gp_end, mk_mamba_ve_end],
                "h": [mk_mamba_pe_end, mk_mamba_gp_end],
            }
            dst2 = xs2 if N_SUBLAYERS > 2 else out_ext
            mk_f1_store = ffn(xs1, dst2, True, region1,
                              lambda gt: [mk_mamba_sta, mk_mamba_stb],
                              pre=ffn1_pre)
            mk_f1_pe = s_pe.mark()
            mk_f1_ve = s_ve.mark()
            mk_f1_sta = sd_sta.mark()
            mk_f1_stb = sd_stb.mark()
            mk_f1_cache = sd_cache.mark()
            mk_f1_ca0 = sd_ca0.mark()
            mk_f1_ca1 = sd_ca1.mark()

        if N_SUBLAYERS >= 3:
            a_xnt = VA(T_XNT, 32768, BF16)
            a_kt = VA(T_KT, 32768, BF16)
            a_v = VA(T_V, 65536 - 32768, BF16)
            a_wt = [VA(T_WT + i * 16384, 16384, BF16) for i in range(2)]
            a_probs = [VA(T_PROBS + i * 4096, 4096, BF16) for i in range(2)]
            a_probt = VA(T_PROBT, 4096, BF16)
            a_x = [VA(T_X + i * 4096, 4096, F32) for i in range(2)]
            a_xn = [VA(T_XN + i * 2048, 2048, BF16) for i in range(2)]
            a_xpno = [VA(T_XPNO + i * 4096, 4096, F32) for i in range(2)]
            a_xpni = [VA(T_XPNI + i * 4096, 4096, F32) for i in range(2)]
            a_xout = [VA(T_XOUT + i * 4096, 4096, F32) for i in range(2)]
            wt_stage = [V(A0 + T_PROBS, 4096, F32),
                        V(A0 + T_PROBS + 4096, 4096, F32)]

            mk_xpn_st = [None] * NT
            prev_x = {0: None, 1: None}
            prev_xpno = {0: None, 1: None}
            a_trb = {4: mk_f1_ve, 5: mk_f1_ve}
            for tpair in range(NT // 2):
                mk_x = []
                for slot in range(2):
                    t = 2 * tpair + slot
                    w = [mk_f1_sta, mk_f1_stb, prev_x[slot]]
                    if t < 2:
                        w.append(mk_f1_ve)
                    mk_x.append(dma("scalar", a_x[slot][:],
                                    xs2[t * P:(t + 1) * P, :],
                                    sd_x[slot], wait=w))
                mst = ln_stats_group([a_x[0][:], a_x[1][:]], 12, mk_x,
                                     extra_1pr=True)
                for slot in range(2):
                    t = 2 * tpair + slot
                    col = 12 + slot
                    mkn = ln_norm(a_xn[slot][:], a_x[slot][:], col, mst,
                                  trdone.get(("a", slot)),
                                  first=(slot == 0))
                    em.op("vector",
                          lambda e, slot=slot, j=col: e.tensor_scalar(
                              a_xpno[slot][:], a_x[slot][:],
                              g_1pr[:, j:j + 1], g_nmr[:, j:j + 1],
                              op0=ALU.mult, op1=ALU.add),
                          wait=[prev_xpno[slot]], inc=[(s_ve, 1)],
                          drain=False)
                    prev_x[slot] = s_ve.mark()
                    mk_xpn_st[t] = dma("gpsimd", xpn_d[t * P:(t + 1) * P, :],
                                       a_xpno[slot][:],
                                       sd_stp if slot == 0 else sd_stq,
                                       wait=[prev_x[slot]])
                    prev_xpno[slot] = mk_xpn_st[t]
                    bank = 4 + slot
                    mk_tr = transpose_tile(a_xn[slot][:], bank, mkn,
                                           a_trb[bank])
                    trdone[("a", slot)] = mk_tr
                    a_trb[bank] = evac_xnt(a_xnt, 2048, t, bank, mk_tr)
            mk_xnt_a = a_trb[4 + (NT - 1) % 2]
            mk_xnt_a2 = s_ve.mark()

            mk_wt = [[None] * ND, [None] * ND]
            prev_stg = {0: None, 1: None}
            wi = 0
            for halfw in range(2):
                for dj in range(ND):
                    slot = wi % 2
                    wi += 1
                    w = [prev_stg[slot]]
                    if halfw == 0 and dj < 2:
                        w += [mk_f1_ve, mk_f1_sta, mk_f1_stb, mk_f1_pe]
                    mkd = dma("sync", wt_stage[slot][:, 0:1024],
                              wt_at[dj * P:(dj + 1) * P,
                                    halfw * 1024:(halfw + 1) * 1024],
                              sd_wa if slot == 0 else sd_wb, wait=w)
                    em.op("vector",
                          lambda e, halfw=halfw, dj=dj, slot=slot: e.tensor_copy(
                              a_wt[halfw][:, dj * 1024:(dj + 1) * 1024],
                              wt_stage[slot][:, 0:1024]),
                          wait=[mkd] + ([mk_f1_pe] if (halfw == 0 and dj < 2) else []),
                          inc=[(s_ve, 1)], drain=False)
                    prev_stg[slot] = mk_wt[halfw][dj] = s_ve.mark()

            kv_ev = {b: None for b in range(4)}
            bi = 0
            for jc in range(ND):
                for n in range(4):
                    bk = bi % 4
                    bi += 1
                    w = [mk_xnt_a2, mk_wt[0][ND - 1], kv_ev[bk]]
                    for dj in range(ND):
                        em.op("tensor",
                              lambda e, jc=jc, n=n, dj=dj, bk=bk: e.matmul(
                                  psb[bk],
                                  a_wt[0][:, dj * 1024 + jc * P: dj * 1024 + (jc + 1) * P],
                                  a_xnt[:, dj * 2048 + n * 512: dj * 2048 + (n + 1) * 512],
                                  start=(dj == 0), stop=(dj == ND - 1)),
                              wait=(w if dj == 0 else ()),
                              inc=([(s_pe, 1)] if dj == ND - 1 else ()))
                    mk_mm = s_pe.mark()
                    em.op("vector",
                          lambda e, jc=jc, n=n, bk=bk: e.tensor_copy(
                              a_kt[:, jc * 2048 + n * 512: jc * 2048 + (n + 1) * 512],
                              psb[bk]),
                          wait=[mk_mm], inc=[(s_ve, 1)], drain=False)
                    kv_ev[bk] = s_ve.mark()
            for tt in range(NT):
                for half in range(2):
                    bk = bi % 4
                    bi += 1
                    w = [mk_wt[1][ND - 1], kv_ev[bk]]
                    for dj in range(ND):
                        em.op("tensor",
                              lambda e, tt=tt, half=half, dj=dj, bk=bk: e.matmul(
                                  psb[bk],
                                  a_xnt[:, dj * 2048 + tt * P: dj * 2048 + (tt + 1) * P],
                                  a_wt[1][:, dj * 1024 + half * 512: dj * 1024 + (half + 1) * 512],
                                  start=(dj == 0), stop=(dj == ND - 1)),
                              wait=(w if dj == 0 else ()),
                              inc=([(s_pe, 1)] if dj == ND - 1 else ()))
                    mk_mm = s_pe.mark()
                    em.op("vector",
                          lambda e, tt=tt, half=half, bk=bk: e.tensor_copy(
                              a_v[:, tt * 1024 + half * 512: tt * 1024 + (half + 1) * 512],
                              psb[bk]),
                          wait=[mk_mm], inc=[(s_ve, 1)], drain=False)
                    kv_ev[bk] = s_ve.mark()
            mk_kv_done = s_ve.mark()
            mk_xpn_all = sd_stp.mark()
            mk_xpn_all2 = sd_stq.mark()

            dst3 = xs3 if N_SUBLAYERS > 3 else out_ext
            mk_probt_ev = None
            mk_av_ev = [None]
            mk_xout_free = [None, None]
            prev_xpni = [None, None]
            mk_exp_hist = {-1: mk_kv_done, -2: mk_kv_done}
            qstate = {}

            def attn_av(qj):
                slotj = qj % 2
                mk_pt, mk_rcj, mk_xpnij = qstate[qj]
                for half in range(2):
                    bk = 6 + half
                    for ci in range(qj + 1):
                        em.op("tensor",
                              lambda e, ci=ci, half=half, bk=bk, qj=qj: e.matmul(
                                  psb[bk],
                                  a_probt[:, ci * P:(ci + 1) * P],
                                  a_v[:, ci * 1024 + half * 512: ci * 1024 + (half + 1) * 512],
                                  start=(ci == 0), stop=(ci == qj)),
                              wait=([mk_pt, mk_av_ev[0]] if (ci == 0 and half == 0) else ()),
                              inc=([(s_pe, 1)] if (ci == qj and half == 1) else ()))
                mk_av = s_pe.mark()
                for half in range(2):
                    em.op("vector",
                          lambda e, half=half, slotj=slotj, bk=6 + half: e.scalar_tensor_tensor(
                              a_xout[slotj][:, half * 512:(half + 1) * 512],
                              psb[bk], g_recip[:, qj:qj + 1],
                              a_xpni[slotj][:, half * 512:(half + 1) * 512],
                              op0=ALU.mult, op1=ALU.add),
                          wait=[mk_av, mk_rcj, mk_xpnij, mk_xout_free[slotj]],
                          inc=[(s_ve, 1)], drain=(half == 0))
                mk_av_ev[0] = s_ve.mark()
                prev_xpni[slotj] = mk_av_ev[0]
                mk_xout_free[slotj] = dma(
                    "scalar", dst3[qj * P:(qj + 1) * P, :], a_xout[slotj][:],
                    sd_st[slotj], wait=[mk_av_ev[0]])

            for qi in range(NT):
                width = (qi + 1) * P
                sbase = (qi % 2) * 1024 if qi <= 7 else 0
                nsl = (width + 511) // 512
                slot = qi % 2
                mk_xpni = dma("scalar", a_xpni[slot][:],
                              xpn_d[qi * P:(qi + 1) * P, :], sd_x[slot],
                              wait=[mk_xpn_all, mk_xpn_all2, prev_xpni[slot]])
                if qi <= 7:
                    w0 = [mk_xnt_a2, mk_exp_hist[qi - 2]]
                elif qi == 8:
                    w0 = [mk_xnt_a2, mk_exp_hist[6], mk_exp_hist[7]]
                else:
                    w0 = [mk_xnt_a2, mk_exp_hist[qi - 1]]
                for n in range(nsl):
                    ncols = min(width, (n + 1) * 512) - n * 512
                    for dj in range(ND):
                        em.op("tensor",
                              lambda e, qi=qi, n=n, dj=dj, ncols=ncols, sbase=sbase: e.matmul(
                                  ps_s[:, sbase + n * 512: sbase + n * 512 + ncols],
                                  a_xnt[:, dj * 2048 + qi * P: dj * 2048 + (qi + 1) * P],
                                  a_kt[:, dj * 2048 + n * 512: dj * 2048 + n * 512 + ncols],
                                  start=(dj == 0), stop=(dj == ND - 1)),
                              wait=(w0 if (dj == 0 and n == 0) else ()),
                              inc=([(s_pe, 1)] if (dj == ND - 1 and n == nsl - 1) else ()))
                mk_sc = s_pe.mark()
                em.op("vector",
                      lambda e, qi=qi, sbase=sbase: e.tensor_add(
                          ps_s[:, sbase + qi * P: sbase + (qi + 1) * P],
                          ps_s[:, sbase + qi * P: sbase + (qi + 1) * P],
                          g_mask[:]),
                      wait=[mk_sc])
                em.op("vector",
                      lambda e, width=width, sbase=sbase: e.reduce_max(
                          g_nmax[:, 0:1], ps_s[:, sbase: sbase + width],
                          axis=AX.X, negate=True),
                      inc=[(s_ve, 1)])
                mk_nm = s_ve.mark()
                em.op("scalar",
                      lambda e, qi=qi, width=width, slot=slot, sbase=sbase: e.activation(
                          a_probs[slot][:, 0:width],
                          ps_s[:, sbase: sbase + width],
                          AF.Exp, bias=g_nmax[:, 0:1], accum_out=g_rsum[:, 0:1]),
                      wait=[mk_nm, mk_probt_ev], inc=[(s_se, 1)])
                mk_exp = s_se.mark()
                mk_exp_hist[qi] = mk_exp
                em.op("vector",
                      lambda e, qi=qi: e.reciprocal(
                          g_recip[:, qi:qi + 1], g_rsum[:, 0:1]),
                      wait=[mk_exp], inc=[(s_ve, 1)], drain=False)
                mk_rc = s_ve.mark()
                if qi >= 1:
                    attn_av(qi - 1)
                ngr = (qi + 1 + 7) // 8
                mk_ptr_ev = mk_probt_ev
                for g in range(ngr):
                    cis = list(range(g * 8, min(qi + 1, g * 8 + 8)))
                    pbf = psb[4].bitcast(BF16)
                    for i, ci in enumerate(cis):
                        em.op("tensor",
                              lambda e, ci=ci, i=i, slot=slot, pbf=pbf: e.transpose(
                                  pbf[:, i * P:(i + 1) * P],
                                  a_probs[slot][:, ci * P:(ci + 1) * P],
                                  g_ident),
                              wait=([mk_exp, mk_ptr_ev] if i == 0 else ()),
                              inc=([(s_pe, 1)] if ci == cis[-1] else ()))
                    mk_tr = s_pe.mark()
                    dstp = ap3(a_probt, g * 8 * P, P, len(cis), P)
                    em.op("vector",
                          lambda e, dstp=dstp, pbf=pbf, ncis=len(cis): e.tensor_copy(
                              dstp, pbf[:, 0:ncis * P]),
                          wait=[mk_tr], inc=[(s_ve, 1)], drain=False)
                    mk_ptr_ev = s_ve.mark()
                mk_probt_ev = mk_ptr_ev
                qstate[qi] = (mk_probt_ev, mk_rc, mk_xpni)
            attn_av(NT - 1)
            mk_attn_ve = s_ve.mark()
            mk_attn_sta = sd_sta.mark()
            mk_attn_stb = sd_stb.mark()

        if N_SUBLAYERS >= 4:
            region2 = {
                "wg": [mk_attn_ve, mk_f1_cache, mk_f1_ca0, mk_f1_ca1],
                "wu": [mk_attn_ve, mk_f1_cache, mk_f1_ca0, mk_f1_ca1],
                "wd": [mk_attn_ve, mk_f1_cache, mk_f1_ca0, mk_f1_ca1],
                "xin": [mk_attn_ve, mk_attn_sta, mk_attn_stb],
                "xnt": [mk_attn_ve],
                "h": [mk_attn_ve, mk_attn_sta, mk_attn_stb],
            }
            ffn(xs3, out_ext, False, region2,
                lambda gt: [mk_attn_sta, mk_attn_stb])

        all_sems = [sd_sm, sd_wa, sd_wb, sd_wc, sd_w4, sd_cache, sd_ca0,
                    sd_ca1, sd_g1, sd_g2, sd_xa, sd_xb, sd_sta, sd_stb,
                    sd_stp, sd_stq, sd_stg, sd_m0, sd_m1]
        em.op("sync", lambda e: None,
              wait=[(sg, sg.n) for sg in all_sems])

        block = ctx.enter_context(nc.Block())

        @block.sync
        def _(e):
            for f in em.q["sync"]:
                f(e)

        @block.tensor
        def _(e):
            for f in em.q["tensor"]:
                f(e)

        @block.vector
        def _(e):
            for f in em.q["vector"]:
                f(e)

        @block.scalar
        def _(e):
            for f in em.q["scalar"]:
                f(e)

        @block.gpsimd
        def _(e):
            for f in em.q["gpsimd"]:
                f(e)

    return nc


_NC_CACHE = None


def kernel(L0_attn_a_x, L0_attn_a_W_in, L0_attn_a_conv_w, L0_attn_a_W_x,
           L0_attn_a_W_dt, L0_attn_a_b_dt, L0_attn_a_W_Bp, L0_attn_a_D_param,
           L0_attn_a_W_out, L0_attn_b_W, L0_ffn_w_gate, L0_ffn_w_up,
           L0_ffn_w_down):
    global _NC_CACHE
    x = np.ascontiguousarray(np.asarray(L0_attn_a_x, dtype=np.float32))
    w_in = np.ascontiguousarray(np.asarray(L0_attn_a_W_in, np.float32))
    conv_w = np.asarray(L0_attn_a_conv_w, np.float32).reshape(DI, 4)
    conv_wt = np.ascontiguousarray(
        conv_w.reshape(NCI, P, 4).transpose(1, 0, 2).reshape(P, NCI * 4))
    w_x = np.asarray(L0_attn_a_W_x, np.float32)
    w_x_pad = np.ascontiguousarray(
        w_x.reshape(NCI, P, 32).transpose(1, 0, 2).reshape(P, NCI * 32))
    w_dt = np.ascontiguousarray(np.concatenate(
        [np.asarray(L0_attn_a_W_dt, np.float32),
         np.asarray(L0_attn_a_b_dt, np.float32).reshape(1, DI)], axis=0))
    b_dt = np.ascontiguousarray(
        np.asarray(L0_attn_a_b_dt, np.float32).reshape(NCI, P).T)
    w_bp = np.ascontiguousarray(np.asarray(L0_attn_a_W_Bp, np.float32))
    d_par = np.ascontiguousarray(
        np.asarray(L0_attn_a_D_param, np.float32).reshape(NCI, P).T)
    w_out = np.ascontiguousarray(np.asarray(L0_attn_a_W_out, np.float32))
    wt_at = np.ascontiguousarray(np.asarray(L0_attn_b_W, np.float32).T)
    wg_t = np.ascontiguousarray(np.asarray(L0_ffn_w_gate, np.int32).T)
    wu_t = np.ascontiguousarray(np.asarray(L0_ffn_w_up, np.int32).T)
    wd_t = np.ascontiguousarray(np.asarray(L0_ffn_w_down, np.int32).T)
    ident = np.eye(P).astype(ml_dtypes.bfloat16)
    mask = np.where(np.tril(np.ones((P, P), bool)), 0.0,
                    -1e30).astype(np.float32)

    if _NC_CACHE is None:
        _NC_CACHE = build_nc()
    nc = _NC_CACHE

    common = {
        "w_in": w_in, "conv_wt": conv_wt, "w_x": w_x_pad, "w_dt": w_dt,
        "b_dt": b_dt, "w_bp": w_bp, "d_par": d_par, "w_out": w_out,
        "wt_at": wt_at, "wg_t": wg_t, "wu_t": wu_t, "wd_t": wd_t,
        "ident": ident, "mask": mask,
    }
    in_maps = [dict(common, x=np.ascontiguousarray(x[c])) for c in range(8)]
    res = run_bass_kernel_spmd(nc, in_maps, core_ids=list(range(8)))
    return np.stack([r["out"] for r in res.results]).astype(np.float32)



# version 32
# speedup vs baseline: 1.0408x; 1.0048x over previous
import numpy as np
import ml_dtypes
from contextlib import ExitStack

import concourse.bass as bass
import concourse.mybir as mybir
from concourse.bass_utils import run_bass_kernel_spmd

F32 = mybir.dt.float32
BF16 = mybir.dt.bfloat16
I32 = mybir.dt.int32
I8 = mybir.dt.int8
AF = mybir.ActivationFunctionType
ALU = mybir.AluOpType
AX = mybir.AxisListType

P = 128
D = 1024
ND = 8
DI = 2048
NCI = 16
L = 2048
NT = 16
H = 3072
NH = 24
TCM = 256
NKM = L // TCM
NTM = TCM // P
TCF = 512
NKF = L // TCF
NTF = TCF // P
EPS = 1e-5

N_SUBLAYERS = 4
CUT = None

ENGINES = ("sync", "tensor", "vector", "scalar", "gpsimd")

AB = 203584
G_IDENT = 0
G_MASK = 256
G_CONVW = 768
G_BDT = 1024
G_DPAR = 1088
G_NSUM = 1152
G_SSQ = 1216
G_VAR = 1280
G_RSTD = 1344
G_NMR = 1408
G_1PR = 1472
G_CARRY = 1536
G_MSQ = 1600
G_NMAX = 1664
G_RSUM = 1728
G_RECIP = 1792
G_SD = 1856
G_DUMP = 4096
A0 = 8192

M_WIN = 0
M_WOUT = 65536
M_WX = 98304
M_XIN = 100352
M_XN = 116736
M_XNT = 120832
M_XC = 129024
M_U = 137472
M_SZ = 145664
M_ACC = 153856
M_R2 = 155904
M_R3 = 160000
M_SP = 164096
M_A2 = 168192
M_UB = 172288
M_YS = 176384
M_Y = 180480
M_STAGE = 188672
M_END = 192768

F_WG = 0
F_STG = 49152
F_STG2 = 55296
F_WU = 65536
F_WD = 114688
F_XINB = 118784
F_XIN2 = 129024
F_XN = 145408
F_XNT = 149504
F_H = 165888
F_GS = 190464

T_XNT = 0
T_KT = 32768
T_V = 65536
T_WT = 98304
T_PROBS = 131072
T_PROBT = 139264
T_X = 143360
T_XN = 151552
T_XPNO = 155648
T_XPNI = 163840
T_XOUT = 172032
T_END = 180224


class Sem:
    __slots__ = ("h", "n")

    def __init__(self, h):
        self.h = h
        self.n = 0

    def mark(self):
        return (self, self.n)


class Em:
    def __init__(self):
        self.q = {e: [] for e in ENGINES}
        self.n_ops = 0

    def op(self, eng, fn, wait=(), inc=(), drain=True):
        self.n_ops += 1
        if CUT is not None and self.n_ops > CUT:
            return
        waits = []
        for m in wait:
            if m is None:
                continue
            s, v = m
            assert v <= s.n, f"forward wait on {eng}: {v} > {s.n}"
            if v > 0:
                waits.append((s.h, v))
        incs = []
        for s, d in inc:
            s.n += d
            incs.append((s.h, d))

        need_drain = drain and eng in ("vector", "scalar", "gpsimd")

        def run(e, fn=fn, waits=waits, incs=incs, nd=need_drain):
            if nd:
                e.drain()
            for h, v in waits:
                e.wait_ge(h, v)
            r = fn(e)
            for h, d in incs:
                r = r.then_inc(h, d)

        self.q[eng].append(run)


def ap3(view, off, blk_step, blk_n, inner_n):
    return bass.AP(view.tensor, view.offset + off,
                   [list(view.ap[0]), [blk_step, blk_n], [1, inner_n]])


def build_nc():
    nc = bass.Bass("TRN2", target_bir_lowering=False)

    x_ext = nc.dram_tensor("x", [L, D], F32, kind="ExternalInput")
    w_in = nc.dram_tensor("w_in", [D, 4096], F32, kind="ExternalInput")
    conv_wt = nc.dram_tensor("conv_wt", [P, 64], F32, kind="ExternalInput")
    w_x = nc.dram_tensor("w_x", [P, NCI * 32], F32, kind="ExternalInput")
    w_dt = nc.dram_tensor("w_dt", [17, DI], F32, kind="ExternalInput")
    b_dt = nc.dram_tensor("b_dt", [P, 16], F32, kind="ExternalInput")
    w_bp = nc.dram_tensor("w_bp", [16, DI], F32, kind="ExternalInput")
    d_par = nc.dram_tensor("d_par", [P, 16], F32, kind="ExternalInput")
    w_out = nc.dram_tensor("w_out", [DI, D], F32, kind="ExternalInput")
    wt_at = nc.dram_tensor("wt_at", [D, 2048], F32, kind="ExternalInput")
    wg_t = nc.dram_tensor("wg_t", [D, H], I32, kind="ExternalInput")
    wu_t = nc.dram_tensor("wu_t", [D, H], I32, kind="ExternalInput")
    wd_t = nc.dram_tensor("wd_t", [H, D], I32, kind="ExternalInput")
    ident_e = nc.dram_tensor("ident", [P, P], BF16, kind="ExternalInput")
    mask_e = nc.dram_tensor("mask", [P, P], F32, kind="ExternalInput")
    out_ext = nc.dram_tensor("out", [L, D], F32, kind="ExternalOutput")

    xs1 = nc.dram_tensor("xs1", [L, D], F32)
    xs2 = nc.dram_tensor("xs2", [L, D], F32)
    xs3 = nc.dram_tensor("xs3", [L, D], F32)
    xpn_d = nc.dram_tensor("xpn_d", [L, D], F32)
    wg_c = nc.dram_tensor("wg_c", [D, H], BF16)
    wu_c = nc.dram_tensor("wu_c", [D, H], BF16)
    wd_c = nc.dram_tensor("wd_c", [H, D], BF16)

    em = Em()

    with ExitStack() as ctx:
        arena = ctx.enter_context(nc.sbuf_tensor("arena", [P, AB], I8))
        wdt_bf = ctx.enter_context(nc.sbuf_tensor("wdt_bf", [17, DI], BF16))
        wbp_bf = ctx.enter_context(nc.sbuf_tensor("wbp_bf", [16, DI], BF16))
        xpd_bf = ctx.enter_context(nc.sbuf_tensor("xpd_bf", [32, TCM], BF16))
        xpb_bf = ctx.enter_context(nc.sbuf_tensor("xpb_bf", [16, TCM], BF16))

        def V(off, nbytes, dt):
            return arena[:, off:off + nbytes].bitcast(dt)

        def VA(off, nbytes, dt):
            return V(A0 + off, nbytes, dt)

        g_ident = V(G_IDENT, 256, BF16)
        g_mask = V(G_MASK, 512, F32)
        g_convw = V(G_CONVW, 256, F32)
        g_bdt = V(G_BDT, 64, F32)
        g_dpar = V(G_DPAR, 64, F32)
        g_nsum = V(G_NSUM, 64, F32)
        g_ssq = V(G_SSQ, 64, F32)
        g_var = V(G_VAR, 64, F32)
        g_rstd = V(G_RSTD, 64, F32)
        g_nmr = V(G_NMR, 64, F32)
        g_1pr = V(G_1PR, 64, F32)
        g_carry = V(G_CARRY, 64, F32)
        g_msq = V(G_MSQ, 64, F32)
        g_nmax = V(G_NMAX, 64, F32)
        g_rsum = V(G_RSUM, 64, F32)
        g_recip = V(G_RECIP, 64, F32)
        g_sd = V(G_SD, 64, F32)
        g_dump = V(G_DUMP, 2048, BF16)

        ps_s = ctx.enter_context(nc.psum_tensor("ps_s", [P, 2048], F32))
        ps4 = ctx.enter_context(nc.psum_tensor("ps4", [P, 512], F32))
        ps5 = ctx.enter_context(nc.psum_tensor("ps5", [P, 512], F32))
        ps6 = ctx.enter_context(nc.psum_tensor("ps6", [P, 512], F32))
        ps7 = ctx.enter_context(nc.psum_tensor("ps7", [P, 512], F32))
        psb = [ps_s[:, i * 512:(i + 1) * 512] for i in range(4)] + \
              [ps4[:], ps5[:], ps6[:], ps7[:]]

        sd_sm = Sem(ctx.enter_context(nc.semaphore("sd_sm")))
        sd_wa = Sem(ctx.enter_context(nc.semaphore("sd_wa")))
        sd_wb = Sem(ctx.enter_context(nc.semaphore("sd_wb")))
        sd_wc = Sem(ctx.enter_context(nc.semaphore("sd_wc")))
        sd_w4 = Sem(ctx.enter_context(nc.semaphore("sd_w4")))
        sd_cache = Sem(ctx.enter_context(nc.semaphore("sd_cache")))
        sd_ca0 = Sem(ctx.enter_context(nc.semaphore("sd_ca0")))
        sd_ca1 = Sem(ctx.enter_context(nc.semaphore("sd_ca1")))
        sd_g1 = Sem(ctx.enter_context(nc.semaphore("sd_g1")))
        sd_g2 = Sem(ctx.enter_context(nc.semaphore("sd_g2")))
        sd_xa = Sem(ctx.enter_context(nc.semaphore("sd_xa")))
        sd_xb = Sem(ctx.enter_context(nc.semaphore("sd_xb")))
        sd_sta = Sem(ctx.enter_context(nc.semaphore("sd_sta")))
        sd_stb = Sem(ctx.enter_context(nc.semaphore("sd_stb")))
        sd_stp = Sem(ctx.enter_context(nc.semaphore("sd_stp")))
        sd_stq = Sem(ctx.enter_context(nc.semaphore("sd_stq")))
        sd_stg = Sem(ctx.enter_context(nc.semaphore("sd_stg")))
        sd_m0 = Sem(ctx.enter_context(nc.semaphore("sd_m0")))
        sd_m1 = Sem(ctx.enter_context(nc.semaphore("sd_m1")))
        sd_x = [sd_xa, sd_xb]
        sd_st = [sd_sta, sd_stb]
        s_pe = Sem(ctx.enter_context(nc.semaphore("s_pe")))
        s_ve = Sem(ctx.enter_context(nc.semaphore("s_ve")))
        s_se = Sem(ctx.enter_context(nc.semaphore("s_se")))
        s_gp = Sem(ctx.enter_context(nc.semaphore("s_gp")))

        col_free = {}
        trdone = {}
        mk_eps = None

        def dma(queue, dst, src, sem, wait=()):
            em.op(queue, lambda e, d=dst, s=src: e.dma_start(d, s),
                  wait=wait, inc=[(sem, 16)])
            return sem.mark()

        def ln_stats_group(tiles, col0, wait_x, extra_1pr=False):
            n = len(tiles)
            J = slice(col0, col0 + n)
            cf = [col_free.get(col0 + i) for i in range(n)]
            for i, t in enumerate(tiles):
                em.op("scalar",
                      lambda e, s=t, j=col0 + i: e.activation(
                          g_dump[:, 0:1024], s, AF.Square,
                          accum_out=g_ssq[:, j:j + 1]),
                      wait=(list(wait_x) + cf) if i == 0 else (),
                      drain=(i > 0),
                      inc=([(s_se, 1)] if i == n - 1 else ()))
            mk_ssq = s_se.mark()
            for i, t in enumerate(tiles):
                em.op("vector",
                      lambda e, s=t, j=col0 + i: e.reduce_sum(
                          g_nsum[:, j:j + 1], s, axis=AX.X, negate=True),
                      wait=(list(wait_x) + cf) if i == 0 else (), drain=False,
                      inc=([(s_ve, 1)] if i == n - 1 else ()))
            em.op("vector", lambda e, J=J: e.tensor_scalar_mul(
                g_nsum[:, J], g_nsum[:, J], 1.0 / D))
            em.op("vector", lambda e, J=J: e.tensor_mul(
                g_msq[:, J], g_nsum[:, J], g_nsum[:, J]))
            em.op("vector", lambda e, J=J: e.scalar_tensor_tensor(
                g_var[:, J], g_ssq[:, J], 1.0 / D,
                g_msq[:, J], op0=ALU.mult, op1=ALU.subtract),
                wait=[mk_ssq], inc=[(s_ve, 1)])
            mk_var = s_ve.mark()
            em.op("scalar", lambda e, J=J: e.activation(
                g_sd[:, J], g_var[:, J], AF.Sqrt,
                bias=g_sd[:, 15:16]),
                wait=[mk_var, mk_eps], inc=[(s_se, 1)])
            mk_sd = s_se.mark()
            em.op("vector", lambda e, J=J: e.reciprocal(
                g_rstd[:, J], g_sd[:, J]), wait=[mk_sd], drain=False)
            em.op("vector", lambda e, J=J: e.tensor_mul(
                g_nmr[:, J], g_nsum[:, J], g_rstd[:, J]),
                inc=[(s_ve, 1)])
            if extra_1pr:
                em.op("vector", lambda e, J=J: e.tensor_scalar_add(
                    g_1pr[:, J], g_rstd[:, J], 1.0),
                    inc=[(s_ve, 1)], drain=False)
            return s_ve.mark()

        def ln_norm(dst_bf, src_tile, col, mk_stats, extra_wait=None,
                    first=True):
            ew = extra_wait if isinstance(extra_wait, list) else [extra_wait]
            em.op("vector", lambda e, d=dst_bf, s=src_tile, j=col:
                  e.tensor_scalar(d, s, g_rstd[:, j:j + 1],
                                  g_nmr[:, j:j + 1],
                                  op0=ALU.mult, op1=ALU.add),
                  wait=[mk_stats] + ew, inc=[(s_ve, 1)], drain=first)
            col_free[col] = s_ve.mark()
            return col_free[col]

        def transpose_tile(xn_bf_tile, bank, mk_xn, mk_bank_free):
            pbf = psb[bank].bitcast(BF16)
            for dj in range(ND):
                em.op("tensor",
                      lambda e, s=xn_bf_tile, dj=dj, pbf=pbf: e.transpose(
                          pbf[:, dj * P:(dj + 1) * P],
                          s[:, dj * P:(dj + 1) * P], g_ident),
                      wait=([mk_xn, mk_bank_free] if dj == 0 else ()),
                      inc=([(s_pe, 1)] if dj == ND - 1 else ()))
            return s_pe.mark()

        def evac_xnt(xnt_view, blk_step, tt, bank, mk_tr):
            pbf = psb[bank].bitcast(BF16)
            dst = ap3(xnt_view, tt * P, blk_step, ND, P)
            em.op("vector", lambda e, d=dst, s=pbf: e.tensor_copy(d, s),
                  wait=[mk_tr], inc=[(s_ve, 1)], drain=False)
            return s_ve.mark()

        xc = VA(M_XC, 8448, BF16)
        em.op("vector", lambda e: e.memset(ap3(xc, 0, 264, NCI, 3), 0.0),
              inc=[(s_ve, 1)], drain=False)
        em.op("vector", lambda e: e.memset(g_carry[:], 0.0), inc=[(s_ve, 1)],
              drain=False)
        em.op("vector", lambda e: e.memset(g_sd[:, 15:16], EPS),
              inc=[(s_ve, 1)], drain=False)
        mk_init = s_ve.mark()
        mk_eps = mk_init

        xin = [VA(M_XIN + i * 4096, 4096, F32) for i in range(4)]
        xnbf = [VA(M_XN + i * 2048, 2048, BF16) for i in range(2)]
        xnt = [VA(M_XNT + i * 4096, 4096, BF16) for i in range(2)]
        u_bf = VA(M_U, 8192, BF16)
        sz_bf = VA(M_SZ, 8192, BF16)
        acc_p = VA(M_ACC, 2048, F32)
        r2_p = [VA(M_R2 + i * 2048, 2048, F32) for i in range(2)]
        r3_p = [VA(M_R3 + i * 2048, 2048, F32) for i in range(2)]
        sp = [VA(M_SP + i * 2048, 2048, F32) for i in range(2)]
        a_f = [VA(M_A2 + i * 2048, 2048, F32) for i in range(2)]
        ub_f = [VA(M_UB + i * 2048, 2048, F32) for i in range(2)]
        ys_f = [VA(M_YS + i * 2048, 2048, F32) for i in range(2)]
        y_bf = VA(M_Y, 8192, BF16)
        st_stage = VA(M_STAGE, 4096, F32)
        yr_res = VA(M_Y, 4096, F32)

        mk_ln = [None] * (NKM + 1)
        mk_xnt = [None] * (NKM + 1)
        mk_cevax = [None, None]
        mk_store = [None] * NKM
        mk_y_done = [None]
        mk_gp_hist = {}

        def mamba_load_ln(k):
            par = k % 2
            for tt in range(NTM):
                if k >= 2:
                    w = [mk_cevax[par]]
                elif k == 1:
                    w = [mk_wx_stage]
                else:
                    w = []
                mk_x = dma("scalar", xin[par * 2 + tt][:],
                           x_ext[(k * NTM + tt) * P:(k * NTM + tt + 1) * P, :],
                           sd_x[par], wait=w)
            mst = ln_stats_group([xin[par * 2 + tt][:] for tt in range(NTM)],
                                 par * 2, [mk_x])
            mks = []
            for tt in range(NTM):
                col = par * 2 + tt
                mks.append(ln_norm(xnbf[tt][:], xin[par * 2 + tt][:], col,
                                   mst, trdone.get(("m", tt)),
                                   first=(tt == 0)))
            mk_ln[k] = mks

        def mamba_transpose(k, mk_free):
            par = k % 2
            banks = [4, 5]
            for tt in range(NTM):
                mk_tr = transpose_tile(xnbf[tt][:], banks[tt],
                                       mk_ln[k][tt], mk_free[tt])
                trdone[("m", tt)] = mk_tr
                mk_xnt[k] = evac_xnt(xnt[par], 256, tt, banks[tt], mk_tr)

        stage = VA(M_STAGE, 4096, F32)
        wx_bf = VA(M_WX, 1024, BF16)
        wx_stage = VA(M_Y, 2048, F32)
        dma("sync", g_ident, ident_e[:], sd_sm)
        dma("sync", g_mask, mask_e[:], sd_sm)
        dma("sync", g_convw, conv_wt[:], sd_sm)
        dma("sync", g_bdt, b_dt[:], sd_sm)
        dma("sync", g_dpar, d_par[:], sd_sm)
        mk_sm = dma("sync", wx_stage[:, 0:512], w_x[:], sd_sm)
        mk_sm_all = mk_sm
        mamba_load_ln(0)
        mamba_transpose(0, [mk_sm_all, mk_sm_all])

        em.op("vector", lambda e: e.tensor_copy(
            wx_bf[:], wx_stage[:, 0:512]), wait=[mk_sm], inc=[(s_ve, 1)],
            drain=False)
        mk_wx = s_ve.mark()
        mk_wx_stage = mk_wx
        prevc = None
        for m, (srcw, dstw, nrow) in enumerate(
                ((w_dt, wdt_bf, 17), (w_bp, wbp_bf, 16))):
            for hh in range(2):
                w = [prevc] if prevc else []
                mkd = dma("sync", stage[0:nrow, 0:1024],
                          srcw[:, hh * 1024:(hh + 1) * 1024], sd_g1, wait=w)
                em.op("vector",
                      lambda e, dstw=dstw, hh=hh, nrow=nrow: e.tensor_copy(
                          dstw[0:nrow, hh * 1024:(hh + 1) * 1024],
                          stage[0:nrow, 0:1024]),
                      wait=[mkd], inc=[(s_ve, 1)], drain=False)
                prevc = s_ve.mark()
        mk_wdtbp = prevc
        em.op("vector", lambda e: e.memset(xpd_bf[0:32, :], 1.0),
              inc=[(s_ve, 1)], drain=False)
        mk_ones = s_ve.mark()

        win_bf = VA(M_WIN, 65536, BF16)
        win_stage = [VA(M_U, 8192, F32), VA(M_SZ, 8192, F32),
                     VA(M_ACC, 8192, F32), VA(M_SP, 8192, F32)]
        win_sem = [sd_wa, sd_wb, sd_wc, sd_w4]
        mk_win_cast = [None] * ND
        chain = {i: None for i in range(4)}
        gi = 0
        for dj in range(ND):
            for hh in range(2):
                slot = gi % 4
                gi += 1
                mkd = dma("sync" if slot % 2 == 0 else "scalar",
                          win_stage[slot][:, 0:2048],
                          w_in[dj * P:(dj + 1) * P,
                               hh * 2048:(hh + 1) * 2048], win_sem[slot],
                          wait=[chain[slot]] if chain[slot] else [])
                em.op("vector",
                      lambda e, dj=dj, hh=hh, slot=slot: e.tensor_copy(
                          win_bf[:, dj * 4096 + hh * 2048:
                                 dj * 4096 + (hh + 1) * 2048],
                          win_stage[slot][:, 0:2048]),
                      wait=[mkd], inc=[(s_ve, 1)], drain=False)
                mk_win_cast[dj] = chain[slot] = s_ve.mark()

        wout_bf = VA(M_WOUT, 32768, BF16)
        wo_stage = [VA(M_UB, 4096, F32), VA(M_YS, 4096, F32)]
        wo_sem = [sd_wa, sd_wb]
        last_cast = {0: chain[0], 1: chain[1]}
        for ci in range(NCI):
            slot = ci % 2
            mkd = dma("sync", wo_stage[slot][:, 0:1024],
                      w_out[ci * P:(ci + 1) * P, :], wo_sem[slot],
                      wait=[last_cast[slot]] if last_cast[slot] else [])
            em.op("vector",
                  lambda e, ci=ci, slot=slot: e.tensor_copy(
                      wout_bf[:, ci * 1024:(ci + 1) * 1024],
                      wo_stage[slot][:, 0:1024]),
                  wait=[mkd], inc=[(s_ve, 1)], drain=False)
            last_cast[slot] = s_ve.mark()
        mk_startup = s_ve.mark()

        dst1 = xs1 if N_SUBLAYERS > 1 else out_ext

        mk_xz_last = None
        xzbank = {0: None, 1: None}
        mk_c67 = [None]
        mk_gp_acc = [None]
        mk_gp_conv = {0: None, 1: None}
        mk_xp_prev = [None]
        mk_scan_hist = {}
        mk_u = {}
        mk_sz = {}
        mk_sp30_hist = {}
        mk_sp2_hist = {}
        mk_ubm_hist = {}
        mk_xpe_k = {}
        mk_t1 = {-1: None}
        mk_r_hist = {}

        def phaseA_xc_pair(kk, p):
            par = kk % 2
            bkx = p % 2
            for half in range(2):
                ci = 2 * p + half
                wait_mm = [mk_xnt[kk], xzbank[bkx] if half == 0 else None]
                if kk == 0 and ci == 0:
                    wait_mm += [mk_win_cast[-1], mk_wdtbp, mk_wx, mk_ones]
                for dj in range(ND):
                    em.op("tensor",
                          lambda e, dj=dj, ci=ci, bkx=bkx, half=half, par=par: e.matmul(
                              psb[bkx][:, half * TCM:(half + 1) * TCM],
                              win_bf[:, dj * 4096 + ci * P: dj * 4096 + (ci + 1) * P],
                              xnt[par][:, dj * 256:(dj + 1) * 256],
                              start=(dj == 0), stop=(dj == ND - 1)),
                          wait=(wait_mm if dj == 0 else ()),
                          inc=([(s_pe, 1)] if (dj == ND - 1 and half == 1) else ()))
            mk_mm = s_pe.mark()
            em.op("scalar",
                  lambda e, p=p, bkx=bkx: e.copy(
                      ap3(xc, 2 * p * 264 + 3, 264, 2, TCM),
                      psb[bkx]),
                  wait=[mk_mm], inc=[(s_se, 1)], drain=False)
            xzbank[bkx] = s_se.mark()
            for half in range(2):
                ci = 2 * p + half
                em.op("vector",
                      lambda e, ci=ci, half=half: e.tensor_scalar(
                          acc_p[:, half * TCM:(half + 1) * TCM],
                          xc[:, ci * 264 + 0: ci * 264 + TCM],
                          g_convw[:, ci * 4: ci * 4 + 1], None,
                          op0=ALU.mult),
                      wait=([mk_gp_acc[0], xzbank[bkx]] if half == 0 else ()),
                      drain=(half == 0))
            for half in range(2):
                ci = 2 * p + half
                em.op("vector",
                      lambda e, ci=ci, half=half: e.scalar_tensor_tensor(
                          acc_p[:, half * TCM:(half + 1) * TCM],
                          xc[:, ci * 264 + 1: ci * 264 + 1 + TCM],
                          g_convw[:, ci * 4 + 1: ci * 4 + 2],
                          acc_p[:, half * TCM:(half + 1) * TCM],
                          op0=ALU.mult, op1=ALU.add),
                      drain=(half == 0),
                      inc=([(s_ve, 1)] if half == 1 else ()))
            mk_acc = s_ve.mark()
            pp = p % 2
            for half in range(2):
                ci = 2 * p + half
                em.op("scalar",
                      lambda e, ci=ci, half=half, pp=pp: e.mul(
                          r2_p[pp][:, half * TCM:(half + 1) * TCM],
                          xc[:, ci * 264 + 2: ci * 264 + 2 + TCM],
                          g_convw[:, ci * 4 + 2: ci * 4 + 3]),
                      wait=([xzbank[bkx], mk_gp_conv[pp]] if half == 0 else ()),
                      drain=False)
                em.op("scalar",
                      lambda e, ci=ci, half=half, pp=pp: e.mul(
                          r3_p[pp][:, half * TCM:(half + 1) * TCM],
                          xc[:, ci * 264 + 3: ci * 264 + 3 + TCM],
                          g_convw[:, ci * 4 + 3: ci * 4 + 4]),
                      drain=False,
                      inc=([(s_se, 1)] if half == 1 else ()))
            mk_r = s_se.mark()
            mk_r_hist[(kk, p)] = mk_r
            em.op("gpsimd",
                  lambda e, pp=pp: e.tensor_add(
                      r2_p[pp][:], r2_p[pp][:], acc_p[:]),
                  wait=[mk_acc, mk_r], inc=[(s_gp, 1)], drain=False)
            mk_gp_acc[0] = s_gp.mark()
            em.op("gpsimd",
                  lambda e, pp=pp: e.tensor_add(
                      r2_p[pp][:], r2_p[pp][:], r3_p[pp][:]),
                  inc=[(s_gp, 1)])
            mk_gp_conv[pp] = s_gp.mark()
            w_sil = [mk_gp_conv[pp]]
            if kk >= 1:
                w_sil += [mk_gp_hist.get((kk - 1, p)), mk_xp_prev[0]]
            else:
                w_sil.append(mk_startup)
            em.op("scalar",
                  lambda e, p=p, pp=pp: e.activation(
                      u_bf[:, 2 * p * TCM: 2 * (p + 1) * TCM],
                      r2_p[pp][:], AF.Silu),
                  wait=w_sil, inc=[(s_se, 1)], drain=False)
            mk_u[(kk, p)] = s_se.mark()

        def phaseA_z_pair(kk, zp):
            par = kk % 2
            bkx = zp % 2
            for half in range(2):
                ci = NCI + 2 * zp + half
                wait_mm = [mk_xnt[kk], xzbank[bkx] if half == 0 else None]
                for dj in range(ND):
                    em.op("tensor",
                          lambda e, dj=dj, ci=ci, bkx=bkx, half=half, par=par: e.matmul(
                              psb[bkx][:, half * TCM:(half + 1) * TCM],
                              win_bf[:, dj * 4096 + ci * P: dj * 4096 + (ci + 1) * P],
                              xnt[par][:, dj * 256:(dj + 1) * 256],
                              start=(dj == 0), stop=(dj == ND - 1)),
                          wait=(wait_mm if dj == 0 else ()),
                          inc=([(s_pe, 1)] if (dj == ND - 1 and half == 1) else ()))
            mk_mm = s_pe.mark()
            w_sil = [mk_mm]
            if kk >= 1:
                w_sil.append(mk_gp_hist.get((kk - 1, zp)))
            elif zp == 0:
                w_sil.append(mk_startup)
            em.op("scalar",
                  lambda e, zp=zp, bkx=bkx: e.activation(
                      sz_bf[:, 2 * zp * TCM: 2 * (zp + 1) * TCM],
                      psb[bkx], AF.Silu),
                  wait=w_sil, inc=[(s_se, 1)], drain=False)
            xzbank[bkx] = mk_sz[(kk, zp)] = s_se.mark()

        def xp_sec(kk):
            em.op("vector",
                  lambda e: e.tensor_copy(
                      ap3(xc, 0, 264, NCI, 3),
                      ap3(xc, 256, 264, NCI, 3)),
                  wait=[mk_r_hist[(kk, 7)]])
            for ci in range(NCI):
                em.op("tensor",
                      lambda e, ci=ci: e.matmul(
                          psb[5][0:16, 0:TCM],
                          wx_bf[:, ci * 32: ci * 32 + 16],
                          u_bf[:, ci * TCM:(ci + 1) * TCM],
                          start=(ci == 0), stop=(ci == NCI - 1)),
                      wait=[mk_u[(kk, ci // 2)]])
                em.op("tensor",
                      lambda e, ci=ci: e.matmul(
                          psb[4][0:16, 0:TCM],
                          wx_bf[:, ci * 32 + 16: ci * 32 + 32],
                          u_bf[:, ci * TCM:(ci + 1) * TCM],
                          start=(ci == 0), stop=(ci == NCI - 1)),
                      inc=([(s_pe, 1)] if ci == NCI - 1 else ()))
            mk_xp = s_pe.mark()
            mk_xp_prev[0] = mk_xp
            em.op("vector", lambda e: e.tensor_copy(
                xpd_bf[0:16, :], psb[5][0:16, 0:TCM]), wait=[mk_xp],
                drain=False)
            em.op("vector", lambda e: e.tensor_copy(
                xpb_bf[0:16, :], psb[4][0:16, 0:TCM]), inc=[(s_ve, 1)],
                drain=False)
            mk_xpe_k[kk] = s_ve.mark()

        def wout_t0(k, pi):
            w0 = [mk_gp_hist[(k, pi)]]
            if pi == 0:
                w0.append(mk_c67[0])
            for j in range(2):
                ci = 2 * pi + j
                for half in range(2):
                    em.op("tensor",
                          lambda e, ci=ci, half=half, bk=6 + half: e.matmul(
                              psb[bk],
                              y_bf[:, ci * TCM: ci * TCM + P],
                              wout_bf[:, ci * 1024 + half * 512: ci * 1024 + (half + 1) * 512],
                              start=(ci == 0), stop=(ci == NCI - 1)),
                          wait=(w0 if (j == 0 and half == 0) else ()),
                          inc=([(s_pe, 1)] if (j == 1 and half == 1) else ()))

        def phaseB_pair(k, pi):
            ci0 = 2 * pi
            pp = pi % 2
            w0 = [mk_xpe_k[k], mk_sp30_hist.get((k, pi - 1)),
                  mk_ubm_hist.get((k, pi - 1))]
            for half in range(2):
                ci = ci0 + half
                em.op("tensor",
                      lambda e, ci=ci, half=half: e.matmul(
                          psb[2][:, half * TCM:(half + 1) * TCM],
                          wdt_bf[0:17, ci * P:(ci + 1) * P],
                          xpd_bf[0:17, :], start=True, stop=True),
                      wait=(w0 if half == 0 else ()))
                em.op("tensor",
                      lambda e, ci=ci, half=half: e.matmul(
                          psb[3][:, half * TCM:(half + 1) * TCM],
                          wbp_bf[0:16, ci * P:(ci + 1) * P],
                          xpb_bf[0:16, :], start=True, stop=True),
                      inc=([(s_pe, 1)] if half == 1 else ()))
            mk_mm = s_pe.mark()
            if pi >= 1:
                wout_t0(k, pi - 1)
            em.op("vector",
                  lambda e, pp=pp: e.tensor_scalar(
                      sp[pp][:], psb[2][:], 30.0, None, op0=ALU.min),
                  wait=[mk_mm, mk_sp2_hist.get((k, pi - 2))],
                  inc=[(s_ve, 1)], drain=False)
            mk_sp30 = mk_sp30_hist[(k, pi)] = s_ve.mark()
            em.op("scalar", lambda e, pp=pp: e.activation(
                sp[pp][:], sp[pp][:], AF.Exp),
                wait=[mk_sp30, mk_scan_hist.get((k, pi - 2))], drain=False)
            em.op("scalar", lambda e, pp=pp: e.activation(
                sp[pp][:], sp[pp][:], AF.Ln, bias=1.0))
            em.op("scalar", lambda e, pp=pp: e.activation(
                a_f[pp][:], sp[pp][:], AF.Relu,
                scale=-1.0, bias=1.0), inc=[(s_se, 1)])
            mk_sp2 = mk_sp2_hist[(k, pi)] = s_se.mark()
            prev_gp = mk_gp_hist.get((k, pi - 2)) or \
                (mk_gp_hist.get((k - 1, 6 + pi)) if k > 0 else None)
            em.op("vector",
                  lambda e, ci0=ci0, pp=pp: e.tensor_mul(
                      ub_f[pp][:],
                      u_bf[:, ci0 * TCM:(ci0 + 2) * TCM],
                      psb[3][:]),
                  wait=[mk_u[(k, pi)], prev_gp], inc=[(s_ve, 1)], drain=False)
            mk_ubm_hist[(k, pi)] = s_ve.mark()
            em.op("vector", lambda e, pp=pp: e.tensor_scalar(
                a_f[pp][:], a_f[pp][:], 0.9999, None,
                op0=ALU.min), wait=[mk_sp2], drain=False)
            for half in range(2):
                ci = ci0 + half
                em.op("vector",
                      lambda e, ci=ci, half=half, pp=pp: e.tensor_tensor_scan(
                          ys_f[pp][:, half * TCM:(half + 1) * TCM],
                          a_f[pp][:, half * TCM:(half + 1) * TCM],
                          ub_f[pp][:, half * TCM:(half + 1) * TCM],
                          g_carry[:, ci:ci + 1],
                          op0=ALU.mult, op1=ALU.add),
                      wait=([prev_gp] if half == 0 else ()),
                      drain=(half == 0),
                      inc=([(s_ve, 1)] if half == 1 else ()))
            mk_scan = mk_scan_hist[(k, pi)] = s_ve.mark()
            for half in range(2):
                ci = ci0 + half
                em.op("gpsimd", lambda e, ci=ci, half=half, pp=pp: e.tensor_copy(
                    g_carry[:, ci:ci + 1],
                    ys_f[pp][:, (half + 1) * TCM - 1:(half + 1) * TCM]),
                    wait=([mk_scan] if half == 0 else ()), drain=False)
            for half in range(2):
                ci = ci0 + half
                em.op("vector",
                      lambda e, ci=ci, half=half, pp=pp: e.scalar_tensor_tensor(
                          ub_f[pp][:, half * TCM:(half + 1) * TCM],
                          u_bf[:, ci * TCM:(ci + 1) * TCM],
                          g_dpar[:, ci:ci + 1],
                          ys_f[pp][:, half * TCM:(half + 1) * TCM],
                          op0=ALU.mult, op1=ALU.add),
                      drain=(half == 0),
                      inc=([(s_ve, 1)] if half == 1 else ()))
            mk_stt = s_ve.mark()
            w_y = [mk_stt, mk_sz.get((k, pi))]
            if k >= 1 and pi == 0:
                w_y.append(mk_t1[k - 1])
                w_y.append(mk_store[k - 1])
            em.op("gpsimd",
                  lambda e, ci0=ci0, pp=pp: e.tensor_mul(
                      y_bf[:, ci0 * TCM:(ci0 + 2) * TCM],
                      ub_f[pp][:],
                      sz_bf[:, ci0 * TCM:(ci0 + 2) * TCM]),
                  wait=w_y, inc=[(s_gp, 1)], drain=False)
            mk_gp_hist[(k, pi)] = s_gp.mark()

        def mamba_tail(k):
            par = k % 2
            mk_t0_mm = s_pe.mark()
            for half in range(2):
                em.op("vector",
                      lambda e, half=half, par=par, bk=6 + half: e.tensor_add(
                          st_stage[:, half * 512:(half + 1) * 512],
                          xin[par * 2][:, half * 512:(half + 1) * 512],
                          psb[bk]),
                      wait=([mk_t0_mm, mk_store[k - 1] if k >= 1 else None]
                            if half == 0 else ()),
                      drain=(half == 0),
                      inc=([(s_ve, 1)] if half == 1 else ()))
            ev0 = s_ve.mark()
            st_sem = sd_m0 if k == 0 else (sd_m1 if k == 1 else sd_st[par])
            mk_store[k] = dma("scalar",
                              dst1[(k * NTM) * P:(k * NTM + 1) * P, :],
                              st_stage[:], st_sem, wait=[ev0])
            for ci in range(NCI):
                for half in range(2):
                    em.op("tensor",
                          lambda e, ci=ci, half=half, bk=6 + half: e.matmul(
                              psb[bk],
                              y_bf[:, ci * TCM + P: ci * TCM + 2 * P],
                              wout_bf[:, ci * 1024 + half * 512: ci * 1024 + (half + 1) * 512],
                              start=(ci == 0), stop=(ci == NCI - 1)),
                          wait=([ev0] if (ci == 0 and half == 0) else ()),
                          inc=([(s_pe, 1)] if (ci == NCI - 1 and half == 1) else ()))
            mk_t1[k] = s_pe.mark()
            for half in range(2):
                em.op("vector",
                      lambda e, half=half, par=par, bk=6 + half: e.tensor_add(
                          yr_res[:, half * 512:(half + 1) * 512],
                          xin[par * 2 + 1][:, half * 512:(half + 1) * 512],
                          psb[bk]),
                      wait=([mk_t1[k], mk_store[k - 1] if k >= 1 else None]
                            if half == 0 else ()),
                      drain=(half == 0),
                      inc=([(s_ve, 1)] if half == 1 else ()))
            ev1 = s_ve.mark()
            mk_c67[0] = ev1
            mk_store[k] = dma("scalar",
                              dst1[(k * NTM + 1) * P:(k * NTM + 2) * P, :],
                              yr_res[:], st_sem, wait=[ev1])
            mk_cevax[par] = ev1

        for p in range(8):
            phaseA_xc_pair(0, p)
        for zp in range(8):
            phaseA_z_pair(0, zp)
        mk_xz_last = s_pe.mark()
        xp_sec(0)
        mamba_load_ln(1)
        mamba_transpose(1, [mk_xpe_k[0], mk_xpe_k[0]])

        wg_bf_g = VA(F_WG, 49152, BF16)
        wu_bf_g = VA(F_WU, 49152, BF16)
        wstg = [VA(F_STG, 6144, I32), VA(F_STG2, 6144, I32),
                VA(F_XIN2 + 2 * 4096, 6144, I32)]
        wsem = [sd_wa, sd_wb, sd_wc]
        w_prev = {0: None, 1: None, 2: None}
        w_mk = {}

        def emit_w_unit(gi):
            m2, r = divmod(gi, 16)
            dj, hh = divmod(r, 2)
            slot = (r % 2) if m2 == 0 else (r % 3)
            srcw = wg_t if m2 == 0 else wu_t
            dstw = wg_bf_g if m2 == 0 else wu_bf_g
            w = [mk_xz_last]
            if slot == 2:
                w.append(mk_mamba_pe_end)
            if w_prev[slot]:
                w.append(w_prev[slot])
            mkd = dma("sync", wstg[slot][:],
                      srcw[dj * P:(dj + 1) * P, hh * 1536:(hh + 1) * 1536],
                      wsem[slot], wait=w)
            cw = [mkd]
            if m2 == 1 and r == 0:
                cw += [mk_mamba_pe_end, mk_mamba_ve_end, mk_mamba_sta,
                       mk_mamba_stb]
            em.op("gpsimd",
                  lambda e, dj=dj, hh=hh, dstw=dstw, slot=slot: e.tensor_copy(
                      dstw[:, dj * H + hh * 1536: dj * H + (hh + 1) * 1536],
                      wstg[slot][:]),
                  wait=cw, inc=[(s_gp, 1)], drain=False)
            w_prev[slot] = w_mk[gi] = s_gp.mark()

        ffn1_pre = {}
        fxA_g = [VA(F_XIN2 + i * 4096, 4096, F32) for i in range(2)]
        fxB_g = [VA(F_XINB + i * 4096, 4096, F32) for i in range(2)]

        def emit_ffn1_preamble():
            mk_m01 = [(sd_m0, 32), (sd_m1, 32)]
            for i in range(2):
                mk = dma("scalar", fxB_g[i][:],
                         xs1[(2 + i) * P:(3 + i) * P, :], sd_sm,
                         wait=[mk_xz_last] + mk_m01)
            for i in range(2):
                mk = dma("scalar", fxA_g[i][:],
                         xs1[i * P:(1 + i) * P, :], sd_sm,
                         wait=[mk_xpe_k[7], mk_r_hist[(7, 7)]] + mk_m01)
            return {"mk_loads": mk}

        for k in range(NKM):
            for pi in range(8):
                phaseB_pair(k, pi)
                if k + 1 < NKM:
                    phaseA_xc_pair(k + 1, pi)
                    phaseA_z_pair(k + 1, pi)
                if k == NKM - 1 and N_SUBLAYERS >= 2:
                    emit_w_unit(2 * pi)
                    emit_w_unit(2 * pi + 1)
            if k + 1 == NKM - 1:
                mk_xz_last = s_pe.mark()
            if k == NKM - 1:
                mk_b7_pe = s_pe.mark()
            wout_t0(k, 7)
            mamba_tail(k)
            if k + 1 < NKM:
                xp_sec(k + 1)
                if k + 2 < NKM:
                    mamba_load_ln(k + 2)
                    mamba_transpose(k + 2, [mk_xpe_k[k + 1], mk_xpe_k[k + 1]])
            if k == NKM - 2 and N_SUBLAYERS >= 2:
                ffn1_pre.update(emit_ffn1_preamble())

        mk_mamba_pe_end = s_pe.mark()
        mk_mamba_ve_end = s_ve.mark()
        mk_mamba_gp_end = s_gp.mark()
        mk_mamba_sta = sd_sta.mark()
        mk_mamba_stb = sd_stb.mark()

        def ffn(src, dst, first, region, store_mark_fn, pre=None):
            wg_bf = VA(F_WG, 49152, BF16)
            wu_bf = VA(F_WU, 49152, BF16)
            wd_sb = [VA(F_WD + i * 2048, 2048, BF16) for i in range(2)]
            fxA = [VA(F_XIN2 + i * 4096, 4096, F32) for i in range(NTF)]
            fxB = [VA(F_XINB + i * 4096, 4096, F32) for i in range(2)]

            fxC = [VA(F_STG, 4096, F32), VA(F_STG + 4096, 4096, F32)]

            def fxt(c, tt):
                if tt >= 2:
                    if c == 0:
                        return fxB[tt - 2] if pre else fxA[tt]
                    if pre:
                        return fxA[tt] if c % 2 == 1 else fxC[tt - 2]
                    return fxC[tt - 2] if c % 2 == 1 else fxA[tt]
                if c % 2 == 1:
                    return fxB[tt]
                return fxA[tt]
            fxn = [VA(F_XN + i * 2048, 2048, BF16) for i in range(2)]
            fxnt = [VA(F_XNT + i * 8192, 8192, BF16) for i in range(2)]
            h_sb = VA(F_H, 24576, BF16)
            gs = [VA(F_GS + i * 2048, 2048, F32) for i in range(2)]
            stg = [VA(F_STG, 6144, I32), VA(F_STG2, 6144, I32)]
            stg_wd = [VA(F_STG, 4096, I32), VA(F_STG2, 4096, I32)]

            mk_ln_f = [None] * NKF
            mk_xnt_f = [None] * NKF
            mk_fstore = [None] * NT
            mk_fst_chunk = [None] * NKF
            tr_banks = {4: None, 5: None}
            gu_mark = {0: None, 1: None}
            dn_ev = {i: None for i in range(8)}

            def f_ln_tiles(c, tts, col0):
                tiles = [fxt(c, tt) for tt in tts]
                if pre and c == 0:
                    mk_x = pre["mk_loads"]
                else:
                    for i, tt in enumerate(tts):
                        gtile = c * NTF + tt
                        w = list(store_mark_fn(gtile))
                        if tt < 2:
                            if c <= 1:
                                w += region["xin"]
                            else:
                                w.append(mk_fst_chunk[c - 2])
                            if first and c == 1:
                                w.append(mk_fst_chunk[0])
                        else:
                            if c == 0:
                                w += region["xin"]
                            elif c == 1:
                                if first:
                                    w.append(prev_cast[0])
                                else:
                                    w += region["xin"]
                            else:
                                w.append(mk_fst_chunk[c - 2])
                        mk_x = dma("scalar", tiles[i][:],
                                   src[gtile * P:(gtile + 1) * P, :],
                                   sd_x[c % 2], wait=w)
                mst = ln_stats_group([t[:] for t in tiles], col0, [mk_x])
                for i, tt in enumerate(tts):
                    col = col0 + i
                    ew = trdone.get(("f", tt % 2))
                    if pre and c == 0 and i == 0 and tts[0] == 0:
                        ew = [mk_b7_pe, mk_mamba_gp_end,
                              mk_mamba_ve_end, ew]
                    mkn = ln_norm(fxn[tt % 2][:], tiles[i][:], col, mst,
                                  ew, first=(i == 0))
                    bank = 4 + tt % 2
                    mk_tr = transpose_tile(fxn[tt % 2][:], bank, mkn,
                                           tr_banks[bank])
                    trdone[("f", tt % 2)] = mk_tr
                    tr_banks[bank] = evac_xnt(fxnt[c % 2], 512, tt, bank,
                                              mk_tr)
                mk_xnt_f[c] = tr_banks[4 + tts[-1] % 2]

            if pre:
                f_ln_tiles(0, [0, 1, 2, 3], 8)
            else:
                f_ln_tiles(0, [0, 1], 8)
                f_ln_tiles(0, [2, 3], 10)

            mk_wg = [None] * ND
            mk_wu = [None] * ND
            prev_cast = [None]
            if first:
                for gi in range(16, 32):
                    emit_w_unit(gi)
                for dj in range(ND):
                    mk_wg[dj] = w_mk[2 * dj + 1]
                    mk_wu[dj] = w_mk[16 + 2 * dj + 1]
                prev_cast[0] = w_mk[31]
                for dstw, cache, mkk in ((wg_bf, wg_c, mk_wg),
                                         (wu_bf, wu_c, mk_wu)):
                    for dj in range(ND):
                        em.op("scalar",
                              lambda e, dj=dj, dstw=dstw, cache=cache: e.dma_start(
                                  cache[dj * P:(dj + 1) * P, :],
                                  dstw[:, dj * H:(dj + 1) * H]),
                              wait=[mkk[dj]], inc=[(sd_cache, 16)],
                              drain=False)
            else:
                for m, (cache, dstw, mkk, gsem) in enumerate(
                        ((wg_c, wg_bf, mk_wg, sd_g1), (wu_c, wu_bf, mk_wu, sd_g2))):
                    mkreg = region["wg"] if m == 0 else region["wu"]
                    for dj in range(ND):
                        mkk[dj] = dma("sync", dstw[:, dj * H:(dj + 1) * H],
                                      cache[dj * P:(dj + 1) * P, :], gsem,
                                      wait=mkreg)
                    mkk[ND - 1] = gsem.mark()

            gu_banks = [(0, 1), (2, 3)]
            cache_mk = {}
            mk_cache_all = []
            prev_wd_cons = {0: None, 1: None}
            for c in range(NKF):
                par = c % 2
                mk_h = {}
                if c == 0:
                    silu_mk = {}
                    for phase in range(2):
                        mkw = mk_wg if phase == 0 else mk_wu
                        for hi in range(NH):
                            bk = gu_banks[hi % 2][phase]
                            if phase == 0:
                                w = [mk_xnt_f[c], mkw[ND - 1],
                                     gu_mark[hi % 2]]
                            else:
                                w = [mkw[ND - 1], silu_mk[hi],
                                     mk_h.get(hi - 2)]
                            wsrc = wg_bf if phase == 0 else wu_bf
                            for dj in range(ND):
                                em.op("tensor",
                                      lambda e, hi=hi, dj=dj, bk=bk, par=par, wsrc=wsrc: e.matmul(
                                          psb[bk],
                                          wsrc[:, dj * H + hi * P: dj * H + (hi + 1) * P],
                                          fxnt[par][:, dj * 512:(dj + 1) * 512],
                                          start=(dj == 0), stop=(dj == ND - 1)),
                                      wait=(w if dj == 0 else ()),
                                      inc=([(s_pe, 1)] if dj == ND - 1 else ()))
                            mk_mm = s_pe.mark()
                            if phase == 0:
                                em.op("scalar",
                                      lambda e, hi=hi, bk=bk: e.activation(
                                          h_sb[:, hi * 512:(hi + 1) * 512],
                                          psb[bk], AF.Silu, scale=1.0 / 127.0),
                                      wait=[mk_mm] + (region["h"] if hi == 0 else []),
                                      inc=[(s_se, 1)], drain=False)
                                gu_mark[hi % 2] = silu_mk[hi] = s_se.mark()
                            else:
                                em.op("vector",
                                      lambda e, hi=hi, bk=bk: e.tensor_mul(
                                          h_sb[:, hi * 512:(hi + 1) * 512],
                                          psb[bk],
                                          h_sb[:, hi * 512:(hi + 1) * 512]),
                                      wait=[mk_mm], inc=[(s_ve, 1)],
                                      drain=False)
                                gu_mark[hi % 2] = mk_h[hi] = s_ve.mark()
                else:
                    for hi in range(NH):
                        bks = gu_banks[hi % 2]
                        w = [mk_xnt_f[c], gu_mark[hi % 2]]
                        for phase in range(2):
                            wsrc = wg_bf if phase == 0 else wu_bf
                            bk = bks[phase]
                            for dj in range(ND):
                                em.op("tensor",
                                      lambda e, hi=hi, dj=dj, bk=bk, par=par, wsrc=wsrc: e.matmul(
                                          psb[bk],
                                          wsrc[:, dj * H + hi * P: dj * H + (hi + 1) * P],
                                          fxnt[par][:, dj * 512:(dj + 1) * 512],
                                          start=(dj == 0), stop=(dj == ND - 1)),
                                      wait=(w if (dj == 0 and phase == 0) else ()),
                                      inc=([(s_pe, 1)] if (dj == ND - 1 and phase == 1) else ()))
                        mk_mm = s_pe.mark()
                        em.op("scalar",
                              lambda e, hi=hi, bk=bks[0]: e.activation(
                                  gs[hi % 2][:], psb[bk], AF.Silu,
                                  scale=1.0 / 127.0),
                              wait=[mk_mm], inc=[(s_se, 1)], drain=False)
                        mk_sil = s_se.mark()
                        em.op("vector",
                              lambda e, hi=hi, bk=bks[1]: e.tensor_mul(
                                  h_sb[:, hi * 512:(hi + 1) * 512],
                                  gs[hi % 2][:], psb[bk]),
                              wait=[mk_sil], inc=[(s_ve, 1)], drain=False)
                        gu_mark[hi % 2] = mk_h[hi] = s_ve.mark()
                mk_h_all = mk_h[NH - 1]
                if c + 1 < NKF and not (first and c == 0):
                    f_ln_tiles(c + 1, [0, 1], 8)
                    f_ln_tiles(c + 1, [2, 3], 10)

                dn_bk = [(0, 1), (2, 3), (6, 7), (4, 5)]
                mk_wd = [None] * NH
                wd_cons = {}
                for hi in range(NH):
                    slot = hi % 2
                    w = []
                    if hi >= 2:
                        w.append(wd_cons[hi - 2])
                        if cache_mk.get(hi - 2):
                            w.append(cache_mk[hi - 2])
                    else:
                        w.append(prev_wd_cons[slot])
                        if c == 0:
                            w += region["wd"]
                    wdsem = sd_wc if slot == 0 else sd_w4
                    wdq = "sync" if slot == 0 else "scalar"
                    if first and c == 0:
                        mkd = dma(wdq, stg_wd[slot][:],
                                  wd_t[hi * P:(hi + 1) * P, :], wdsem,
                                  wait=w + ([prev_cast[0]] if (hi < 2 and prev_cast[0]) else []))
                        em.op("gpsimd",
                              lambda e, slot=slot: e.tensor_copy(
                                  wd_sb[slot][:], stg_wd[slot][:]),
                              wait=[mkd], inc=[(s_gp, 1)], drain=False)
                        mk_wd[hi] = s_gp.mark()
                        cache_mk[hi] = dma("gpsimd",
                                           wd_c[hi * P:(hi + 1) * P, :],
                                           wd_sb[slot][:],
                                           sd_ca0 if slot == 0 else sd_ca1,
                                           wait=[mk_wd[hi]])
                    else:
                        if first:
                            w += mk_cache_all
                        mk_wd[hi] = dma(wdq, wd_sb[slot][:],
                                        wd_c[hi * P:(hi + 1) * P, :], wdsem,
                                        wait=w)
                    for tt in range(NTF):
                        for half in range(2):
                            bk = dn_bk[tt][half]
                            w2 = [mk_h_all, mk_wd[hi]]
                            if hi == 0:
                                w2.append(gu_mark[0] if bk in (0, 1, 2, 3) else None)
                                w2.append(gu_mark[1] if bk in (2, 3) else None)
                                w2.append(tr_banks.get(bk) if bk in (4, 5) else None)
                                w2.append(dn_ev.get(bk) if bk in (6, 7) else None)
                            em.op("tensor",
                                  lambda e, hi=hi, tt=tt, half=half, bk=bk, slot=slot: e.matmul(
                                      psb[bk],
                                      h_sb[:, hi * 512 + tt * P: hi * 512 + (tt + 1) * P],
                                      wd_sb[slot][:, half * 512:(half + 1) * 512],
                                      start=(hi == 0), stop=(hi == NH - 1)),
                                  wait=(w2 if hi == 0 or (tt == 0 and half == 0) else ()),
                                  inc=([(s_pe, 1)] if (tt == NTF - 1 and half == 1) else ()))
                    wd_cons[hi] = s_pe.mark()
                prev_wd_cons[0] = wd_cons[NH - 2]
                prev_wd_cons[1] = wd_cons[NH - 1]
                mk_dn = s_pe.mark()
                for tt in range(NTF):
                    fxc = fxt(c, tt)
                    for half in range(2):
                        bk = dn_bk[tt][half]
                        em.op("vector",
                              lambda e, fxc=fxc, half=half, bk=bk: e.scalar_tensor_tensor(
                                  fxc[:, half * 512:(half + 1) * 512],
                                  psb[bk], 1.0 / (127.0 * 127.0),
                                  fxc[:, half * 512:(half + 1) * 512],
                                  op0=ALU.mult, op1=ALU.add),
                              wait=[mk_dn], inc=[(s_ve, 1)], drain=False)
                    mk_ev = s_ve.mark()
                    for bk in dn_bk[tt]:
                        dn_ev[bk] = mk_ev
                    gtile = c * NTF + tt
                    mk_fstore[gtile] = dma(
                        "scalar", dst[gtile * P:(gtile + 1) * P, :],
                        fxc[:], sd_st[c % 2], wait=[mk_ev])
                    mk_fst_chunk[c] = mk_fstore[gtile]
                if c == 0:
                    mk_cache_all.append(sd_cache.mark())
                    mk_cache_all.append(sd_ca0.mark())
                    mk_cache_all.append(sd_ca1.mark())
                    mk_cache_all.append(sd_wc.mark())
                    mk_cache_all.append(sd_w4.mark())
                if c + 1 < NKF and first and c == 0:
                    f_ln_tiles(c + 1, [0, 1], 8)
                    f_ln_tiles(c + 1, [2, 3], 10)
                gu_mark[0] = dn_ev[0]
                gu_mark[1] = dn_ev[2]
                tr_banks[4] = dn_ev[4]
                tr_banks[5] = dn_ev[4]
            return mk_fstore

        if N_SUBLAYERS >= 2:
            region1 = {
                "wg": [mk_xz_last],
                "wu": [mk_mamba_pe_end, mk_mamba_ve_end, mk_mamba_sta,
                       mk_mamba_stb],
                "wd": [mk_mamba_pe_end, mk_mamba_ve_end, mk_mamba_sta,
                       mk_mamba_stb],
                "xin": [mk_mamba_gp_end, mk_mamba_ve_end],
                "xnt": [mk_mamba_gp_end, mk_mamba_ve_end],
                "h": [mk_mamba_pe_end, mk_mamba_gp_end, mk_mamba_sta,
                      mk_mamba_stb, (sd_m0, 32), (sd_m1, 32)],
            }
            dst2 = xs2 if N_SUBLAYERS > 2 else out_ext
            mk_f1_store = ffn(xs1, dst2, True, region1,
                              lambda gt: [mk_mamba_sta, mk_mamba_stb],
                              pre=ffn1_pre)
            mk_f1_pe = s_pe.mark()
            mk_f1_ve = s_ve.mark()
            mk_f1_sta = sd_sta.mark()
            mk_f1_stb = sd_stb.mark()
            mk_f1_cache = sd_cache.mark()
            mk_f1_ca0 = sd_ca0.mark()
            mk_f1_ca1 = sd_ca1.mark()

        if N_SUBLAYERS >= 3:
            a_xnt = VA(T_XNT, 32768, BF16)
            a_kt = VA(T_KT, 32768, BF16)
            a_v = VA(T_V, 65536 - 32768, BF16)
            a_wt = [VA(T_WT + i * 16384, 16384, BF16) for i in range(2)]
            a_probs = [VA(T_PROBS + i * 4096, 4096, BF16) for i in range(2)]
            a_probt = VA(T_PROBT, 4096, BF16)
            a_x = [VA(T_X + i * 4096, 4096, F32) for i in range(2)]
            a_xn = [VA(T_XN + i * 2048, 2048, BF16) for i in range(2)]
            a_xpno = [VA(T_XPNO + i * 4096, 4096, F32) for i in range(2)]
            a_xpni = [VA(T_XPNI + i * 4096, 4096, F32) for i in range(2)]
            a_xout = [VA(T_XOUT + i * 4096, 4096, F32) for i in range(2)]
            wt_stage = [V(A0 + T_PROBS, 4096, F32),
                        V(A0 + T_PROBS + 4096, 4096, F32)]

            mk_xpn_st = [None] * NT
            prev_x = {0: None, 1: None}
            prev_xpno = {0: None, 1: None}
            a_trb = {4: mk_f1_ve, 5: mk_f1_ve}
            for tpair in range(NT // 2):
                mk_x = []
                for slot in range(2):
                    t = 2 * tpair + slot
                    w = [mk_f1_sta, mk_f1_stb, prev_x[slot]]
                    if t < 2:
                        w.append(mk_f1_ve)
                    mk_x.append(dma("scalar", a_x[slot][:],
                                    xs2[t * P:(t + 1) * P, :],
                                    sd_x[slot], wait=w))
                mst = ln_stats_group([a_x[0][:], a_x[1][:]], 12, mk_x,
                                     extra_1pr=True)
                for slot in range(2):
                    t = 2 * tpair + slot
                    col = 12 + slot
                    mkn = ln_norm(a_xn[slot][:], a_x[slot][:], col, mst,
                                  trdone.get(("a", slot)),
                                  first=(slot == 0))
                    em.op("vector",
                          lambda e, slot=slot, j=col: e.tensor_scalar(
                              a_xpno[slot][:], a_x[slot][:],
                              g_1pr[:, j:j + 1], g_nmr[:, j:j + 1],
                              op0=ALU.mult, op1=ALU.add),
                          wait=[prev_xpno[slot]], inc=[(s_ve, 1)],
                          drain=False)
                    prev_x[slot] = s_ve.mark()
                    mk_xpn_st[t] = dma("gpsimd", xpn_d[t * P:(t + 1) * P, :],
                                       a_xpno[slot][:],
                                       sd_stp if slot == 0 else sd_stq,
                                       wait=[prev_x[slot]])
                    prev_xpno[slot] = mk_xpn_st[t]
                    bank = 4 + slot
                    mk_tr = transpose_tile(a_xn[slot][:], bank, mkn,
                                           a_trb[bank])
                    trdone[("a", slot)] = mk_tr
                    a_trb[bank] = evac_xnt(a_xnt, 2048, t, bank, mk_tr)
            mk_xnt_a = a_trb[4 + (NT - 1) % 2]
            mk_xnt_a2 = s_ve.mark()

            mk_wt = [[None] * ND, [None] * ND]
            prev_stg = {0: None, 1: None}
            wi = 0
            for halfw in range(2):
                for dj in range(ND):
                    slot = wi % 2
                    wi += 1
                    w = [prev_stg[slot]]
                    if halfw == 0 and dj < 2:
                        w += [mk_f1_ve, mk_f1_sta, mk_f1_stb, mk_f1_pe]
                    mkd = dma("sync", wt_stage[slot][:, 0:1024],
                              wt_at[dj * P:(dj + 1) * P,
                                    halfw * 1024:(halfw + 1) * 1024],
                              sd_wa if slot == 0 else sd_wb, wait=w)
                    em.op("vector",
                          lambda e, halfw=halfw, dj=dj, slot=slot: e.tensor_copy(
                              a_wt[halfw][:, dj * 1024:(dj + 1) * 1024],
                              wt_stage[slot][:, 0:1024]),
                          wait=[mkd] + ([mk_f1_pe] if (halfw == 0 and dj < 2) else []),
                          inc=[(s_ve, 1)], drain=False)
                    prev_stg[slot] = mk_wt[halfw][dj] = s_ve.mark()

            kv_ev = {b: None for b in range(4)}
            bi = 0
            for jc in range(ND):
                for n in range(4):
                    bk = bi % 4
                    bi += 1
                    w = [mk_xnt_a2, mk_wt[0][ND - 1], kv_ev[bk]]
                    for dj in range(ND):
                        em.op("tensor",
                              lambda e, jc=jc, n=n, dj=dj, bk=bk: e.matmul(
                                  psb[bk],
                                  a_wt[0][:, dj * 1024 + jc * P: dj * 1024 + (jc + 1) * P],
                                  a_xnt[:, dj * 2048 + n * 512: dj * 2048 + (n + 1) * 512],
                                  start=(dj == 0), stop=(dj == ND - 1)),
                              wait=(w if dj == 0 else ()),
                              inc=([(s_pe, 1)] if dj == ND - 1 else ()))
                    mk_mm = s_pe.mark()
                    em.op("vector",
                          lambda e, jc=jc, n=n, bk=bk: e.tensor_copy(
                              a_kt[:, jc * 2048 + n * 512: jc * 2048 + (n + 1) * 512],
                              psb[bk]),
                          wait=[mk_mm], inc=[(s_ve, 1)], drain=False)
                    kv_ev[bk] = s_ve.mark()
            for tt in range(NT):
                for half in range(2):
                    bk = bi % 4
                    bi += 1
                    w = [mk_wt[1][ND - 1], kv_ev[bk]]
                    for dj in range(ND):
                        em.op("tensor",
                              lambda e, tt=tt, half=half, dj=dj, bk=bk: e.matmul(
                                  psb[bk],
                                  a_xnt[:, dj * 2048 + tt * P: dj * 2048 + (tt + 1) * P],
                                  a_wt[1][:, dj * 1024 + half * 512: dj * 1024 + (half + 1) * 512],
                                  start=(dj == 0), stop=(dj == ND - 1)),
                              wait=(w if dj == 0 else ()),
                              inc=([(s_pe, 1)] if dj == ND - 1 else ()))
                    mk_mm = s_pe.mark()
                    em.op("vector",
                          lambda e, tt=tt, half=half, bk=bk: e.tensor_copy(
                              a_v[:, tt * 1024 + half * 512: tt * 1024 + (half + 1) * 512],
                              psb[bk]),
                          wait=[mk_mm], inc=[(s_ve, 1)], drain=False)
                    kv_ev[bk] = s_ve.mark()
            mk_kv_done = s_ve.mark()
            mk_xpn_all = sd_stp.mark()
            mk_xpn_all2 = sd_stq.mark()

            dst3 = xs3 if N_SUBLAYERS > 3 else out_ext
            mk_probt_ev = None
            mk_av_ev = [None]
            mk_xout_free = [None, None]
            prev_xpni = [None, None]
            mk_exp_hist = {-1: mk_kv_done, -2: mk_kv_done}
            qstate = {}

            def attn_av(qj):
                slotj = qj % 2
                mk_pt, mk_rcj, mk_xpnij = qstate[qj]
                for half in range(2):
                    bk = 6 + half
                    for ci in range(qj + 1):
                        em.op("tensor",
                              lambda e, ci=ci, half=half, bk=bk, qj=qj: e.matmul(
                                  psb[bk],
                                  a_probt[:, ci * P:(ci + 1) * P],
                                  a_v[:, ci * 1024 + half * 512: ci * 1024 + (half + 1) * 512],
                                  start=(ci == 0), stop=(ci == qj)),
                              wait=([mk_pt, mk_av_ev[0]] if (ci == 0 and half == 0) else ()),
                              inc=([(s_pe, 1)] if (ci == qj and half == 1) else ()))
                mk_av = s_pe.mark()
                for half in range(2):
                    em.op("vector",
                          lambda e, half=half, slotj=slotj, bk=6 + half: e.scalar_tensor_tensor(
                              a_xout[slotj][:, half * 512:(half + 1) * 512],
                              psb[bk], g_recip[:, qj:qj + 1],
                              a_xpni[slotj][:, half * 512:(half + 1) * 512],
                              op0=ALU.mult, op1=ALU.add),
                          wait=[mk_av, mk_rcj, mk_xpnij, mk_xout_free[slotj]],
                          inc=[(s_ve, 1)], drain=(half == 0))
                mk_av_ev[0] = s_ve.mark()
                prev_xpni[slotj] = mk_av_ev[0]
                mk_xout_free[slotj] = dma(
                    "scalar", dst3[qj * P:(qj + 1) * P, :], a_xout[slotj][:],
                    sd_st[slotj], wait=[mk_av_ev[0]])

            for qi in range(NT):
                width = (qi + 1) * P
                sbase = (qi % 2) * 1024 if qi <= 7 else 0
                nsl = (width + 511) // 512
                slot = qi % 2
                mk_xpni = dma("scalar", a_xpni[slot][:],
                              xpn_d[qi * P:(qi + 1) * P, :], sd_x[slot],
                              wait=[mk_xpn_all, mk_xpn_all2, prev_xpni[slot]])
                if qi <= 7:
                    w0 = [mk_xnt_a2, mk_exp_hist[qi - 2]]
                elif qi == 8:
                    w0 = [mk_xnt_a2, mk_exp_hist[6], mk_exp_hist[7]]
                else:
                    w0 = [mk_xnt_a2, mk_exp_hist[qi - 1]]
                for n in range(nsl):
                    ncols = min(width, (n + 1) * 512) - n * 512
                    for dj in range(ND):
                        em.op("tensor",
                              lambda e, qi=qi, n=n, dj=dj, ncols=ncols, sbase=sbase: e.matmul(
                                  ps_s[:, sbase + n * 512: sbase + n * 512 + ncols],
                                  a_xnt[:, dj * 2048 + qi * P: dj * 2048 + (qi + 1) * P],
                                  a_kt[:, dj * 2048 + n * 512: dj * 2048 + n * 512 + ncols],
                                  start=(dj == 0), stop=(dj == ND - 1)),
                              wait=(w0 if (dj == 0 and n == 0) else ()),
                              inc=([(s_pe, 1)] if (dj == ND - 1 and n == nsl - 1) else ()))
                mk_sc = s_pe.mark()
                em.op("vector",
                      lambda e, qi=qi, sbase=sbase: e.tensor_add(
                          ps_s[:, sbase + qi * P: sbase + (qi + 1) * P],
                          ps_s[:, sbase + qi * P: sbase + (qi + 1) * P],
                          g_mask[:]),
                      wait=[mk_sc])
                em.op("vector",
                      lambda e, width=width, sbase=sbase: e.reduce_max(
                          g_nmax[:, 0:1], ps_s[:, sbase: sbase + width],
                          axis=AX.X, negate=True),
                      inc=[(s_ve, 1)])
                mk_nm = s_ve.mark()
                em.op("scalar",
                      lambda e, qi=qi, width=width, slot=slot, sbase=sbase: e.activation(
                          a_probs[slot][:, 0:width],
                          ps_s[:, sbase: sbase + width],
                          AF.Exp, bias=g_nmax[:, 0:1], accum_out=g_rsum[:, 0:1]),
                      wait=[mk_nm, mk_probt_ev], inc=[(s_se, 1)])
                mk_exp = s_se.mark()
                mk_exp_hist[qi] = mk_exp
                em.op("vector",
                      lambda e, qi=qi: e.reciprocal(
                          g_recip[:, qi:qi + 1], g_rsum[:, 0:1]),
                      wait=[mk_exp], inc=[(s_ve, 1)], drain=False)
                mk_rc = s_ve.mark()
                if qi >= 1:
                    attn_av(qi - 1)
                ngr = (qi + 1 + 7) // 8
                mk_ptr_ev = mk_probt_ev
                for g in range(ngr):
                    cis = list(range(g * 8, min(qi + 1, g * 8 + 8)))
                    pbf = psb[4].bitcast(BF16)
                    for i, ci in enumerate(cis):
                        em.op("tensor",
                              lambda e, ci=ci, i=i, slot=slot, pbf=pbf: e.transpose(
                                  pbf[:, i * P:(i + 1) * P],
                                  a_probs[slot][:, ci * P:(ci + 1) * P],
                                  g_ident),
                              wait=([mk_exp, mk_ptr_ev] if i == 0 else ()),
                              inc=([(s_pe, 1)] if ci == cis[-1] else ()))
                    mk_tr = s_pe.mark()
                    dstp = ap3(a_probt, g * 8 * P, P, len(cis), P)
                    em.op("vector",
                          lambda e, dstp=dstp, pbf=pbf, ncis=len(cis): e.tensor_copy(
                              dstp, pbf[:, 0:ncis * P]),
                          wait=[mk_tr], inc=[(s_ve, 1)], drain=False)
                    mk_ptr_ev = s_ve.mark()
                mk_probt_ev = mk_ptr_ev
                qstate[qi] = (mk_probt_ev, mk_rc, mk_xpni)
            attn_av(NT - 1)
            mk_attn_ve = s_ve.mark()
            mk_attn_sta = sd_sta.mark()
            mk_attn_stb = sd_stb.mark()

        if N_SUBLAYERS >= 4:
            region2 = {
                "wg": [mk_attn_ve, mk_f1_cache, mk_f1_ca0, mk_f1_ca1],
                "wu": [mk_attn_ve, mk_f1_cache, mk_f1_ca0, mk_f1_ca1],
                "wd": [mk_attn_ve, mk_f1_cache, mk_f1_ca0, mk_f1_ca1],
                "xin": [mk_attn_ve, mk_attn_sta, mk_attn_stb],
                "xnt": [mk_attn_ve],
                "h": [mk_attn_ve, mk_attn_sta, mk_attn_stb],
            }
            ffn(xs3, out_ext, False, region2,
                lambda gt: [mk_attn_sta, mk_attn_stb])

        all_sems = [sd_sm, sd_wa, sd_wb, sd_wc, sd_w4, sd_cache, sd_ca0,
                    sd_ca1, sd_g1, sd_g2, sd_xa, sd_xb, sd_sta, sd_stb,
                    sd_stp, sd_stq, sd_stg, sd_m0, sd_m1]
        em.op("sync", lambda e: None,
              wait=[(sg, sg.n) for sg in all_sems])

        block = ctx.enter_context(nc.Block())

        @block.sync
        def _(e):
            for f in em.q["sync"]:
                f(e)

        @block.tensor
        def _(e):
            for f in em.q["tensor"]:
                f(e)

        @block.vector
        def _(e):
            for f in em.q["vector"]:
                f(e)

        @block.scalar
        def _(e):
            for f in em.q["scalar"]:
                f(e)

        @block.gpsimd
        def _(e):
            for f in em.q["gpsimd"]:
                f(e)

    return nc


_NC_CACHE = None


def kernel(L0_attn_a_x, L0_attn_a_W_in, L0_attn_a_conv_w, L0_attn_a_W_x,
           L0_attn_a_W_dt, L0_attn_a_b_dt, L0_attn_a_W_Bp, L0_attn_a_D_param,
           L0_attn_a_W_out, L0_attn_b_W, L0_ffn_w_gate, L0_ffn_w_up,
           L0_ffn_w_down):
    global _NC_CACHE
    x = np.ascontiguousarray(np.asarray(L0_attn_a_x, dtype=np.float32))
    w_in = np.ascontiguousarray(np.asarray(L0_attn_a_W_in, np.float32))
    conv_w = np.asarray(L0_attn_a_conv_w, np.float32).reshape(DI, 4)
    conv_wt = np.ascontiguousarray(
        conv_w.reshape(NCI, P, 4).transpose(1, 0, 2).reshape(P, NCI * 4))
    w_x = np.asarray(L0_attn_a_W_x, np.float32)
    w_x_pad = np.ascontiguousarray(
        w_x.reshape(NCI, P, 32).transpose(1, 0, 2).reshape(P, NCI * 32))
    w_dt = np.ascontiguousarray(np.concatenate(
        [np.asarray(L0_attn_a_W_dt, np.float32),
         np.asarray(L0_attn_a_b_dt, np.float32).reshape(1, DI)], axis=0))
    b_dt = np.ascontiguousarray(
        np.asarray(L0_attn_a_b_dt, np.float32).reshape(NCI, P).T)
    w_bp = np.ascontiguousarray(np.asarray(L0_attn_a_W_Bp, np.float32))
    d_par = np.ascontiguousarray(
        np.asarray(L0_attn_a_D_param, np.float32).reshape(NCI, P).T)
    w_out = np.ascontiguousarray(np.asarray(L0_attn_a_W_out, np.float32))
    wt_at = np.ascontiguousarray(np.asarray(L0_attn_b_W, np.float32).T)
    wg_t = np.ascontiguousarray(np.asarray(L0_ffn_w_gate, np.int32).T)
    wu_t = np.ascontiguousarray(np.asarray(L0_ffn_w_up, np.int32).T)
    wd_t = np.ascontiguousarray(np.asarray(L0_ffn_w_down, np.int32).T)
    ident = np.eye(P).astype(ml_dtypes.bfloat16)
    mask = np.where(np.tril(np.ones((P, P), bool)), 0.0,
                    -1e30).astype(np.float32)

    if _NC_CACHE is None:
        _NC_CACHE = build_nc()
    nc = _NC_CACHE

    common = {
        "w_in": w_in, "conv_wt": conv_wt, "w_x": w_x_pad, "w_dt": w_dt,
        "b_dt": b_dt, "w_bp": w_bp, "d_par": d_par, "w_out": w_out,
        "wt_at": wt_at, "wg_t": wg_t, "wu_t": wu_t, "wd_t": wd_t,
        "ident": ident, "mask": mask,
    }
    in_maps = [dict(common, x=np.ascontiguousarray(x[c])) for c in range(8)]
    res = run_bass_kernel_spmd(nc, in_maps, core_ids=list(range(8)))
    return np.stack([r["out"] for r in res.results]).astype(np.float32)

